# version 22
# baseline (speedup 1.0000x reference)
import math
import numpy as np
import ml_dtypes
from contextlib import ExitStack

import concourse.bass as bass
import concourse.mybir as mybir
from concourse.bass_utils import run_bass_kernel_spmd

F32 = mybir.dt.float32
BF16 = mybir.dt.bfloat16
U32 = mybir.dt.uint32
AF = mybir.ActivationFunctionType
ALU = mybir.AluOpType
AX = mybir.AxisListType

S = 4096
D = 1024
H = 8
CH = 12
CTX = 24
NPOS = 25
SCAP = 50.0
Q_SCALE = (128 ** -0.5) / math.log(2)
K_SCALE = math.log(1.0 + math.e) / math.log(2)
E = 8
LAT = 512

MACRO = 384
N_MACRO = 11


def _macros():
    ms = []
    for m in range(N_MACRO):
        ntok = MACRO if m < 10 else 256
        nblk = 32 if m < 10 else 22
        subs = []
        b = 0
        while b < nblk:
            nb = min(8, nblk - b)
            subs.append((12 * b, nb))
            b += nb
        ms.append(dict(m=m, t0=m * MACRO, ntok=ntok, ngrp=ntok // 128, nblk=nblk, subs=subs))
    return ms

MACROS = _macros()


class Res:
    __slots__ = ("name", "wr", "rd")
    def __init__(self, name):
        self.name = name
        self.wr = []
        self.rd = []


class Sched:
    ENGINES = ("tensor", "vector", "scalar", "sync", "gpsimd")

    def __init__(self, nc):
        self.nc = nc
        self.prog = {e: [] for e in self.ENGINES}
        self.cnt = {}
        self.sem = {}
        self.last_wait = {e: {} for e in self.ENGINES}
        self.pend_wr = {e: [] for e in self.ENGINES}
        self.pend_rd = {e: [] for e in self.ENGINES}
        self.pw_eng = {}
        self.pr_eng = {}
        self.esem = {"tensor": "cT", "vector": "cV", "scalar": "cS",
                     "sync": "cY", "gpsimd": "cG"}
        self.lastsmall = {e: set() for e in self.ENGINES}
        self.junk = None
        self.nops = 0

    def add_sem(self, name, handle):
        self.sem[name] = handle
        self.cnt[name] = 0

    def _wait(self, eng, sem, val):
        if val <= 0:
            return
        assert val <= self.cnt[sem], \
            f"wait {sem}>={val} but only {self.cnt[sem]} emitted (engine {eng})"
        lw = self.last_wait[eng]
        if lw.get(sem, 0) >= val:
            return
        lw[sem] = val
        self.prog[eng].append(("wait", sem, val))

    def bump(self, eng):
        sname = self.esem[eng]
        prog = self.prog[eng]
        for i in range(len(prog) - 1, -1, -1):
            if prog[i][0] == "op":
                incs = prog[i][2]
                for (s, _v) in incs:
                    if s == sname:
                        return self.cnt[sname]
                self.cnt[sname] += 1
                incs.append((sname, 1))
                c = self.cnt[sname]
                for r in self.pend_wr[eng]:
                    r.wr = [(sname, c)]
                    r.rd = []
                    self.pw_eng.pop(id(r), None)
                for r in self.pend_rd[eng]:
                    r.rd.append((sname, c))
                    s_ = self.pr_eng.get(id(r))
                    if s_:
                        s_.discard(eng)
                self.pend_wr[eng] = []
                self.pend_rd[eng] = []
                return c
        raise RuntimeError(f"bump with no op on {eng}")

    def _resolve_cross(self, eng, reses):
        for r in reses:
            pe = self.pw_eng.get(id(r))
            if pe is not None and pe != eng:
                self.bump(pe)
            for pe2 in list(self.pr_eng.get(id(r), ())):
                if pe2 != eng:
                    self.bump(pe2)

    def emit(self, eng, fn, reads=(), writes=(), acq=(), dma_sem=None, small=False, bump=False,
             dma_wait_prior=True):
        reads = list(reads); writes = list(writes); acq = list(acq)
        self._resolve_cross(eng, reads + writes + acq)
        if eng in ("vector", "scalar"):
            if any(r.name in self.lastsmall[eng] for r in reads):
                j0, j1 = self.junk[eng]
                self.prog[eng].append(("op", j0, []))
                self.prog[eng].append(("op", j1, []))
                self.nops += 2
                self.lastsmall[eng] = set()
        for r in reads:
            for (s, v) in r.wr:
                self._wait(eng, s, v)
        for w in writes + acq:
            for (s, v) in w.rd:
                self._wait(eng, s, v)
            for (s, v) in w.wr:
                if s != self.esem[eng]:
                    self._wait(eng, s, v)
        if dma_sem is not None and dma_wait_prior:
            self._wait(eng, dma_sem, self.cnt[dma_sem])
        incs = []
        self.prog[eng].append(("op", fn, incs))
        if dma_sem is not None:
            self.cnt[dma_sem] += 16
            incs.append((dma_sem, 16))
            c = self.cnt[dma_sem]
            for w in writes:
                w.wr = [(dma_sem, c)]
                w.rd = []
            for r in reads:
                r.rd.append((dma_sem, c))
        else:
            for w in writes:
                self.pend_wr[eng].append(w)
                self.pw_eng[id(w)] = eng
            for r in reads:
                self.pend_rd[eng].append(r)
                self.pr_eng.setdefault(id(r), set()).add(eng)
            if bump:
                self.bump(eng)
        if eng in ("vector", "scalar"):
            self.lastsmall[eng] = set(w.name for w in writes) if small else set()

    def replay(self, block):
        def make(eng_name):
            def run(eng):
                for item in self.prog[eng_name]:
                    if item[0] == "wait":
                        eng.wait_ge(self.sem[item[1]], item[2])
                    else:
                        ins = item[1](eng)
                        for (s, v) in item[2]:
                            ins.then_inc(self.sem[s], v)
            return run
        block.tensor(make("tensor"))
        block.vector(make("vector"))
        block.scalar(make("scalar"))
        block.sync(make("sync"))
        block.gpsimd(make("gpsimd"))


def build_nc(debug_stage="full", n_macro=N_MACRO, want_handles=False, stop_after="full", debug_dump=False):
    nc = bass.Bass("TRN2", target_bir_lowering=False, debug=False,
                   detect_race_conditions=False)

    x_in = nc.dram_tensor("x", [S, D], F32, kind="ExternalInput").ap()
    wqT = nc.dram_tensor("wqT", [D, D], F32, kind="ExternalInput").ap()
    wkT = nc.dram_tensor("wkT", [D, D], F32, kind="ExternalInput").ap()
    wvT = nc.dram_tensor("wvT", [D, D], F32, kind="ExternalInput").ap()
    wpT = nc.dram_tensor("wpT", [D, D], F32, kind="ExternalInput").ap()
    wrT = nc.dram_tensor("wrT", [D, D], F32, kind="ExternalInput").ap()
    posemb = nc.dram_tensor("posemb", [NPOS, D], F32, kind="ExternalInput").ap()
    pds = nc.dram_tensor("pds", [128], F32, kind="ExternalInput").ap()
    gateT = nc.dram_tensor("gateT", [D, E], F32, kind="ExternalInput").ap()
    corrb = nc.dram_tensor("corrb", [E], F32, kind="ExternalInput").ap()
    w1T = nc.dram_tensor("w1T", [E, LAT, 512], F32, kind="ExternalInput").ap()
    w2T = nc.dram_tensor("w2T", [E, 512, LAT], F32, kind="ExternalInput").ap()
    fc1T = nc.dram_tensor("fc1T", [D, LAT], F32, kind="ExternalInput").ap()
    fc2T = nc.dram_tensor("fc2T", [LAT, D], F32, kind="ExternalInput").ap()
    upT = nc.dram_tensor("upT", [D, 4096], F32, kind="ExternalInput").ap()
    dnT = nc.dram_tensor("dnT", [4096, D], F32, kind="ExternalInput").ap()
    idbf_in = nc.dram_tensor("idbf", [128, 128], BF16, kind="ExternalInput").ap()
    idf_in = nc.dram_tensor("idf", [128, 128], F32, kind="ExternalInput").ap()
    iota_in = nc.dram_tensor("iota8", [128, E], F32, kind="ExternalInput").ap()

    out_d = nc.dram_tensor("out", [S, D], F32, kind="ExternalOutput").ap()

    bdscr = nc.dram_tensor("bdscr", [2, H, 8, 300], F32, kind="Internal").ap()
    w1s = nc.dram_tensor("w1s", [E, LAT, 512], BF16, kind="Internal").ap()
    w2s = nc.dram_tensor("w2s", [E, 512, LAT], BF16, kind="Internal").ap()
    ups = nc.dram_tensor("ups", [D, 4096], BF16, kind="Internal").ap()
    dns = nc.dram_tensor("dns", [4096, D], BF16, kind="Internal").ap()

    es = ExitStack()
    def sb(name, shape, dtype=F32):
        return es.enter_context(nc.sbuf_tensor(name, shape, dtype))

    wq_sb = sb("wq_sb", [128, 8 * D], BF16)
    wk_sb = sb("wk_sb", [128, 8 * D], BF16)
    wv_sb = sb("wv_sb", [128, 8 * D], BF16)
    wp_sb = sb("wp_sb", [128, 8 * D], BF16)
    fc1_sb = sb("fc1_sb", [128, 8 * LAT], BF16)
    fc2_sb = sb("fc2_sb", [128, 4 * D], BF16)
    gate_sb = sb("gate_sb", [128, 8 * E], F32)
    relk_sb = sb("relk_sb", [128, H * NPOS], BF16)
    posfm = sb("posfm", [128, 8 * NPOS], BF16)
    idbf = sb("idbf_s", [128, 128], BF16)
    idf = sb("idf_s", [128, 128], F32)
    iota8 = sb("iota8_s", [128, E], F32)
    ones1 = sb("ones1", [1, 128], F32)
    corrb_sb = sb("corrb_sb", [1, E], F32)
    biasb = sb("biasb", [128, E], F32)
    qscale = sb("qscale", [128, 1], F32)
    pds_sb = sb("pds_sb", [128, 4], F32)

    x_ld = sb("x_ld", [128, D], F32)
    x_res = [x_ld]
    xn32 = sb("xn32", [128, D], F32)
    xnfm32 = sb("xnfm32", [128, D], F32)
    xn_sb = [sb(f"xn_sb{i}", [128, 8 * 396], BF16) for i in range(2)]
    q_sb = sb("q_sb", [128, H * 384], BF16)
    k_sb = sb("k_sb", [128, H * 396], BF16)
    v_sb = [sb(f"v_sb{i}", [120, 1024], BF16) for i in range(4)]
    attn_sb = sb("attn_sb", [128, H * 384], BF16)
    s_sb = sb("s_sb", [128, 3 * D], F32)
    lat_sb = sb("lat_sb", [128, 4 * 384], BF16)

    raw1_sb = [sb("raw1_sb0", [96, H * NPOS], F32)]
    bdm = [sb(f"bdm{i}", [96, H * 108], F32) for i in range(2)]
    ubuf = sb("ubuf", [96, H * 108], F32)
    tbuf = sb("tbuf", [96, H * 108], F32)
    sums = sb("sums", [96, H], F32)
    inv = sb("inv", [96, H], F32)
    ptbf = sb("ptbf", [108, H * 96], BF16)

    rt_sig = sb("rt_sig", [128, E], F32)
    rt_tt = sb("rt_tt", [128, E], F32)
    rt_sfc = sb("rt_sfc", [128, E], F32)
    rt_pad = sb("rt_pad", [128, 16], F32)
    rt_srt = sb("rt_srt", [128, 16], F32)
    rt_gs = sb("rt_gs", [128, 4], F32)
    rt_m = sb("rt_m", [128, E], F32)
    rt_sm = sb("rt_sm", [128, E], F32)
    rt_idx = sb("rt_idx", [128, E], U32)
    rt_idxf = sb("rt_idxf", [128, 2], F32)
    rt_sel = sb("rt_sel", [128, 8], F32)
    rt_eq = sb("rt_eq", [128, 2 * E], F32)
    rt_W = sb("rt_W", [128, E], F32)
    rt_Wg = sb("rt_Wg", [128, 3 * E], F32)
    wfm = [sb(f"wfm{i}", [8, 384], F32) for i in range(2)]

    lnst = sb("lnst", [128, 36], F32)
    lnmv = sb("lnmv", [128, 6], F32)
    lnv = sb("lnv", [128, 3], F32)
    lny = sb("lny", [128, 6], F32)
    lnt = sb("lnt", [128, 6], F32)

    scr34 = sb("scr34", [128, 864], F32)
    trelu = [scr34[:, 0:384]]
    ttw = scr34[:, 384:768]
    interbf = sb("interbf", [128, 4 * 384], BF16)
    combbf = sb("combbf", [128, 4 * 384], BF16)
    upbf = [sb(f"upbf{i}", [128, 384], BF16) for i in range(2)]
    g2b = xnfm32
    ttn = xn32
    outb = [sb("outb0", [128, D], F32)]

    w1b = [sb(f"w1b{i}", [128, 4 * 512], BF16) for i in range(2)]
    w2b = [sb(f"w2b{i}", [128, 4 * 512], BF16) for i in range(2)]
    upb = [sb(f"upb{i}", [128, 8 * 128], BF16) for i in range(2)]
    dnb = [sb(f"dnb{i}", [128, D], BF16) for i in range(2)]
    junkv = sb("junkv", [128, 8], F32)
    junks = junkv

    pbank = [es.enter_context(nc.psum_tensor(f"pbank{i}", [128, 512], F32)) for i in range(8)]
    def BANK(i):
        return pbank[i][:]

    sched = Sched(nc)
    for sname in ("cT", "cV", "cS", "cG",
                  "sW10", "sW11", "sW20", "sW21", "sU0", "sU1", "sD0", "sD1",
                  "sX", "sO", "sBW0", "sBW1", "sBM0", "sBM1", "sRW", "sI", "sY0", "sG0", "sG1", "sC0", "sC1", "sC2", "sC3", "sBM2", "sBM3", "sBM4", "sBM5", "sBM6", "sBM7"):
        sched.add_sem(sname, es.enter_context(nc.semaphore(sname)))
    sched.junk = {
        "vector": [(lambda v: v.reduce_sum(out=junkv[:, 0:1], in_=wq_sb[:, 0:32], axis=AX.X)),
                   (lambda v: v.reduce_sum(out=junkv[:, 4:5], in_=wq_sb[:, 32:64], axis=AX.X))],
        "scalar": [(lambda s_: s_.copy(out=junks[0:64, 0:4], in_=iota8[0:64, 0:4])),
                   (lambda s_: s_.copy(out=junks[0:64, 4:8], in_=iota8[0:64, 4:8]))],
    }

    r_bank = [Res(f"bank{i}") for i in range(8)]
    r_wres = Res("wres")
    r_xld = Res("x_ld")
    r_xn32 = Res("xn32")
    r_xnfm32 = Res("xnfm32")
    r_xn = [Res(f"xn{i}") for i in range(2)]
    r_q = Res("q"); r_k = Res("k")
    r_v = [Res(f"v{i}") for i in range(4)]
    r_attn = Res("attn"); r_s = Res("s"); r_lat = Res("lat")
    r_raw1 = [Res("raw1_0")]
    r_bdscr = [Res(f"bdscr{i}") for i in range(2)]
    r_bdm = [Res(f"bdm{i}") for i in range(2)]
    r_scr = Res("scr34")
    r_ubufh = [Res("ubuf0"), Res("ubuf1")]
    r_tbh = [[Res("tb00"), Res("tb01")], [r_scr, r_scr]]
    r_ubuf = r_ubufh[0]
    r_sums = Res("sums"); r_inv = Res("inv")
    r_ptbf = Res("ptbf")
    r_rt = Res("rt"); r_idx = Res("rt_idx"); r_W = Res("rt_W"); r_Wg = Res("rt_Wg")
    r_wfm = [Res(f"wfm{i}") for i in range(2)]
    r_ln = Res("ln")
    r_trelu = [r_scr]
    r_ttw = r_scr
    r_interbf = Res("interbf"); r_combbf = Res("combbf")
    r_upbf = [Res(f"upbf{i}") for i in range(2)]
    r_outb = [Res("outb0")]
    r_w1b = [Res(f"w1b{i}") for i in range(2)]
    r_w2b = [Res(f"w2b{i}") for i in range(2)]
    r_upb = [Res(f"upb{i}") for i in range(2)]
    r_dnb = [Res(f"dnb{i}") for i in range(2)]
    r_wscr = Res("wscr")
    r_postm = Res("pos_tm"); r_posfm = Res("posfm")
    r_out = Res("out")
    r_g2 = r_xnfm32
    r_ttn = r_xn32
    r_xres = [r_xld]

    EM = sched.emit

    EM("sync", lambda e: e.dma_start(out=idbf[:], in_=idbf_in), writes=[r_wres], dma_sem="sY0")
    EM("sync", lambda e: e.dma_start(out=idf[:], in_=idf_in), writes=[r_wres], dma_sem="sY0")
    EM("sync", lambda e: e.dma_start(out=iota8[:], in_=iota_in), writes=[r_wres], dma_sem="sY0")
    EM("sync", lambda e: e.dma_start(out=pds_sb[:, 0:1], in_=pds.rearrange("(p o) -> p o", o=1)),
       writes=[r_wres], dma_sem="sY0")
    EM("sync", lambda e: e.dma_start(out=corrb_sb[:], in_=corrb.rearrange("(o e) -> o e", o=1)),
       writes=[r_wres], dma_sem="sY0")
    EM("sync", lambda e: e.dma_start(out=gate_sb[:].rearrange("p (c e) -> p c e", c=8),
                                     in_=gateT.rearrange("(c p) e -> p c e", p=128)),
       writes=[r_wres], dma_sem="sY0")
    EM("sync", lambda e: e.dma_start(out=x_ld[0:25, :], in_=posemb), writes=[r_postm], dma_sem="sY0")

    for (dst, src) in ((wq_sb, wqT), (wk_sb, wkT), (wv_sb, wvT), (wp_sb, wpT)):
        EM("gpsimd", lambda e, d=dst, s_=src: e.dma_start(
            out=d[:].rearrange("p (c o) -> p c o", c=8),
            in_=s_.rearrange("(c p) o -> p c o", p=128)),
           writes=[r_wres], dma_sem="sI")
    EM("gpsimd", lambda e: e.dma_start(out=fc1_sb[:].rearrange("p (c o) -> p c o", c=8),
                                       in_=fc1T.rearrange("(c p) o -> p c o", p=128)),
       writes=[r_wres], dma_sem="sI")
    EM("gpsimd", lambda e: e.dma_start(out=fc2_sb[:].rearrange("p (c o) -> p c o", c=4),
                                       in_=fc2T.rearrange("(c p) o -> p c o", p=128)),
       writes=[r_wres], dma_sem="sI")

    sched._wait("vector", "sI", sched.cnt["sI"])
    sched._wait("vector", "sY0", sched.cnt["sY0"])
    EM("vector", lambda e: e.memset(ones1[:], 1.0), writes=[r_wres, r_postm], bump=True)
    EM("scalar", lambda e: e.activation(out=pds_sb[:, 1:2], in_=pds_sb[:, 0:1], func=AF.Abs),
       reads=[r_wres], writes=[r_wres], small=True)
    EM("scalar", lambda e: e.activation(out=pds_sb[:, 2:3], in_=pds_sb[:, 1:2], func=AF.Exp, scale=-1.0),
       reads=[r_wres], writes=[r_wres], small=True)
    EM("vector", lambda e: e.tensor_scalar_add(out=pds_sb[:, 3:4], in0=pds_sb[:, 2:3], scalar1=1.0),
       reads=[r_wres], writes=[r_wres], small=True, bump=True)
    EM("scalar", lambda e: e.activation(out=pds_sb[:, 2:3], in_=pds_sb[:, 3:4], func=AF.Ln),
       reads=[r_wres], writes=[r_wres], small=True)
    EM("vector", lambda e: e.tensor_scalar_max(out=pds_sb[:, 1:2], in0=pds_sb[:, 0:1], scalar1=0.0),
       reads=[r_wres], writes=[r_wres], small=True, bump=True)
    EM("vector", lambda e: e.tensor_add(out=pds_sb[:, 3:4], in0=pds_sb[:, 1:2], in1=pds_sb[:, 2:3]),
       reads=[r_wres], writes=[r_wres], small=True)
    EM("vector", lambda e: e.tensor_scalar_mul(out=qscale[:], in0=pds_sb[:, 3:4], scalar1=Q_SCALE),
       reads=[r_wres], writes=[r_wres], small=True, bump=True)
    EM("tensor", lambda e: e.matmul(out=BANK(7)[0:128, 0:E], lhsT=ones1[:],
                                    rhs=corrb_sb[:], start=True, stop=True),
       reads=[r_wres], acq=[r_bank[7]], writes=[r_bank[7]], bump=True)
    EM("vector", lambda e: e.tensor_copy(out=biasb[:], in_=BANK(7)[0:128, 0:E]),
       reads=[r_bank[7]], writes=[r_wres], bump=True)


    for dc in range(8):
        EM("tensor", lambda e, dc=dc: e.transpose(out=BANK(6)[0:128, 0:NPOS],
                                                  in_=x_ld[0:25, dc * 128:(dc + 1) * 128],
                                                  identity=idf[0:25, 0:25]),
           reads=[r_postm, r_wres], acq=[r_bank[6]], writes=[r_bank[6]], bump=True)
        EM("vector", lambda e, dc=dc: e.tensor_copy(out=posfm[:, dc * NPOS:(dc + 1) * NPOS],
                                                    in_=BANK(6)[0:128, 0:NPOS]),
           reads=[r_bank[6]], writes=[r_posfm], bump=True)
    for quarter in range(4):
        buf = (w1b[0], w2b[0])[quarter % 2]
        rbuf = (r_w1b[0], r_w2b[0])[quarter % 2]
        ssem = ("sG0", "sG1")[quarter % 2]
        EM("gpsimd", lambda e, q=quarter, buf=buf: e.dma_start(
            out=buf[:, 0:2048].rearrange("p (c o) -> p c o", c=2),
            in_=wrT[q * 256:(q + 1) * 256, :].rearrange("(c p) o -> p c o", p=128)),
           reads=[rbuf], writes=[rbuf], dma_sem=ssem)
        for j in range(2):
            dc = quarter * 2 + j
            for h in range(H):
                EM("tensor", lambda e, j=j, dc=dc, h=h, buf=buf: e.matmul(
                    out=BANK(6)[0:128, h * NPOS:(h + 1) * NPOS],
                    lhsT=buf[:, j * 1024 + h * 128: j * 1024 + (h + 1) * 128],
                    rhs=posfm[:, dc * NPOS:(dc + 1) * NPOS],
                    start=(dc == 0 and h == 0), stop=(dc == 7 and h == H - 1)),
                   reads=[rbuf, r_posfm],
                   acq=[r_bank[6]] if (dc == 0 and h == 0) else [],
                   writes=[r_bank[6]] if (dc == 7 and h == H - 1) else [],
                   bump=(dc == 7 and h == H - 1))
    EM("vector", lambda e: e.tensor_scalar_mul(out=relk_sb[:],
                                               in0=BANK(6)[0:128, 0:H * NPOS], scalar1=1.0 / SCAP),
       reads=[r_bank[6]], writes=[r_wres], bump=True)

    for (dst_d, src_d, ssem) in ((ups, upT, "sC2"), (dns, dnT, "sC3"),
                                 (w1s.rearrange("e l i -> (e l) i"), w1T.rearrange("e l i -> (e l) i"), "sC0"),
                                 (w2s.rearrange("e i l -> (e i) l"), w2T.rearrange("e i l -> (e i) l"), "sC1")):
        EM("gpsimd", lambda e, d=dst_d, s_=src_d: e.dma_start(out=d, in_=s_),
           writes=[r_wscr], dma_sem=ssem, dma_wait_prior=False)
    for _s in ("sC0", "sC1", "sC2", "sC3"):
        sched._wait("gpsimd", _s, sched.cnt[_s])
    EM("gpsimd", lambda e: e.memset(junkv[:, 0:4], 0.0), writes=[r_wscr], bump=True)
    for i in range(2):
        EM("vector", lambda e, i=i: e.memset(bdm[i][:], -1000.0 / SCAP), writes=[r_bdm[i]], bump=True)
    EM("vector", lambda e: e.memset(rt_pad[:, 4:8], -1e30), writes=[r_rt], bump=True)
    EM("vector", lambda e: e.memset(rt_pad[:, 12:16], -1e30), writes=[r_rt], bump=True)
    EM("vector", lambda e: e.memset(k_sb[:].rearrange("p (h t) -> p h t", h=H)[:, :, 0:12], 0.0),
       writes=[r_k], bump=True)
    EM("vector", lambda e: e.memset(xn_sb[0][:].rearrange("p (c t) -> p c t", c=8)[:, :, 0:12], 0.0),
       writes=[r_xn[0]], bump=True)


    def emit_P012(mi, groups=None):
        import os
        p012_lvl = int(os.environ.get("P012_LVL", "9"))
        mm = MACROS[mi]
        if groups is None:
            groups = list(range(mm["ngrp"]))
        xn = xn_sb[mi % 2]; rxn = r_xn[mi % 2]
        xnv = xn[:].rearrange("p (c t) -> p c t", c=8)
        if mi > 0 and 0 in groups:
            prev = xn_sb[(mi - 1) % 2]; rprev = r_xn[(mi - 1) % 2]
            po = MACROS[mi - 1]["ntok"]
            EM("vector", lambda e, prev=prev, xnv=xnv, po=po: e.tensor_copy(
                out=xnv[:, :, 0:12],
                in_=prev[:].rearrange("p (c t) -> p c t", c=8)[:, :, po:po + 12]),
               reads=[rprev], writes=[rxn], bump=True)
        wf = wfm[mi % 2]; rwf = r_wfm[mi % 2]
        for g in groups:
            if g >= mm["ngrp"]:
                continue
            t0 = mm["t0"] + g * 128
            EM("scalar", lambda e, t0=t0: e.dma_start(out=x_ld[:], in_=x_in[t0:t0 + 128, :]),
               reads=[r_xld], writes=[r_xld, r_postm], dma_sem="sX")
            EM("vector", lambda e, g=g: e.bn_stats(out=lnst[:, 12 * g:12 * g + 6], in_=x_ld[:, 0:512]),
               reads=[r_xld], writes=[r_ln], small=True)
            EM("vector", lambda e, g=g: e.bn_stats(out=lnst[:, 12 * g + 6:12 * g + 12],
                                                   in_=x_ld[:, 512:1024]),
               reads=[r_xld], writes=[r_ln], small=True)
            EM("vector", lambda e, g=g: e.bn_aggr(out=lnmv[:, 2 * g:2 * g + 2],
                                                  in_=lnst[:, 12 * g:12 * g + 12]),
               reads=[r_ln], writes=[r_ln], small=True)
            EM("vector", lambda e, g=g: e.tensor_scalar_add(out=lnv[:, g:g + 1],
                                                            in0=lnmv[:, 2 * g + 1:2 * g + 2], scalar1=1e-5),
               reads=[r_ln], writes=[r_ln], small=True)
            EM("vector", lambda e, g=g: e.reciprocal(out=lny[:, 2 * g:2 * g + 1], in_=lnv[:, g:g + 1]),
               reads=[r_ln], writes=[r_ln], small=True)
            for it in range(4):
                a, b = 2 * g + (it % 2), 2 * g + ((it + 1) % 2)
                EM("vector", lambda e, a=a: e.tensor_mul(out=lnt[:, 0:1], in0=lny[:, a:a + 1],
                                                         in1=lny[:, a:a + 1]),
                   reads=[r_ln], writes=[r_ln], small=True)
                EM("vector", lambda e, g=g: e.tensor_mul(out=lnt[:, 1:2], in0=lnt[:, 0:1],
                                                         in1=lnv[:, g:g + 1]),
                   reads=[r_ln], writes=[r_ln], small=True)
                EM("vector", lambda e: e.tensor_scalar(out=lnt[:, 2:3], in0=lnt[:, 1:2], scalar1=-0.5,
                                                       scalar2=1.5, op0=ALU.mult, op1=ALU.add),
                   reads=[r_ln], writes=[r_ln], small=True)
                EM("vector", lambda e, a=a, b=b: e.tensor_mul(out=lny[:, b:b + 1],
                                                              in0=lny[:, a:a + 1], in1=lnt[:, 2:3]),
                   reads=[r_ln], writes=[r_ln], small=True)
            if p012_lvl < 1:
                continue
            EM("vector", lambda e, g=g: e.tensor_scalar(out=xn32[:], in0=x_ld[:],
                                                        scalar1=lnmv[:, 2 * g:2 * g + 1],
                                                        scalar2=lny[:, 2 * g:2 * g + 1],
                                                        op0=ALU.subtract, op1=ALU.mult),
               reads=[r_xld, r_ln], writes=[r_xn32], bump=True)
            if p012_lvl < 2:
                continue
            for half in range(2):
                for j in range(4):
                    dc = half * 4 + j
                    EM("tensor", lambda e, dc=dc, j=j: e.transpose(
                        out=BANK(7)[0:128, j * 128:(j + 1) * 128],
                        in_=xn32[:, dc * 128:(dc + 1) * 128], identity=idf[:]),
                       reads=[r_xn32, r_wres],
                       acq=[r_bank[7]] if j == 0 else [],
                       writes=[r_bank[7]] if j == 3 else [], bump=(j == 3))
                EM("vector", lambda e, half=half: e.tensor_copy(
                    out=xnfm32[:, half * 512:(half + 1) * 512], in_=BANK(7)[:]),
                   reads=[r_bank[7]], writes=[r_xnfm32])
                EM("vector", lambda e, half=half, g=g, xnv=xnv: e.tensor_copy(
                    out=xnv[:, half * 4:half * 4 + 4, 12 + g * 128:12 + g * 128 + 128],
                    in_=BANK(7)[:].rearrange("p (c t) -> p c t", c=4)),
                   reads=[r_bank[7]], writes=[rxn, r_bank[7]], bump=True)
            if p012_lvl < 3:
                continue
            for dc in range(8):
                EM("tensor", lambda e, dc=dc: e.matmul(
                    out=BANK(7)[0:128, 0:E],
                    lhsT=xnfm32[:, dc * 128:(dc + 1) * 128],
                    rhs=gate_sb[:, dc * E:(dc + 1) * E],
                    start=(dc == 0), stop=(dc == 7)),
                   reads=[r_xnfm32, r_wres],
                   acq=[r_bank[7]] if dc == 0 else [],
                   writes=[r_bank[7]] if dc == 7 else [], bump=(dc == 7))
            if p012_lvl < 4:
                continue
            EM("vector", lambda e: e.tensor_copy(out=rt_m[:], in_=BANK(7)[0:128, 0:E]),
               reads=[r_bank[7]], writes=[r_rt, r_bank[7]], small=True, bump=True)
            EM("scalar", lambda e: e.activation(out=rt_tt[:], in_=rt_m[:],
                                                func=AF.Tanh, scale=0.5),
               reads=[r_rt], writes=[r_rt], small=True, bump=True)
            EM("vector", lambda e: e.tensor_scalar(out=rt_sig[:], in0=rt_tt[:], scalar1=0.5,
                                                   scalar2=0.5, op0=ALU.mult, op1=ALU.add),
               reads=[r_rt], writes=[r_rt], small=True)
            EM("vector", lambda e: e.tensor_add(out=rt_sfc[:], in0=rt_sig[:], in1=biasb[:]),
               reads=[r_rt, r_wres], writes=[r_rt], small=True)
            EM("vector", lambda e: e.tensor_copy(out=rt_pad[:, 0:4], in_=rt_sfc[:, 0:4]),
               reads=[r_rt], writes=[r_rt], small=True)
            EM("vector", lambda e: e.tensor_copy(out=rt_pad[:, 8:12], in_=rt_sfc[:, 4:8]),
               reads=[r_rt], writes=[r_rt], small=True)
            EM("vector", lambda e: e.max(out=rt_srt[:, 0:8], in_=rt_pad[:, 0:8]),
               reads=[r_rt], writes=[r_rt], small=True)
            EM("vector", lambda e: e.max(out=rt_srt[:, 8:16], in_=rt_pad[:, 8:16]),
               reads=[r_rt], writes=[r_rt], small=True)
            EM("vector", lambda e: e.tensor_add(out=rt_gs[:, 0:2], in0=rt_srt[:, 0:16:8],
                                                in1=rt_srt[:, 1:16:8]),
               reads=[r_rt], writes=[r_rt], small=True)
            EM("vector", lambda e: e.tensor_tensor(out=rt_gs[:, 2:3], in0=rt_gs[:, 0:1],
                                                   in1=rt_gs[:, 1:2], op=ALU.is_ge),
               reads=[r_rt], writes=[r_rt], small=True)
            EM("vector", lambda e: e.tensor_scalar(out=rt_gs[:, 3:4], in0=rt_gs[:, 2:3], scalar1=-1.0,
                                                   scalar2=1.0, op0=ALU.mult, op1=ALU.add),
               reads=[r_rt], writes=[r_rt], small=True)
            EM("vector", lambda e: e.tensor_scalar_mul(out=rt_m[:, 0:4], in0=rt_sfc[:, 0:4],
                                                       scalar1=rt_gs[:, 2:3]),
               reads=[r_rt], writes=[r_rt], small=True)
            EM("vector", lambda e: e.tensor_scalar_mul(out=rt_m[:, 4:8], in0=rt_sfc[:, 4:8],
                                                       scalar1=rt_gs[:, 3:4]),
               reads=[r_rt], writes=[r_rt], small=True)
            EM("vector", lambda e: e.max(out=rt_sm[:], in_=rt_m[:]),
               reads=[r_rt], writes=[r_rt], small=True)
            EM("vector", lambda e: e.tensor_copy(out=junkv[:, 0:8], in_=rt_sm[:]),
               reads=[r_rt], writes=[], small=True)
            if p012_lvl < 5:
                continue
            EM("vector", lambda e: e.max_index(out=rt_idx[:], in_max=rt_sm[:], in_values=rt_m[:]),
               reads=[r_rt], writes=[r_idx], small=True)
            EM("vector", lambda e: e.tensor_copy(out=rt_idxf[:], in_=rt_idx[:, 0:2]),
               reads=[r_idx], writes=[r_idx], small=True)
            EM("vector", lambda e: e.scalar_tensor_tensor(out=rt_eq[:, 0:8], in0=iota8[:],
                                                          scalar=rt_idxf[:, 0:1], in1=rt_sig[:],
                                                          op0=ALU.is_equal, op1=ALU.mult,
                                                          accum_out=rt_sel[:, 0:1]),
               reads=[r_idx, r_rt, r_wres], writes=[r_rt], small=True)
            EM("vector", lambda e: e.scalar_tensor_tensor(out=rt_eq[:, 8:16], in0=iota8[:],
                                                          scalar=rt_idxf[:, 1:2], in1=rt_sig[:],
                                                          op0=ALU.is_equal, op1=ALU.mult,
                                                          accum_out=rt_sel[:, 1:2]),
               reads=[r_idx, r_rt, r_wres], writes=[r_rt], small=True)
            EM("vector", lambda e: e.tensor_add(out=rt_sel[:, 2:3], in0=rt_sel[:, 0:1],
                                                in1=rt_sel[:, 1:2]),
               reads=[r_rt], writes=[r_rt], small=True)
            EM("vector", lambda e: e.tensor_scalar_add(out=rt_sel[:, 3:4], in0=rt_sel[:, 2:3],
                                                       scalar1=1e-20),
               reads=[r_rt], writes=[r_rt], small=True)
            EM("vector", lambda e: e.reciprocal(out=rt_sel[:, 4:5], in_=rt_sel[:, 3:4]),
               reads=[r_rt], writes=[r_rt], small=True)
            EM("vector", lambda e: e.tensor_mul(out=rt_sel[:, 5:6], in0=rt_sel[:, 0:1],
                                                in1=rt_sel[:, 4:5]),
               reads=[r_rt], writes=[r_rt], small=True)
            EM("vector", lambda e: e.tensor_mul(out=rt_sel[:, 6:7], in0=rt_sel[:, 1:2],
                                                in1=rt_sel[:, 4:5]),
               reads=[r_rt], writes=[r_rt], small=True)
            EM("vector", lambda e: e.tensor_single_scalar(out=rt_eq[:, 0:8], in_=iota8[:],
                                                          scalar=rt_idxf[:, 0:1], op=ALU.is_equal),
               reads=[r_idx, r_wres], writes=[r_rt], small=True)
            EM("vector", lambda e: e.tensor_single_scalar(out=rt_eq[:, 8:16], in_=iota8[:],
                                                          scalar=rt_idxf[:, 1:2], op=ALU.is_equal),
               reads=[r_idx, r_wres], writes=[r_rt], small=True)
            EM("vector", lambda e: e.tensor_scalar_mul(out=rt_eq[:, 0:8], in0=rt_eq[:, 0:8],
                                                       scalar1=rt_sel[:, 5:6]),
               reads=[r_rt], writes=[r_rt], small=True)
            EM("vector", lambda e: e.tensor_scalar_mul(out=rt_eq[:, 8:16], in0=rt_eq[:, 8:16],
                                                       scalar1=rt_sel[:, 6:7]),
               reads=[r_rt], writes=[r_rt], small=True)
            EM("vector", lambda e: e.tensor_add(out=rt_W[:], in0=rt_eq[:, 0:8], in1=rt_eq[:, 8:16]),
               reads=[r_rt], writes=[r_W], small=True, bump=True)
            if p012_lvl < 6:
                continue
            EM("vector", lambda e, g=g: e.tensor_copy(out=rt_Wg[:, g * E:(g + 1) * E], in_=rt_W[:]),
               reads=[r_W], writes=[r_Wg], small=True, bump=True)

    def emit_wtrans(mi, g):
        mm = MACROS[mi]
        if g >= mm["ngrp"]:
            return
        wf = wfm[mi % 2]; rwf = r_wfm[mi % 2]
        EM("tensor", lambda e, g=g: e.transpose(out=BANK(7)[0:8, 0:128],
                                                in_=rt_Wg[:, g * E:(g + 1) * E], identity=idf[:]),
           reads=[r_Wg, r_wres], acq=[r_bank[7]], writes=[r_bank[7]], bump=True)
        EM("vector", lambda e, wf=wf, g=g: e.tensor_copy(out=wf[0:8, g * 128:(g + 1) * 128],
                                                         in_=BANK(7)[0:8, 0:128]),
           reads=[r_bank[7]], writes=[rwf], bump=True)

    def bd_emit(mi, si):
        mm = MACROS[mi]
        if si >= len(mm["subs"]):
            return
        q0, nb = mm["subs"][si]
        qcnt = 12 * nb
        slot = si % 2
        rr = r_raw1[0]; rsc = r_bdscr[slot]; rb = r_bdm[slot]
        for h in range(H):
            EM("tensor", lambda e, h=h, q0=q0, qcnt=qcnt: e.matmul(
                out=BANK(6)[0:96, h * NPOS:(h + 1) * NPOS][0:qcnt, :],
                lhsT=q_sb[:, h * 384 + q0:h * 384 + q0 + qcnt],
                rhs=relk_sb[:, h * NPOS:(h + 1) * NPOS],
                start=True, stop=True),
               reads=[r_q, r_wres],
               acq=[r_bank[6]] if h == 0 else [],
               writes=[r_bank[6]] if h == H - 1 else [], bump=(h == H - 1))
        EM("vector", lambda e, qcnt=qcnt, slot=slot: e.tensor_copy(
            out=raw1_sb[0][0:qcnt, :], in_=BANK(6)[0:96, 0:H * NPOS][0:qcnt, :]),
           reads=[r_bank[6]], writes=[rr], bump=True)
        EM("sync", lambda e, qcnt=qcnt, slot=slot: e.dma_start(
            out=bdscr[slot].rearrange("h nb (c p) -> (nb c) h p", c=CH, p=NPOS)[0:qcnt, :, :],
            in_=raw1_sb[0][0:qcnt, :].rearrange("j (h p) -> j h p", h=H)),
           reads=[rr], writes=[rsc], dma_sem=("sBW0", "sBW1")[slot])
        for b in range(nb):
            EM("sync", lambda e, b=b, slot=slot: e.dma_start(
                out=bdm[slot][b * CH:(b + 1) * CH, :].rearrange("c (h q) -> c h q", h=H)[:, :, 12 * b:12 * b + CTX],
                in_=bdscr[slot][:, b, 0:288].rearrange("h (c k) -> c h k", c=CH)),
               reads=[rsc], writes=[rb],
               dma_sem=("sBM0", "sBM1", "sBM2", "sBM3", "sBM4", "sBM5", "sBM6", "sBM7")[slot * 4 + b % 4])

    def emit_P3(mi, part=None):
        mm = MACROS[mi]
        xn = xn_sb[mi % 2]; rxn = r_xn[mi % 2]
        ntok = mm["ntok"]; npad = 12 * mm["nblk"] - ntok
        xnv = xn[:].rearrange("p (c t) -> p c t", c=8)
        if part in (None, 0):
          if npad:
            EM("vector", lambda e, xnv=xnv, ntok=ntok, npad=npad: e.memset(
                xnv[:, :, 12 + ntok:12 + ntok + npad], 0.0), writes=[rxn], bump=True)
          for h in range(H):
            bq = r_bank[6 + h % 2]
            for dc in range(8):
                EM("tensor", lambda e, h=h, dc=dc, xnv=xnv, ntok=ntok: e.matmul(
                    out=BANK(6 + h % 2)[0:128, 0:ntok],
                    lhsT=wq_sb[:, dc * D + h * 128: dc * D + (h + 1) * 128],
                    rhs=xnv[:, dc, 12:12 + ntok],
                    start=(dc == 0), stop=(dc == 7)),
                   reads=[rxn, r_wres], acq=[bq] if dc == 0 else [],
                   writes=[bq] if dc == 7 else [], bump=(dc == 7))
            EM("vector", lambda e, h=h, ntok=ntok: e.tensor_scalar_mul(
                out=q_sb[:, h * 384:h * 384 + ntok], in0=BANK(6 + h % 2)[0:128, 0:ntok],
                scalar1=qscale[:, 0:1]),
               reads=[bq, r_wres], writes=[r_q], bump=True)
        if part in (None, 0):
            bd_emit(mi, 0)
        if part in (None, 1):
          for h in range(H):
            bk = r_bank[(6, 7, 0, 1)[h % 4]]
            for dc in range(8):
                EM("tensor", lambda e, h=h, dc=dc, xnv=xnv, ntok=ntok: e.matmul(
                    out=BANK((6, 7, 0, 1)[h % 4])[0:128, 0:ntok],
                    lhsT=wk_sb[:, dc * D + h * 128: dc * D + (h + 1) * 128],
                    rhs=xnv[:, dc, 12:12 + ntok],
                    start=(dc == 0), stop=(dc == 7)),
                   reads=[rxn, r_wres], acq=[bk] if dc == 0 else [],
                   writes=[bk] if dc == 7 else [], bump=(dc == 7))
            EM("vector", lambda e, h=h, ntok=ntok: e.tensor_scalar_mul(
                out=k_sb[:, h * 396 + 12:h * 396 + 12 + ntok], in0=BANK((6, 7, 0, 1)[h % 4])[0:128, 0:ntok],
                scalar1=K_SCALE),
               reads=[bk], writes=[r_k], bump=True)
        if part in (None, 1):
            bd_emit(mi, 1)
        if part not in (None, 2):
            return
        if npad:
            EM("vector", lambda e, ntok=ntok, npad=npad: e.memset(
                k_sb[:].rearrange("p (h t) -> p h t", h=H)[:, :, 12 + ntok:12 + ntok + npad], 0.0),
               writes=[r_k], bump=True)
        for si, (q0, nb) in enumerate(mm["subs"]):
            kw = 12 * nb + 12
            b0, b1 = 6, 7
            for dc in range(8):
                for half in range(2):
                    b = (b0, b1)[half]
                    EM("tensor", lambda e, dc=dc, half=half, q0=q0, kw=kw, b=b, xnv=xnv: e.matmul(
                        out=BANK(b)[0:kw, 0:512],
                        lhsT=xnv[:, dc, q0:q0 + kw],
                        rhs=wv_sb[:, dc * D + half * 512: dc * D + half * 512 + 512],
                        start=(dc == 0), stop=(dc == 7)),
                       reads=[rxn, r_wres],
                       acq=[r_bank[b]] if dc == 0 else [],
                       writes=[r_bank[b]] if dc == 7 else [],
                       bump=(dc == 7 and half == 1))
            EM("vector", lambda e, si=si, kw=kw, b0=b0: e.tensor_copy(
                out=v_sb[si][0:kw, 0:512], in_=BANK(b0)[0:kw, 0:512]),
               reads=[r_bank[b0]], writes=[r_v[si]])
            EM("vector", lambda e, si=si, kw=kw, b1=b1: e.tensor_copy(
                out=v_sb[si][0:kw, 512:1024], in_=BANK(b1)[0:kw, 0:512]),
               reads=[r_bank[b1]], writes=[r_v[si]], bump=True)

    def emit_P4(mi):
        mm = MACROS[mi]
        tbufs = [tbuf, scr34]

        def stage1(si):
            q0, nb = mm["subs"][si]
            qcnt = 12 * nb
            kw = qcnt + 12
            slot = si % 2
            pb = (0, 2)[si % 2]
            tb = tbufs[si % 2]
            for h in range(H):
                bi = pb + h // 4
                EM("tensor", lambda e, h=h, bi=bi, q0=q0, qcnt=qcnt, kw=kw: e.matmul(
                    out=BANK(bi)[0:qcnt, (h % 4) * 108:(h % 4) * 108 + kw],
                    lhsT=q_sb[:, h * 384 + q0:h * 384 + q0 + qcnt],
                    rhs=k_sb[:, h * 396 + q0:h * 396 + q0 + kw],
                    start=True, stop=True),
                   reads=[r_q, r_k],
                   acq=[r_bank[bi]] if h % 4 == 0 else [],
                   writes=[r_bank[bi]] if h % 4 == 3 else [], bump=(h % 4 == 3))
            for j in range(2):
                bi = pb + j
                EM("vector", lambda e, j=j, bi=bi, qcnt=qcnt, kw=kw, slot=slot: e.scalar_tensor_tensor(
                    out=ubuf[0:qcnt, :].rearrange("p (h q) -> p h q", h=H)[:, j * 4:j * 4 + 4, 0:kw],
                    in0=BANK(bi)[0:qcnt, 0:432].rearrange("p (h q) -> p h q", h=4)[:, :, 0:kw],
                    scalar=1.0 / SCAP,
                    in1=bdm[slot][0:qcnt, :].rearrange("p (h q) -> p h q", h=H)[:, j * 4:j * 4 + 4, 0:kw],
                    op0=ALU.mult, op1=ALU.add),
                   reads=[r_bank[bi], r_bdm[slot]], writes=[r_ubufh[j]], bump=True)
            bd_emit(mi, si + 2)
            for j in range(2):
                hs = slice(j * 432, j * 432 + 432)
                EM("scalar", lambda e, qcnt=qcnt, hs=hs, tb=tb: e.activation(
                    out=tb[0:qcnt, hs] if not hasattr(tb, 'tensor') else tb[0:qcnt, hs],
                    in_=ubuf[0:qcnt, hs], func=AF.Tanh),
                   reads=[r_ubufh[j]], writes=[r_tbh[si % 2][j]])
                EM("scalar", lambda e, qcnt=qcnt, hs=hs, tb=tb: e.activation(
                    out=ubuf[0:qcnt, hs], in_=tb[0:qcnt, hs], func=AF.Exp, scale=SCAP),
                   reads=[r_tbh[si % 2][j]], writes=[r_ubufh[j]], bump=True)
                EM("vector", lambda e, qcnt=qcnt, kw=kw, j=j: e.reduce_sum(
                    out=sums[0:qcnt, j * 4:j * 4 + 4],
                    in_=ubuf[0:qcnt, j * 432:j * 432 + 432].rearrange("p (h q) -> p h q", h=4)[:, :, 0:kw],
                    axis=AX.X),
                   reads=[r_ubufh[j]], writes=[r_sums], small=True)
                EM("vector", lambda e, qcnt=qcnt, j=j: e.reciprocal(
                    out=inv[0:qcnt, j * 4:j * 4 + 4], in_=sums[0:qcnt, j * 4:j * 4 + 4]),
                   reads=[r_sums], writes=[r_inv], small=True, bump=True)
                EM("vector", lambda e, qcnt=qcnt, kw=kw, j=j, tb=tb: e.tensor_mul(
                    out=tb[0:qcnt, j * 432:j * 432 + 432].rearrange("p (h q) -> p h q", h=4)[:, :, 0:kw],
                    in0=ubuf[0:qcnt, j * 432:j * 432 + 432].rearrange("p (h q) -> p h q", h=4)[:, :, 0:kw],
                    in1=inv[0:qcnt, j * 4:j * 4 + 4].rearrange("p (h o) -> p h o", o=1).broadcast_to([qcnt, 4, kw])),
                   reads=[r_ubufh[j], r_inv], writes=[r_tbh[si % 2][j]], bump=True)

        def stage2(si):
            q0, nb = mm["subs"][si]
            qcnt = 12 * nb
            kw = qcnt + 12
            pb = (0, 2)[si % 2]
            tb = tbufs[si % 2]
            for h in range(H):
                bi = pb + h // 4
                EM("tensor", lambda e, h=h, bi=bi, qcnt=qcnt, kw=kw, tb=tb: e.transpose(
                    out=BANK(bi)[0:kw, (h % 4) * 96:(h % 4) * 96 + qcnt],
                    in_=tb[0:qcnt, h * 108:h * 108 + kw], identity=idf[0:qcnt, 0:qcnt]),
                   reads=[r_tbh[si % 2][h // 4], r_wres],
                   acq=[r_bank[bi]] if h % 4 == 0 else [],
                   writes=[r_bank[bi]] if h % 4 == 3 else [], bump=(h % 4 == 3))
            for j in range(2):
                bi = pb + j
                EM("vector", lambda e, bi=bi, j=j, kw=kw: e.tensor_copy(
                    out=ptbf[0:kw, j * 384:j * 384 + 384], in_=BANK(bi)[0:kw, 0:384]),
                   reads=[r_bank[bi]], writes=[r_ptbf], bump=True)
            for h in range(H):
                bi = 4 + h // 4
                EM("tensor", lambda e, h=h, bi=bi, si=si, qcnt=qcnt, kw=kw: e.matmul(
                    out=BANK(bi)[0:128, (h % 4) * 96:(h % 4) * 96 + qcnt],
                    lhsT=v_sb[si][0:kw, h * 128:(h + 1) * 128],
                    rhs=ptbf[0:kw, h * 96:h * 96 + qcnt],
                    start=True, stop=True),
                   reads=[r_v[si], r_ptbf],
                   acq=[r_bank[bi]] if h % 4 == 0 else [],
                   writes=[r_bank[bi]] if h % 4 == 3 else [], bump=(h % 4 == 3))
            for bi in (4, 5):
                h0 = (bi - 4) * 4
                EM("vector", lambda e, bi=bi, h0=h0, q0=q0, qcnt=qcnt: e.tensor_copy(
                    out=attn_sb[:].rearrange("p (h t) -> p h t", h=H)[:, h0:h0 + 4, q0:q0 + qcnt],
                    in_=BANK(bi)[0:128, 0:384].rearrange("p (h t) -> p h t", h=4)[:, :, 0:qcnt]),
                   reads=[r_bank[bi]], writes=[r_attn], bump=True)

        for si in range(len(mm["subs"])):
            stage1(si)
            if si > 0:
                stage2(si - 1)
        stage2(len(mm["subs"]) - 1)

    def emit_P5_group(mi, g):
        mm = MACROS[mi]
        if g >= mm["ngrp"]:
            return
        for half in range(2):
            b = 2 * g + half
            for dc in range(8):
                EM("tensor", lambda e, g=g, dc=dc, half=half, b=b: e.matmul(
                    out=BANK(b)[0:128, 0:512],
                    lhsT=attn_sb[:, dc * 384 + g * 128: dc * 384 + g * 128 + 128],
                    rhs=wp_sb[:, dc * D + half * 512: dc * D + half * 512 + 512],
                    start=(dc == 0), stop=(dc == 7)),
                   reads=[r_attn, r_wres], acq=[r_bank[b]] if dc == 0 else [],
                   writes=[r_bank[b]] if dc == 7 else [], bump=(dc == 7))
        for half in range(2):
            b = 2 * g + half
            EM("vector", lambda e, g=g, half=half, b=b: e.tensor_copy(
                out=s_sb[:, g * D + half * 512:g * D + half * 512 + 512],
                in_=BANK(b)[0:128, 0:512]),
               reads=[r_bank[b]], writes=[r_s], bump=True)

    def emit_P5(mi):
        mm = MACROS[mi]
        for g in range(mm["ngrp"]):
            emit_P5_group(mi, g)
        if mi + 1 < n_macro:
            po = mm["ntok"]
            EM("vector", lambda e, po=po: e.tensor_copy(
                out=k_sb[:].rearrange("p (h t) -> p h t", h=H)[:, :, 0:12],
                in_=k_sb[:].rearrange("p (h t) -> p h t", h=H)[:, :, po:po + 12]),
               reads=[r_k], writes=[r_k], bump=True)

    def emit_P6(mi):
        mm = MACROS[mi]
        ntok = mm["ntok"]
        xn = xn_sb[mi % 2]; rxn = r_xn[mi % 2]
        xnv = xn[:].rearrange("p (c t) -> p c t", c=8)
        for lc in range(4):
            b = (2, 3, 4, 5)[lc]
            for dc in range(8):
                EM("tensor", lambda e, lc=lc, dc=dc, b=b, xnv=xnv, ntok=ntok: e.matmul(
                    out=BANK(b)[0:128, 0:ntok],
                    lhsT=fc1_sb[:, dc * LAT + lc * 128: dc * LAT + (lc + 1) * 128],
                    rhs=xnv[:, dc, 12:12 + ntok],
                    start=(dc == 0), stop=(dc == 7)),
                   reads=[rxn, r_wres], acq=[r_bank[b]] if dc == 0 else [],
                   writes=[r_bank[b]] if dc == 7 else [], bump=(dc == 7))
            EM("vector", lambda e, lc=lc, b=b, ntok=ntok: e.tensor_copy(
                out=lat_sb[:, lc * 384:lc * 384 + ntok], in_=BANK(b)[0:128, 0:ntok]),
               reads=[r_bank[b]], writes=[r_lat], bump=True)

    def emit_stream_w12(ei):
        sl = ei % 2
        EM("sync", lambda e, ei=ei, sl=sl: e.dma_start(
            out=w1b[sl][:].rearrange("p (c i) -> p c i", c=4),
            in_=w1s[ei].rearrange("(c p) i -> p c i", p=128)),
           reads=[r_wscr], writes=[r_w1b[sl]], dma_sem=("sW10", "sW11")[sl])
        EM("sync", lambda e, ei=ei, sl=sl: e.dma_start(
            out=w2b[sl][:].rearrange("p (c i) -> p c i", c=4),
            in_=w2s[ei].rearrange("(c p) i -> p c i", p=128)),
           reads=[r_wscr], writes=[r_w2b[sl]], dma_sem=("sW20", "sW21")[sl])

    def emit_stream_up(oc):
        sl = oc % 2
        EM("scalar", lambda e, oc=oc, sl=sl: e.dma_start(
            out=upb[sl][:].rearrange("p (c o) -> p c o", c=8),
            in_=ups.rearrange("(c p) o -> p c o", p=128)[:, :, oc * 128:(oc + 1) * 128]),
           reads=[r_wscr], writes=[r_upb[sl]], dma_sem=("sU0", "sU1")[sl])

    def emit_stream_dn(oc):
        sl = oc % 2
        EM("scalar", lambda e, oc=oc, sl=sl: e.dma_start(
            out=dnb[sl][:], in_=dns.rearrange("(g p) o -> p g o", p=128)[:, oc, :]),
           reads=[r_wscr], writes=[r_dnb[sl]], dma_sem=("sD0", "sD1")[sl])

    def emit_stream_updn(oc):
        emit_stream_up(oc)
        emit_stream_dn(oc)

    def emit_P7_expert(mi, ei):
        mm = MACROS[mi]
        ntok = mm["ntok"]
        sl = ei % 2
        wf = wfm[mi % 2]; rwf = r_wfm[mi % 2]
        EM("gpsimd", lambda e, ei=ei, wf=wf, ntok=ntok: e.dma_start(
            out=ubuf[0:1, 0:ntok], in_=wf[ei:ei + 1, 0:ntok]),
           reads=[rwf], writes=[r_ubuf], dma_sem="sRW")
        EM("tensor", lambda e, ntok=ntok: e.matmul(
            out=BANK(6)[0:128, 0:ntok], lhsT=ones1[:], rhs=ubuf[0:1, 0:ntok],
            start=True, stop=True),
           reads=[r_ubuf, r_wres], acq=[r_bank[6]], writes=[r_bank[6]], bump=True)
        for ic in range(4):
            b = 4 + ic % 2
            for lc in range(4):
                EM("tensor", lambda e, ic=ic, lc=lc, b=b, sl=sl, ntok=ntok: e.matmul(
                    out=BANK(b)[0:128, 0:ntok],
                    lhsT=w1b[sl][:, lc * 512 + ic * 128: lc * 512 + (ic + 1) * 128],
                    rhs=lat_sb[:, lc * 384:lc * 384 + ntok],
                    start=(lc == 0), stop=(lc == 3)),
                   reads=[r_w1b[sl], r_lat], acq=[r_bank[b]] if lc == 0 else [],
                   writes=[r_bank[b]] if lc == 3 else [], bump=(lc == 3))
            tr = trelu[0]; rtr = r_trelu[0]
            EM("vector", lambda e, b=b, tr=tr, ntok=ntok: e.tensor_scalar_max(
                out=tr[:, 0:ntok], in0=BANK(b)[0:128, 0:ntok], scalar1=0.0),
               reads=[r_bank[b]], writes=[rtr])
            EM("vector", lambda e, tr=tr, ntok=ntok: e.tensor_mul(
                out=ttw[:, 0:ntok], in0=tr[:, 0:ntok], in1=BANK(6)[0:128, 0:ntok]),
               reads=[rtr, r_bank[6]], writes=[r_ttw])
            EM("gpsimd", lambda e, ic=ic, tr=tr, ntok=ntok: e.tensor_mul(
                out=interbf[:, ic * 384:ic * 384 + ntok], in0=tr[:, 0:ntok], in1=ttw[:, 0:ntok]),
               reads=[rtr, r_ttw], writes=[r_interbf], bump=True)
        for lc in range(4):
            for ic in range(4):
                first = (ei == 0 and ic == 0)
                last = (ei == E - 1 and ic == 3)
                EM("tensor", lambda e, lc=lc, ic=ic, sl=sl, first=first, last=last, ntok=ntok: e.matmul(
                    out=BANK(lc)[0:128, 0:ntok],
                    lhsT=w2b[sl][:, ic * 512 + lc * 128: ic * 512 + (lc + 1) * 128],
                    rhs=interbf[:, ic * 384:ic * 384 + ntok],
                    start=first, stop=last),
                   reads=[r_w2b[sl], r_interbf],
                   acq=[r_bank[lc]] if first else [],
                   writes=[r_bank[lc]] if last else [], bump=last)

    def emit_P8(mi):
        mm = MACROS[mi]
        ntok = mm["ntok"]
        for lc in range(4):
            EM("vector", lambda e, lc=lc, ntok=ntok: e.tensor_copy(
                out=combbf[:, lc * 384:lc * 384 + ntok], in_=BANK(lc)[0:128, 0:ntok]),
               reads=[r_bank[lc]], writes=[r_combbf], bump=True)
        for g in range(mm["ngrp"]):
            for half in range(2):
                b = 2 * g + half
                for lc in range(4):
                    EM("tensor", lambda e, g=g, half=half, lc=lc, b=b: e.matmul(
                        out=BANK(b)[0:128, 0:512],
                        lhsT=combbf[:, lc * 384 + g * 128: lc * 384 + g * 128 + 128],
                        rhs=fc2_sb[:, lc * D + half * 512: lc * D + half * 512 + 512],
                        start=(lc == 0), stop=False),
                       reads=[r_combbf, r_wres],
                       acq=[r_bank[b]] if lc == 0 else [], writes=[], bump=False)

    def emit_P9(mi):
        mm = MACROS[mi]
        ntok = mm["ntok"]
        xn = xn_sb[mi % 2]; rxn = r_xn[mi % 2]
        xnv = xn[:].rearrange("p (c t) -> p c t", c=8)
        def emit_down(oc):
            sl = oc % 2
            for g in range(mm["ngrp"]):
                for half in range(2):
                    fb = 2 * g + half
                    last = (oc == 31)
                    EM("tensor", lambda e, oc=oc, half=half, fb=fb, sl=sl, g=g, last=last: e.matmul(
                        out=BANK(fb)[0:128, 0:512],
                        lhsT=upbf[oc % 2][:, g * 128:g * 128 + 128],
                        rhs=dnb[sl][:, half * 512:half * 512 + 512],
                        start=False, stop=last),
                       reads=[r_upbf[oc % 2], r_dnb[sl]],
                       writes=[r_bank[fb]] if last else [], bump=last)
        for oc in range(32):
            sl = oc % 2
            b = 6 + oc % 2
            for dc in range(8):
                EM("tensor", lambda e, dc=dc, b=b, sl=sl, xnv=xnv, ntok=ntok: e.matmul(
                    out=BANK(b)[0:128, 0:ntok],
                    lhsT=upb[sl][:, dc * 128:(dc + 1) * 128],
                    rhs=xnv[:, dc, 12:12 + ntok],
                    start=(dc == 0), stop=(dc == 7)),
                   reads=[r_upb[sl], rxn], acq=[r_bank[b]] if dc == 0 else [],
                   writes=[r_bank[b]] if dc == 7 else [], bump=(dc == 7))
            tr = trelu[0]; rtr = r_trelu[0]
            EM("vector", lambda e, b=b, tr=tr, ntok=ntok: e.tensor_scalar_max(
                out=tr[:, 0:ntok], in0=BANK(b)[0:128, 0:ntok], scalar1=0.0),
               reads=[r_bank[b]], writes=[rtr])
            EM("scalar", lambda e, oc=oc, tr=tr, ntok=ntok: e.activation(
                out=upbf[oc % 2][:, 0:ntok], in_=tr[:, 0:ntok], func=AF.Square),
               reads=[rtr], writes=[r_upbf[oc % 2]], bump=True)
            sched.bump("vector")
            if oc > 0:
                emit_down(oc - 1)
            if oc + 2 < 32:
                emit_stream_up(oc + 2)
            if 2 <= oc + 1 < 32:
                emit_stream_dn(oc + 1)
        emit_down(31)

    def emit_P10(mi, groups=None):
        mm = MACROS[mi]
        if groups is None:
            groups = list(range(mm["ngrp"]))
        for g in groups:
            t0 = mm["t0"] + g * 128
            xr = x_res[0]; rxr = r_xres[0]
            EM("scalar", lambda e, t0=t0, xr=xr: e.dma_start(out=xr[:], in_=x_in[t0:t0 + 128, :]),
               reads=[rxr], writes=[rxr], dma_sem="sX")
            ob = outb[0]; rob = r_outb[0]
            for half in range(2):
                fb = 2 * g + half
                hs = slice(half * 512, half * 512 + 512)
                EM("vector", lambda e, fb=fb, hs=hs, ob=ob: e.tensor_copy(
                    out=ob[:, hs], in_=BANK(fb)),
                   reads=[r_bank[fb]], writes=[rob, r_bank[fb]], bump=True)
                EM("scalar", lambda e, hs=hs, ob=ob: e.activation(
                    out=ttn[:, hs], in_=ob[:, hs], func=AF.Tanh, scale=0.5),
                   reads=[rob], writes=[r_ttn], bump=True)
                EM("vector", lambda e, hs=hs, ob=ob: e.scalar_tensor_tensor(
                    out=g2b[:, hs], in0=ttn[:, hs], scalar=1.0, in1=ob[:, hs],
                    op0=ALU.add, op1=ALU.mult),
                   reads=[r_ttn, rob], writes=[r_g2], bump=True)
                EM("vector", lambda e, g=g, half=half, hs=hs: e.scalar_tensor_tensor(
                    out=ttn[:, hs], in0=g2b[:, hs], scalar=0.5,
                    in1=s_sb[:, g * D + half * 512:g * D + half * 512 + 512],
                    op0=ALU.mult, op1=ALU.mult),
                   reads=[r_g2, r_s], writes=[r_ttn])
                EM("vector", lambda e, hs=hs, ob=ob, xr=xr: e.tensor_add(
                    out=ob[:, hs], in0=ttn[:, hs], in1=xr[:, hs]),
                   reads=[r_ttn, rxr], writes=[rob], bump=True)
            EM("scalar", lambda e, t0=t0, ob=ob: e.dma_start(out=out_d[t0:t0 + 128, :], in_=ob[:]),
               reads=[rob], writes=[r_out], dma_sem="sO")

    ORDER = ["init", "P012", "P3", "P4", "P5", "P6", "P7", "P8", "P9", "full"]
    lvl = ORDER.index(stop_after)
    if lvl >= 1:
        emit_P012(0)
        for g0 in range(3):
            emit_wtrans(0, g0)
    if lvl >= 2:
        emit_P3(0)
    for mi in range(n_macro):
        if lvl >= 3:
            emit_P4(mi)
        if lvl >= 4:
            emit_P5(mi)
        if lvl >= 5:
            emit_P6(mi)
        if lvl >= 6:
            emit_stream_w12(0)
            emit_stream_w12(1)
            for ei in range(E):
                emit_P7_expert(mi, ei)
                if ei + 2 < E:
                    emit_stream_w12(ei + 2)
                if mi + 1 < n_macro and ei in (2, 4, 6):
                    emit_P012(mi + 1, groups=[ei // 2 - 1])
                if mi + 1 < n_macro and ei in (3, 5, 7):
                    emit_wtrans(mi + 1, ei // 2 - 1)
        elif mi + 1 < n_macro and lvl >= 2:
            emit_P012(mi + 1)
        if lvl >= 7:
            emit_P8(mi)
        if lvl >= 8:
            emit_stream_updn(0)
            emit_stream_updn(1)
            emit_P9(mi)
        if lvl >= 9 and lvl >= 2 and mi + 1 < n_macro:
            emit_P3(mi + 1, part=0)
            emit_P10(mi, groups=[0])
            emit_P3(mi + 1, part=1)
            emit_P10(mi, groups=[1])
            emit_P3(mi + 1, part=2)
            emit_P10(mi, groups=[g for g in range(2, MACROS[mi]["ngrp"])])
        else:
            if lvl >= 9:
                emit_P10(mi)
            if lvl >= 2 and mi + 1 < n_macro:
                emit_P3(mi + 1)

    if debug_dump:
        dbg_specs = [("xn_sb0", xn_sb[0], r_xn[0]), ("q_sb", q_sb, r_q), ("k_sb", k_sb, r_k),
                     ("lat_sb", lat_sb, r_lat), ("s_sb", s_sb, r_s), ("attn_sb", attn_sb, r_attn),
                     ("combbf", combbf, r_combbf), ("v_sb0", v_sb[0], r_v[0]),
                     ("bdm0", bdm[0], r_bdm[0]), ("bdm1", bdm[1], r_bdm[1]),
                     ("ubuf_d", ubuf, r_ubuf), ("wfm0", wfm[0], r_wfm[0])]
        for nm, tt, rr_ in dbg_specs:
            shp = list(tt[:].shape)
            dt_ = tt[:].dtype
            od = nc.dram_tensor("dbg_" + nm, shp, dt_, kind="ExternalOutput").ap()
            EM("sync", lambda e, od=od, tt=tt: e.dma_start(out=od, in_=tt[:]),
               reads=[rr_], writes=[], dma_sem="sO")
    for sname in ("sO", "sW10", "sW11", "sW20", "sW21", "sU0", "sU1", "sD0", "sD1",
                  "sX", "sBW0", "sBW1", "sBM0", "sBM1", "sRW", "sI", "sY0", "sG0", "sG1"):
        if sched.cnt[sname]:
            sched.prog["sync"].append(("wait", sname, sched.cnt[sname]))

    print("prog sizes", {k: len(v) for k, v in sched.prog.items()},
          "sems", {k: v for k, v in sched.cnt.items()}, "spacers", sched.nops)
    with nc.Block() as block:
        sched.replay(block)
    if want_handles:
        h = dict(xn_sb=xn_sb, q_sb=q_sb, k_sb=k_sb, v_sb=v_sb, attn_sb=attn_sb,
                 s_sb=s_sb, lat_sb=lat_sb, combbf=combbf, wfm=wfm, relk_sb=relk_sb,
                 qscale=qscale, bdm=bdm, ubuf=ubuf, tbuf=tbuf, sums=sums, inv=inv,
                 ptbf=ptbf, rt_sig=rt_sig, rt_W=rt_W, rt_idx=rt_idx, biasb=biasb,
                 interbf=interbf, posfm=posfm, rt_sfc=rt_sfc, rt_m=rt_m, rt_sel=rt_sel,
                 pbank=pbank, upbf=upbf, trelu=trelu)
        return nc, h, es
    es.close()
    n_ins = sum(len(v) for v in sched.prog.values())
    return nc


_NC_CACHE = {}

def _get_nc():
    if "nc" not in _NC_CACHE:
        _NC_CACHE["nc"] = build_nc()
    return _NC_CACHE["nc"]


def kernel(L0_attn_hidden_states, L0_attn_position_embeddings, L0_attn_Wq, L0_attn_Wk,
           L0_attn_Wv, L0_attn_Wpost, L0_attn_Wrel, L0_attn_per_dim_scale,
           L0_ffn_gate, L0_ffn_corr_bias, L0_ffn_w1, L0_ffn_w2, L0_ffn_fc1_w,
           L0_ffn_fc2_w, L0_ffn_up_w, L0_ffn_down_w):
    f32 = np.float32
    Ct = lambda a: np.ascontiguousarray(np.asarray(a, dtype=f32))
    x = Ct(L0_attn_hidden_states)
    base = {
        "wqT": Ct(np.asarray(L0_attn_Wq, f32).T),
        "wkT": Ct(np.asarray(L0_attn_Wk, f32).T),
        "wvT": Ct(np.asarray(L0_attn_Wv, f32).T),
        "wpT": Ct(np.asarray(L0_attn_Wpost, f32).T),
        "wrT": Ct(np.asarray(L0_attn_Wrel, f32).T),
        "posemb": Ct(L0_attn_position_embeddings),
        "pds": Ct(L0_attn_per_dim_scale),
        "gateT": Ct(np.asarray(L0_ffn_gate, f32).T),
        "corrb": Ct(L0_ffn_corr_bias),
        "w1T": Ct(np.asarray(L0_ffn_w1, f32).transpose(0, 2, 1)),
        "w2T": Ct(np.asarray(L0_ffn_w2, f32).transpose(0, 2, 1)),
        "fc1T": Ct(np.asarray(L0_ffn_fc1_w, f32).T),
        "fc2T": Ct(np.asarray(L0_ffn_fc2_w, f32).T),
        "upT": Ct(np.asarray(L0_ffn_up_w, f32).T),
        "dnT": Ct(np.asarray(L0_ffn_down_w, f32).T),
        "idbf": np.eye(128, dtype=ml_dtypes.bfloat16),
        "idf": np.eye(128, dtype=np.float32),
        "iota8": np.tile(np.arange(8, dtype=np.float32), (128, 1)),
    }
    in_maps = [dict(base, x=np.ascontiguousarray(x[b])) for b in range(8)]
    nc = _get_nc()
    res = run_bass_kernel_spmd(nc, in_maps, core_ids=list(range(8)))
    return np.stack([np.asarray(r["out"]) for r in res.results], axis=0)


if __name__ == "__main__":
    print("building nc...")
    import time
    t = time.time()
    nc = build_nc()
    print("built in", time.time() - t, "s")


# version 23 (final)
# speedup vs baseline: 1.1026x; 1.1026x over previous
import math
import numpy as np
import ml_dtypes
from contextlib import ExitStack

import concourse.bass as bass
import concourse.mybir as mybir
from concourse.bass_utils import run_bass_kernel_spmd

F32 = mybir.dt.float32
BF16 = mybir.dt.bfloat16
U32 = mybir.dt.uint32
AF = mybir.ActivationFunctionType
ALU = mybir.AluOpType
AX = mybir.AxisListType

S = 4096
D = 1024
H = 8
CH = 12
CTX = 24
NPOS = 25
SCAP = 50.0
Q_SCALE = (128 ** -0.5) / math.log(2)
K_SCALE = math.log(1.0 + math.e) / math.log(2)
E = 8
LAT = 512

MACRO = 384
N_MACRO = 11


def _macros():
    ms = []
    for m in range(N_MACRO):
        ntok = MACRO if m < 10 else 256
        nblk = 32 if m < 10 else 22
        subs = []
        b = 0
        while b < nblk:
            nb = min(8, nblk - b)
            subs.append((12 * b, nb))
            b += nb
        ms.append(dict(m=m, t0=m * MACRO, ntok=ntok, ngrp=ntok // 128, nblk=nblk, subs=subs))
    return ms

MACROS = _macros()


class Res:
    __slots__ = ("name", "wr", "rd")
    def __init__(self, name):
        self.name = name
        self.wr = []
        self.rd = []


class Sched:
    ENGINES = ("tensor", "vector", "scalar", "sync", "gpsimd")

    def __init__(self, nc):
        self.nc = nc
        self.prog = {e: [] for e in self.ENGINES}
        self.cnt = {}
        self.sem = {}
        self.last_wait = {e: {} for e in self.ENGINES}
        self.pend_wr = {e: [] for e in self.ENGINES}
        self.pend_rd = {e: [] for e in self.ENGINES}
        self.pw_eng = {}
        self.pr_eng = {}
        self.esem = {"tensor": "cT", "vector": "cV", "scalar": "cS",
                     "sync": "cY", "gpsimd": "cG"}
        self.lastsmall = {e: set() for e in self.ENGINES}
        self.junk = None
        self.nops = 0

    def add_sem(self, name, handle):
        self.sem[name] = handle
        self.cnt[name] = 0

    def _wait(self, eng, sem, val):
        if val <= 0:
            return
        assert val <= self.cnt[sem], \
            f"wait {sem}>={val} but only {self.cnt[sem]} emitted (engine {eng})"
        lw = self.last_wait[eng]
        if lw.get(sem, 0) >= val:
            return
        lw[sem] = val
        self.prog[eng].append(("wait", sem, val))

    def bump(self, eng):
        sname = self.esem[eng]
        prog = self.prog[eng]
        for i in range(len(prog) - 1, -1, -1):
            if prog[i][0] == "op":
                incs = prog[i][2]
                for (s, _v) in incs:
                    if s == sname:
                        return self.cnt[sname]
                self.cnt[sname] += 1
                incs.append((sname, 1))
                c = self.cnt[sname]
                for r in self.pend_wr[eng]:
                    r.wr = [(sname, c)]
                    r.rd = []
                    self.pw_eng.pop(id(r), None)
                for r in self.pend_rd[eng]:
                    r.rd.append((sname, c))
                    s_ = self.pr_eng.get(id(r))
                    if s_:
                        s_.discard(eng)
                self.pend_wr[eng] = []
                self.pend_rd[eng] = []
                return c
        raise RuntimeError(f"bump with no op on {eng}")

    def _resolve_cross(self, eng, reses):
        for r in reses:
            pe = self.pw_eng.get(id(r))
            if pe is not None and pe != eng:
                self.bump(pe)
            for pe2 in list(self.pr_eng.get(id(r), ())):
                if pe2 != eng:
                    self.bump(pe2)

    def emit(self, eng, fn, reads=(), writes=(), acq=(), dma_sem=None, small=False, bump=False,
             dma_wait_prior=True):
        reads = list(reads); writes = list(writes); acq = list(acq)
        self._resolve_cross(eng, reads + writes + acq)
        if eng in ("vector", "scalar"):
            if any(r.name in self.lastsmall[eng] for r in reads):
                j0, j1 = self.junk[eng]
                self.prog[eng].append(("op", j0, []))
                self.prog[eng].append(("op", j1, []))
                self.nops += 2
                self.lastsmall[eng] = set()
        for r in reads:
            for (s, v) in r.wr:
                self._wait(eng, s, v)
        for w in writes + acq:
            for (s, v) in w.rd:
                self._wait(eng, s, v)
            for (s, v) in w.wr:
                if s != self.esem[eng]:
                    self._wait(eng, s, v)
        if dma_sem is not None and dma_wait_prior:
            self._wait(eng, dma_sem, self.cnt[dma_sem])
        incs = []
        self.prog[eng].append(("op", fn, incs))
        if dma_sem is not None:
            self.cnt[dma_sem] += 16
            incs.append((dma_sem, 16))
            c = self.cnt[dma_sem]
            for w in writes:
                w.wr = [(dma_sem, c)]
                w.rd = []
            for r in reads:
                r.rd.append((dma_sem, c))
        else:
            for w in writes:
                self.pend_wr[eng].append(w)
                self.pw_eng[id(w)] = eng
            for r in reads:
                self.pend_rd[eng].append(r)
                self.pr_eng.setdefault(id(r), set()).add(eng)
            if bump:
                self.bump(eng)
        if eng in ("vector", "scalar"):
            self.lastsmall[eng] = set(w.name for w in writes) if small else set()

    def replay(self, block):
        def make(eng_name):
            def run(eng):
                for item in self.prog[eng_name]:
                    if item[0] == "wait":
                        eng.wait_ge(self.sem[item[1]], item[2])
                    else:
                        ins = item[1](eng)
                        for (s, v) in item[2]:
                            ins.then_inc(self.sem[s], v)
            return run
        block.tensor(make("tensor"))
        block.vector(make("vector"))
        block.scalar(make("scalar"))
        block.sync(make("sync"))
        block.gpsimd(make("gpsimd"))


def build_nc(debug_stage="full", n_macro=N_MACRO, want_handles=False, stop_after="full", debug_dump=False):
    nc = bass.Bass("TRN2", target_bir_lowering=False, debug=False,
                   detect_race_conditions=False)

    x_in = nc.dram_tensor("x", [S, D], F32, kind="ExternalInput").ap()
    wqT = nc.dram_tensor("wqT", [D, D], F32, kind="ExternalInput").ap()
    wkT = nc.dram_tensor("wkT", [D, D], F32, kind="ExternalInput").ap()
    wvT = nc.dram_tensor("wvT", [D, D], F32, kind="ExternalInput").ap()
    wpT = nc.dram_tensor("wpT", [D, D], F32, kind="ExternalInput").ap()
    wrT = nc.dram_tensor("wrT", [D, D], F32, kind="ExternalInput").ap()
    posemb = nc.dram_tensor("posemb", [NPOS, D], F32, kind="ExternalInput").ap()
    pds = nc.dram_tensor("pds", [128], F32, kind="ExternalInput").ap()
    gateT = nc.dram_tensor("gateT", [D, E], F32, kind="ExternalInput").ap()
    corrb = nc.dram_tensor("corrb", [E], F32, kind="ExternalInput").ap()
    w1T = nc.dram_tensor("w1T", [E, LAT, 512], F32, kind="ExternalInput").ap()
    w2T = nc.dram_tensor("w2T", [E, 512, LAT], F32, kind="ExternalInput").ap()
    fc1T = nc.dram_tensor("fc1T", [D, LAT], F32, kind="ExternalInput").ap()
    fc2T = nc.dram_tensor("fc2T", [LAT, D], F32, kind="ExternalInput").ap()
    upT = nc.dram_tensor("upT", [D, 4096], F32, kind="ExternalInput").ap()
    dnT = nc.dram_tensor("dnT", [4096, D], F32, kind="ExternalInput").ap()
    idbf_in = nc.dram_tensor("idbf", [128, 128], BF16, kind="ExternalInput").ap()
    idf_in = nc.dram_tensor("idf", [128, 128], F32, kind="ExternalInput").ap()
    iota_in = nc.dram_tensor("iota8", [128, E], F32, kind="ExternalInput").ap()

    out_d = nc.dram_tensor("out", [S, D], F32, kind="ExternalOutput").ap()

    bdscr = nc.dram_tensor("bdscr", [2, H, 8, 300], F32, kind="Internal").ap()
    w1s = nc.dram_tensor("w1s", [E, LAT, 512], BF16, kind="Internal").ap()
    w2s = nc.dram_tensor("w2s", [E, 512, LAT], BF16, kind="Internal").ap()
    ups = nc.dram_tensor("ups", [D, 4096], BF16, kind="Internal").ap()
    dns = nc.dram_tensor("dns", [4096, D], BF16, kind="Internal").ap()

    es = ExitStack()
    def sb(name, shape, dtype=F32):
        return es.enter_context(nc.sbuf_tensor(name, shape, dtype))

    wq_sb = sb("wq_sb", [128, 8 * D], BF16)
    wk_sb = sb("wk_sb", [128, 8 * D], BF16)
    wv_sb = sb("wv_sb", [128, 8 * D], BF16)
    wp_sb = sb("wp_sb", [128, 8 * D], BF16)
    fc1_sb = sb("fc1_sb", [128, 8 * LAT], BF16)
    fc2_sb = sb("fc2_sb", [128, 4 * D], BF16)
    gate_sb = sb("gate_sb", [128, 8 * E], F32)
    relk_sb = sb("relk_sb", [128, H * NPOS], BF16)
    posfm = sb("posfm", [128, 8 * NPOS], BF16)
    idbf = sb("idbf_s", [128, 128], BF16)
    idf = sb("idf_s", [128, 128], F32)
    iota8 = sb("iota8_s", [128, E], F32)
    ones1 = sb("ones1", [1, 128], F32)
    corrb_sb = sb("corrb_sb", [1, E], F32)
    biasb = sb("biasb", [128, E], F32)
    qscale = sb("qscale", [128, 1], F32)
    pds_sb = sb("pds_sb", [128, 4], F32)

    x_ld = sb("x_ld", [128, D], F32)
    x_res = [x_ld]
    xn32 = sb("xn32", [128, D], F32)
    xnfm32 = sb("xnfm32", [128, D], F32)
    xn_sb = [sb(f"xn_sb{i}", [128, 8 * 396], BF16) for i in range(2)]
    q_sb = sb("q_sb", [128, H * 384], BF16)
    k_sb = sb("k_sb", [128, H * 396], BF16)
    v_sb = [sb(f"v_sb{i}", [120, 1024], BF16) for i in range(4)]
    attn_sb = sb("attn_sb", [128, H * 384], BF16)
    s_sb = sb("s_sb", [128, 3 * D], F32)
    lat_sb = sb("lat_sb", [128, 4 * 384], BF16)

    raw1_sb = [sb("raw1_sb0", [96, H * NPOS], F32)]
    bdm = [sb(f"bdm{i}", [96, H * 108], F32) for i in range(2)]
    ubuf = sb("ubuf", [96, H * 108], F32)
    tbuf = sb("tbuf", [96, H * 108], F32)
    sums = sb("sums", [96, H], F32)
    inv = sb("inv", [96, H], F32)
    ptbf = sb("ptbf", [108, H * 96], BF16)

    rt_sig = sb("rt_sig", [128, E], F32)
    rt_tt = sb("rt_tt", [128, E], F32)
    rt_sfc = sb("rt_sfc", [128, E], F32)
    rt_pad = sb("rt_pad", [128, 16], F32)
    rt_srt = sb("rt_srt", [128, 16], F32)
    rt_gs = sb("rt_gs", [128, 4], F32)
    rt_m = sb("rt_m", [128, E], F32)
    rt_sm = sb("rt_sm", [128, E], F32)
    rt_idx = sb("rt_idx", [128, E], U32)
    rt_idxf = sb("rt_idxf", [128, 2], F32)
    rt_sel = sb("rt_sel", [128, 8], F32)
    rt_eq = sb("rt_eq", [128, 2 * E], F32)
    rt_W = sb("rt_W", [128, E], F32)
    rt_Wg = sb("rt_Wg", [128, 3 * E], F32)
    wfm = [sb(f"wfm{i}", [8, 384], F32) for i in range(2)]

    lnst = sb("lnst", [128, 36], F32)
    lnmv = sb("lnmv", [128, 6], F32)
    lnv = sb("lnv", [128, 3], F32)
    lny = sb("lny", [128, 6], F32)
    lnt = sb("lnt", [128, 6], F32)

    scr34 = sb("scr34", [128, 864], F32)
    trelu = [scr34[:, 0:384]]
    ttw = scr34[:, 384:768]
    interbf = sb("interbf", [128, 4 * 384], BF16)
    combbf = sb("combbf", [128, 4 * 384], BF16)
    upbf = [sb(f"upbf{i}", [128, 384], BF16) for i in range(2)]
    g2b = xnfm32
    ttn = xn32
    outb = [sb("outb0", [128, D], F32)]

    w1b = [sb(f"w1b{i}", [128, 4 * 512], BF16) for i in range(2)]
    w2b = [sb(f"w2b{i}", [128, 4 * 512], BF16) for i in range(2)]
    upb = [sb(f"upb{i}", [128, 8 * 128], BF16) for i in range(2)]
    dnb = [sb(f"dnb{i}", [128, D], BF16) for i in range(2)]
    junkv = sb("junkv", [128, 8], F32)
    junks = junkv

    pbank = [es.enter_context(nc.psum_tensor(f"pbank{i}", [128, 512], F32)) for i in range(8)]
    def BANK(i):
        return pbank[i][:]

    sched = Sched(nc)
    for sname in ("cT", "cV", "cS", "cG",
                  "sW10", "sW11", "sW20", "sW21", "sU0", "sU1", "sD0", "sD1",
                  "sX", "sO", "sBW0", "sBW1", "sBM0", "sBM1", "sRW", "sI", "sY0", "sG0", "sG1", "sC0", "sC1", "sC2", "sC3", "sBM2", "sBM3", "sBM4", "sBM5", "sBM6", "sBM7"):
        sched.add_sem(sname, es.enter_context(nc.semaphore(sname)))
    sched.junk = {
        "vector": [(lambda v: v.reduce_sum(out=junkv[:, 0:1], in_=wq_sb[:, 0:32], axis=AX.X)),
                   (lambda v: v.reduce_sum(out=junkv[:, 4:5], in_=wq_sb[:, 32:64], axis=AX.X))],
        "scalar": [(lambda s_: s_.copy(out=junks[0:64, 0:4], in_=iota8[0:64, 0:4])),
                   (lambda s_: s_.copy(out=junks[0:64, 4:8], in_=iota8[0:64, 4:8]))],
    }

    r_bank = [Res(f"bank{i}") for i in range(8)]
    r_wres = Res("wres")
    r_xld = Res("x_ld")
    r_xn32 = Res("xn32")
    r_xnfm32 = Res("xnfm32")
    r_xn = [Res(f"xn{i}") for i in range(2)]
    r_q = Res("q"); r_k = Res("k")
    r_v = [Res(f"v{i}") for i in range(4)]
    r_attn = Res("attn"); r_s = Res("s"); r_lat = Res("lat")
    r_raw1 = [Res("raw1_0")]
    r_bdscr = [Res(f"bdscr{i}") for i in range(2)]
    r_bdm = [Res(f"bdm{i}") for i in range(2)]
    r_scr = Res("scr34")
    r_ubufh = [Res("ubuf0"), Res("ubuf1")]
    r_tbh = [[Res("tb00"), Res("tb01")], [r_scr, r_scr]]
    r_ubuf = r_ubufh[0]
    r_sums = Res("sums"); r_inv = Res("inv")
    r_ptbf = Res("ptbf")
    r_rt = Res("rt"); r_idx = Res("rt_idx"); r_W = Res("rt_W"); r_Wg = Res("rt_Wg")
    r_wfm = [Res(f"wfm{i}") for i in range(2)]
    r_ln = Res("ln")
    r_trelu = [r_scr]
    r_ttw = r_scr
    r_interbf = Res("interbf"); r_combbf = Res("combbf")
    r_upbf = [Res(f"upbf{i}") for i in range(2)]
    r_outb = [Res("outb0")]
    r_w1b = [Res(f"w1b{i}") for i in range(2)]
    r_w2b = [Res(f"w2b{i}") for i in range(2)]
    r_upb = [Res(f"upb{i}") for i in range(2)]
    r_dnb = [Res(f"dnb{i}") for i in range(2)]
    r_wscr = Res("wscr")
    r_postm = Res("pos_tm"); r_posfm = Res("posfm")
    r_out = Res("out")
    r_g2 = r_xnfm32
    r_ttn = r_xn32
    r_xres = [r_xld]

    EM = sched.emit

    EM("sync", lambda e: e.dma_start(out=idbf[:], in_=idbf_in), writes=[r_wres], dma_sem="sY0")
    EM("sync", lambda e: e.dma_start(out=idf[:], in_=idf_in), writes=[r_wres], dma_sem="sY0")
    EM("sync", lambda e: e.dma_start(out=iota8[:], in_=iota_in), writes=[r_wres], dma_sem="sY0")
    EM("sync", lambda e: e.dma_start(out=pds_sb[:, 0:1], in_=pds.rearrange("(p o) -> p o", o=1)),
       writes=[r_wres], dma_sem="sY0")
    EM("sync", lambda e: e.dma_start(out=corrb_sb[:], in_=corrb.rearrange("(o e) -> o e", o=1)),
       writes=[r_wres], dma_sem="sY0")
    EM("sync", lambda e: e.dma_start(out=gate_sb[:].rearrange("p (c e) -> p c e", c=8),
                                     in_=gateT.rearrange("(c p) e -> p c e", p=128)),
       writes=[r_wres], dma_sem="sY0")
    EM("sync", lambda e: e.dma_start(out=x_ld[0:25, :], in_=posemb), writes=[r_postm], dma_sem="sY0")

    for (dst, src) in ((wq_sb, wqT), (wk_sb, wkT), (wv_sb, wvT), (wp_sb, wpT)):
        EM("gpsimd", lambda e, d=dst, s_=src: e.dma_start(
            out=d[:].rearrange("p (c o) -> p c o", c=8),
            in_=s_.rearrange("(c p) o -> p c o", p=128)),
           writes=[r_wres], dma_sem="sI")
    EM("gpsimd", lambda e: e.dma_start(out=fc1_sb[:].rearrange("p (c o) -> p c o", c=8),
                                       in_=fc1T.rearrange("(c p) o -> p c o", p=128)),
       writes=[r_wres], dma_sem="sI")
    EM("gpsimd", lambda e: e.dma_start(out=fc2_sb[:].rearrange("p (c o) -> p c o", c=4),
                                       in_=fc2T.rearrange("(c p) o -> p c o", p=128)),
       writes=[r_wres], dma_sem="sI")

    sched._wait("vector", "sI", sched.cnt["sI"])
    sched._wait("vector", "sY0", sched.cnt["sY0"])
    EM("vector", lambda e: e.memset(ones1[:], 1.0), writes=[r_wres, r_postm], bump=True)
    EM("scalar", lambda e: e.activation(out=pds_sb[:, 1:2], in_=pds_sb[:, 0:1], func=AF.Abs),
       reads=[r_wres], writes=[r_wres], small=True)
    EM("scalar", lambda e: e.activation(out=pds_sb[:, 2:3], in_=pds_sb[:, 1:2], func=AF.Exp, scale=-1.0),
       reads=[r_wres], writes=[r_wres], small=True)
    EM("vector", lambda e: e.tensor_scalar_add(out=pds_sb[:, 3:4], in0=pds_sb[:, 2:3], scalar1=1.0),
       reads=[r_wres], writes=[r_wres], small=True, bump=True)
    EM("scalar", lambda e: e.activation(out=pds_sb[:, 2:3], in_=pds_sb[:, 3:4], func=AF.Ln),
       reads=[r_wres], writes=[r_wres], small=True)
    EM("vector", lambda e: e.tensor_scalar_max(out=pds_sb[:, 1:2], in0=pds_sb[:, 0:1], scalar1=0.0),
       reads=[r_wres], writes=[r_wres], small=True, bump=True)
    EM("vector", lambda e: e.tensor_add(out=pds_sb[:, 3:4], in0=pds_sb[:, 1:2], in1=pds_sb[:, 2:3]),
       reads=[r_wres], writes=[r_wres], small=True)
    EM("vector", lambda e: e.tensor_scalar_mul(out=qscale[:], in0=pds_sb[:, 3:4], scalar1=Q_SCALE),
       reads=[r_wres], writes=[r_wres], small=True, bump=True)
    EM("tensor", lambda e: e.matmul(out=BANK(7)[0:128, 0:E], lhsT=ones1[:],
                                    rhs=corrb_sb[:], start=True, stop=True),
       reads=[r_wres], acq=[r_bank[7]], writes=[r_bank[7]], bump=True)
    EM("vector", lambda e: e.tensor_copy(out=biasb[:], in_=BANK(7)[0:128, 0:E]),
       reads=[r_bank[7]], writes=[r_wres], bump=True)


    for dc in range(8):
        EM("tensor", lambda e, dc=dc: e.transpose(out=BANK(6)[0:128, 0:NPOS],
                                                  in_=x_ld[0:25, dc * 128:(dc + 1) * 128],
                                                  identity=idf[0:25, 0:25]),
           reads=[r_postm, r_wres], acq=[r_bank[6]], writes=[r_bank[6]], bump=True)
        EM("vector", lambda e, dc=dc: e.tensor_copy(out=posfm[:, dc * NPOS:(dc + 1) * NPOS],
                                                    in_=BANK(6)[0:128, 0:NPOS]),
           reads=[r_bank[6]], writes=[r_posfm], bump=True)
    for quarter in range(4):
        buf = (w1b[0], w2b[0])[quarter % 2]
        rbuf = (r_w1b[0], r_w2b[0])[quarter % 2]
        ssem = ("sG0", "sG1")[quarter % 2]
        EM("gpsimd", lambda e, q=quarter, buf=buf: e.dma_start(
            out=buf[:, 0:2048].rearrange("p (c o) -> p c o", c=2),
            in_=wrT[q * 256:(q + 1) * 256, :].rearrange("(c p) o -> p c o", p=128)),
           reads=[rbuf], writes=[rbuf], dma_sem=ssem)
        for j in range(2):
            dc = quarter * 2 + j
            for h in range(H):
                EM("tensor", lambda e, j=j, dc=dc, h=h, buf=buf: e.matmul(
                    out=BANK(6)[0:128, h * NPOS:(h + 1) * NPOS],
                    lhsT=buf[:, j * 1024 + h * 128: j * 1024 + (h + 1) * 128],
                    rhs=posfm[:, dc * NPOS:(dc + 1) * NPOS],
                    start=(dc == 0 and h == 0), stop=(dc == 7 and h == H - 1)),
                   reads=[rbuf, r_posfm],
                   acq=[r_bank[6]] if (dc == 0 and h == 0) else [],
                   writes=[r_bank[6]] if (dc == 7 and h == H - 1) else [],
                   bump=(dc == 7 and h == H - 1))
    EM("vector", lambda e: e.tensor_scalar_mul(out=relk_sb[:],
                                               in0=BANK(6)[0:128, 0:H * NPOS], scalar1=1.0 / SCAP),
       reads=[r_bank[6]], writes=[r_wres], bump=True)

    for (dst_d, src_d, ssem) in ((ups, upT, "sC2"), (dns, dnT, "sC3"),
                                 (w1s.rearrange("e l i -> (e l) i"), w1T.rearrange("e l i -> (e l) i"), "sC0"),
                                 (w2s.rearrange("e i l -> (e i) l"), w2T.rearrange("e i l -> (e i) l"), "sC1")):
        EM("gpsimd", lambda e, d=dst_d, s_=src_d: e.dma_start(out=d, in_=s_),
           writes=[r_wscr], dma_sem=ssem, dma_wait_prior=False)
    for _s in ("sC0", "sC1", "sC2", "sC3"):
        sched._wait("gpsimd", _s, sched.cnt[_s])
    EM("gpsimd", lambda e: e.memset(junkv[:, 0:4], 0.0), writes=[r_wscr], bump=True)
    for i in range(2):
        EM("vector", lambda e, i=i: e.memset(bdm[i][:], -1000.0 / SCAP), writes=[r_bdm[i]], bump=True)
    EM("vector", lambda e: e.memset(rt_pad[:, 4:8], -1e30), writes=[r_rt], bump=True)
    EM("vector", lambda e: e.memset(rt_pad[:, 12:16], -1e30), writes=[r_rt], bump=True)
    EM("vector", lambda e: e.memset(k_sb[:].rearrange("p (h t) -> p h t", h=H)[:, :, 0:12], 0.0),
       writes=[r_k], bump=True)
    EM("vector", lambda e: e.memset(xn_sb[0][:].rearrange("p (c t) -> p c t", c=8)[:, :, 0:12], 0.0),
       writes=[r_xn[0]], bump=True)


    def emit_P012(mi, groups=None, phase=None):
        import os
        p012_lvl = int(os.environ.get("P012_LVL", "9"))
        mm = MACROS[mi]
        if groups is None:
            groups = list(range(mm["ngrp"]))
        xn = xn_sb[mi % 2]; rxn = r_xn[mi % 2]
        xnv = xn[:].rearrange("p (c t) -> p c t", c=8)
        if mi > 0 and 0 in groups:
            prev = xn_sb[(mi - 1) % 2]; rprev = r_xn[(mi - 1) % 2]
            po = MACROS[mi - 1]["ntok"]
            EM("vector", lambda e, prev=prev, xnv=xnv, po=po: e.tensor_copy(
                out=xnv[:, :, 0:12],
                in_=prev[:].rearrange("p (c t) -> p c t", c=8)[:, :, po:po + 12]),
               reads=[rprev], writes=[rxn], bump=True)
        wf = wfm[mi % 2]; rwf = r_wfm[mi % 2]
        for g in groups:
            if g >= mm["ngrp"]:
                continue
            t0 = mm["t0"] + g * 128
            if phase != "rt":
              EM("scalar", lambda e, t0=t0: e.dma_start(out=x_ld[:], in_=x_in[t0:t0 + 128, :]),
               reads=[r_xld], writes=[r_xld, r_postm], dma_sem="sX")
              EM("vector", lambda e, g=g: e.bn_stats(out=lnst[:, 12 * g:12 * g + 6], in_=x_ld[:, 0:512]),
                 reads=[r_xld], writes=[r_ln], small=True)
              EM("vector", lambda e, g=g: e.bn_stats(out=lnst[:, 12 * g + 6:12 * g + 12],
                                                     in_=x_ld[:, 512:1024]),
                 reads=[r_xld], writes=[r_ln], small=True)
              EM("vector", lambda e, g=g: e.bn_aggr(out=lnmv[:, 2 * g:2 * g + 2],
                                                    in_=lnst[:, 12 * g:12 * g + 12]),
                 reads=[r_ln], writes=[r_ln], small=True)
              EM("vector", lambda e, g=g: e.tensor_scalar_add(out=lnv[:, g:g + 1],
                                                              in0=lnmv[:, 2 * g + 1:2 * g + 2], scalar1=1e-5),
                 reads=[r_ln], writes=[r_ln], small=True)
              EM("vector", lambda e, g=g: e.reciprocal(out=lny[:, 2 * g:2 * g + 1], in_=lnv[:, g:g + 1]),
                 reads=[r_ln], writes=[r_ln], small=True)
              for it in range(4):
                  a, b = 2 * g + (it % 2), 2 * g + ((it + 1) % 2)
                  EM("vector", lambda e, a=a: e.tensor_mul(out=lnt[:, 0:1], in0=lny[:, a:a + 1],
                                                           in1=lny[:, a:a + 1]),
                     reads=[r_ln], writes=[r_ln], small=True)
                  EM("vector", lambda e, g=g: e.tensor_mul(out=lnt[:, 1:2], in0=lnt[:, 0:1],
                                                           in1=lnv[:, g:g + 1]),
                     reads=[r_ln], writes=[r_ln], small=True)
                  EM("vector", lambda e: e.tensor_scalar(out=lnt[:, 2:3], in0=lnt[:, 1:2], scalar1=-0.5,
                                                         scalar2=1.5, op0=ALU.mult, op1=ALU.add),
                     reads=[r_ln], writes=[r_ln], small=True)
                  EM("vector", lambda e, a=a, b=b: e.tensor_mul(out=lny[:, b:b + 1],
                                                                in0=lny[:, a:a + 1], in1=lnt[:, 2:3]),
                     reads=[r_ln], writes=[r_ln], small=True)
              if p012_lvl < 1:
                  continue
              EM("vector", lambda e, g=g: e.tensor_scalar(out=xn32[:], in0=x_ld[:],
                                                          scalar1=lnmv[:, 2 * g:2 * g + 1],
                                                          scalar2=lny[:, 2 * g:2 * g + 1],
                                                          op0=ALU.subtract, op1=ALU.mult),
                 reads=[r_xld, r_ln], writes=[r_xn32], bump=True)
            if p012_lvl < 2:
                continue
            if phase == "ln":
                continue
            for half in range(2):
                for j in range(4):
                    dc = half * 4 + j
                    EM("tensor", lambda e, dc=dc, j=j: e.transpose(
                        out=BANK(7)[0:128, j * 128:(j + 1) * 128],
                        in_=xn32[:, dc * 128:(dc + 1) * 128], identity=idf[:]),
                       reads=[r_xn32, r_wres],
                       acq=[r_bank[7]] if j == 0 else [],
                       writes=[r_bank[7]] if j == 3 else [], bump=(j == 3))
                EM("vector", lambda e, half=half: e.tensor_copy(
                    out=xnfm32[:, half * 512:(half + 1) * 512], in_=BANK(7)[:]),
                   reads=[r_bank[7]], writes=[r_xnfm32])
                EM("vector", lambda e, half=half, g=g, xnv=xnv: e.tensor_copy(
                    out=xnv[:, half * 4:half * 4 + 4, 12 + g * 128:12 + g * 128 + 128],
                    in_=BANK(7)[:].rearrange("p (c t) -> p c t", c=4)),
                   reads=[r_bank[7]], writes=[rxn, r_bank[7]], bump=True)
            if p012_lvl < 3:
                continue
            for dc in range(8):
                EM("tensor", lambda e, dc=dc: e.matmul(
                    out=BANK(7)[0:128, 0:E],
                    lhsT=xnfm32[:, dc * 128:(dc + 1) * 128],
                    rhs=gate_sb[:, dc * E:(dc + 1) * E],
                    start=(dc == 0), stop=(dc == 7)),
                   reads=[r_xnfm32, r_wres],
                   acq=[r_bank[7]] if dc == 0 else [],
                   writes=[r_bank[7]] if dc == 7 else [], bump=(dc == 7))
            if p012_lvl < 4:
                continue
            EM("vector", lambda e: e.tensor_copy(out=rt_m[:], in_=BANK(7)[0:128, 0:E]),
               reads=[r_bank[7]], writes=[r_rt, r_bank[7]], small=True, bump=True)
            EM("scalar", lambda e: e.activation(out=rt_tt[:], in_=rt_m[:],
                                                func=AF.Tanh, scale=0.5),
               reads=[r_rt], writes=[r_rt], small=True, bump=True)
            EM("vector", lambda e: e.tensor_scalar(out=rt_sig[:], in0=rt_tt[:], scalar1=0.5,
                                                   scalar2=0.5, op0=ALU.mult, op1=ALU.add),
               reads=[r_rt], writes=[r_rt], small=True)
            EM("vector", lambda e: e.tensor_add(out=rt_sfc[:], in0=rt_sig[:], in1=biasb[:]),
               reads=[r_rt, r_wres], writes=[r_rt], small=True)
            EM("vector", lambda e: e.tensor_copy(out=rt_pad[:, 0:4], in_=rt_sfc[:, 0:4]),
               reads=[r_rt], writes=[r_rt], small=True)
            EM("vector", lambda e: e.tensor_copy(out=rt_pad[:, 8:12], in_=rt_sfc[:, 4:8]),
               reads=[r_rt], writes=[r_rt], small=True)
            EM("vector", lambda e: e.max(out=rt_srt[:, 0:8], in_=rt_pad[:, 0:8]),
               reads=[r_rt], writes=[r_rt], small=True)
            EM("vector", lambda e: e.max(out=rt_srt[:, 8:16], in_=rt_pad[:, 8:16]),
               reads=[r_rt], writes=[r_rt], small=True)
            EM("vector", lambda e: e.tensor_add(out=rt_gs[:, 0:2], in0=rt_srt[:, 0:16:8],
                                                in1=rt_srt[:, 1:16:8]),
               reads=[r_rt], writes=[r_rt], small=True)
            EM("vector", lambda e: e.tensor_tensor(out=rt_gs[:, 2:3], in0=rt_gs[:, 0:1],
                                                   in1=rt_gs[:, 1:2], op=ALU.is_ge),
               reads=[r_rt], writes=[r_rt], small=True)
            EM("vector", lambda e: e.tensor_scalar(out=rt_gs[:, 3:4], in0=rt_gs[:, 2:3], scalar1=-1.0,
                                                   scalar2=1.0, op0=ALU.mult, op1=ALU.add),
               reads=[r_rt], writes=[r_rt], small=True)
            EM("vector", lambda e: e.tensor_scalar_mul(out=rt_m[:, 0:4], in0=rt_sfc[:, 0:4],
                                                       scalar1=rt_gs[:, 2:3]),
               reads=[r_rt], writes=[r_rt], small=True)
            EM("vector", lambda e: e.tensor_scalar_mul(out=rt_m[:, 4:8], in0=rt_sfc[:, 4:8],
                                                       scalar1=rt_gs[:, 3:4]),
               reads=[r_rt], writes=[r_rt], small=True)
            EM("vector", lambda e: e.max(out=rt_sm[:], in_=rt_m[:]),
               reads=[r_rt], writes=[r_rt], small=True)
            EM("vector", lambda e: e.tensor_copy(out=junkv[:, 0:8], in_=rt_sm[:]),
               reads=[r_rt], writes=[], small=True)
            if p012_lvl < 5:
                continue
            EM("vector", lambda e: e.max_index(out=rt_idx[:], in_max=rt_sm[:], in_values=rt_m[:]),
               reads=[r_rt], writes=[r_idx], small=True)
            EM("vector", lambda e: e.tensor_copy(out=rt_idxf[:], in_=rt_idx[:, 0:2]),
               reads=[r_idx], writes=[r_idx], small=True)
            EM("vector", lambda e: e.scalar_tensor_tensor(out=rt_eq[:, 0:8], in0=iota8[:],
                                                          scalar=rt_idxf[:, 0:1], in1=rt_sig[:],
                                                          op0=ALU.is_equal, op1=ALU.mult,
                                                          accum_out=rt_sel[:, 0:1]),
               reads=[r_idx, r_rt, r_wres], writes=[r_rt], small=True)
            EM("vector", lambda e: e.scalar_tensor_tensor(out=rt_eq[:, 8:16], in0=iota8[:],
                                                          scalar=rt_idxf[:, 1:2], in1=rt_sig[:],
                                                          op0=ALU.is_equal, op1=ALU.mult,
                                                          accum_out=rt_sel[:, 1:2]),
               reads=[r_idx, r_rt, r_wres], writes=[r_rt], small=True)
            EM("vector", lambda e: e.tensor_add(out=rt_sel[:, 2:3], in0=rt_sel[:, 0:1],
                                                in1=rt_sel[:, 1:2]),
               reads=[r_rt], writes=[r_rt], small=True)
            EM("vector", lambda e: e.tensor_scalar_add(out=rt_sel[:, 3:4], in0=rt_sel[:, 2:3],
                                                       scalar1=1e-20),
               reads=[r_rt], writes=[r_rt], small=True)
            EM("vector", lambda e: e.reciprocal(out=rt_sel[:, 4:5], in_=rt_sel[:, 3:4]),
               reads=[r_rt], writes=[r_rt], small=True)
            EM("vector", lambda e: e.tensor_mul(out=rt_sel[:, 5:6], in0=rt_sel[:, 0:1],
                                                in1=rt_sel[:, 4:5]),
               reads=[r_rt], writes=[r_rt], small=True)
            EM("vector", lambda e: e.tensor_mul(out=rt_sel[:, 6:7], in0=rt_sel[:, 1:2],
                                                in1=rt_sel[:, 4:5]),
               reads=[r_rt], writes=[r_rt], small=True)
            EM("vector", lambda e: e.tensor_single_scalar(out=rt_eq[:, 0:8], in_=iota8[:],
                                                          scalar=rt_idxf[:, 0:1], op=ALU.is_equal),
               reads=[r_idx, r_wres], writes=[r_rt], small=True)
            EM("vector", lambda e: e.tensor_single_scalar(out=rt_eq[:, 8:16], in_=iota8[:],
                                                          scalar=rt_idxf[:, 1:2], op=ALU.is_equal),
               reads=[r_idx, r_wres], writes=[r_rt], small=True)
            EM("vector", lambda e: e.tensor_scalar_mul(out=rt_eq[:, 0:8], in0=rt_eq[:, 0:8],
                                                       scalar1=rt_sel[:, 5:6]),
               reads=[r_rt], writes=[r_rt], small=True)
            EM("vector", lambda e: e.tensor_scalar_mul(out=rt_eq[:, 8:16], in0=rt_eq[:, 8:16],
                                                       scalar1=rt_sel[:, 6:7]),
               reads=[r_rt], writes=[r_rt], small=True)
            EM("vector", lambda e: e.tensor_add(out=rt_W[:], in0=rt_eq[:, 0:8], in1=rt_eq[:, 8:16]),
               reads=[r_rt], writes=[r_W], small=True, bump=True)
            if p012_lvl < 6:
                continue
            EM("vector", lambda e, g=g: e.tensor_copy(out=rt_Wg[:, g * E:(g + 1) * E], in_=rt_W[:]),
               reads=[r_W], writes=[r_Wg], small=True, bump=True)

    def emit_wtrans(mi, g):
        mm = MACROS[mi]
        if g >= mm["ngrp"]:
            return
        wf = wfm[mi % 2]; rwf = r_wfm[mi % 2]
        EM("tensor", lambda e, g=g: e.transpose(out=BANK(7)[0:8, 0:128],
                                                in_=rt_Wg[:, g * E:(g + 1) * E], identity=idf[:]),
           reads=[r_Wg, r_wres], acq=[r_bank[7]], writes=[r_bank[7]], bump=True)
        EM("vector", lambda e, wf=wf, g=g: e.tensor_copy(out=wf[0:8, g * 128:(g + 1) * 128],
                                                         in_=BANK(7)[0:8, 0:128]),
           reads=[r_bank[7]], writes=[rwf], bump=True)

    def bd_emit(mi, si):
        mm = MACROS[mi]
        if si >= len(mm["subs"]):
            return
        q0, nb = mm["subs"][si]
        qcnt = 12 * nb
        slot = si % 2
        rr = r_raw1[0]; rsc = r_bdscr[slot]; rb = r_bdm[slot]
        for h in range(H):
            EM("tensor", lambda e, h=h, q0=q0, qcnt=qcnt: e.matmul(
                out=BANK(6)[0:96, h * NPOS:(h + 1) * NPOS][0:qcnt, :],
                lhsT=q_sb[:, h * 384 + q0:h * 384 + q0 + qcnt],
                rhs=relk_sb[:, h * NPOS:(h + 1) * NPOS],
                start=True, stop=True),
               reads=[r_q, r_wres],
               acq=[r_bank[6]] if h == 0 else [],
               writes=[r_bank[6]] if h == H - 1 else [], bump=(h == H - 1))
        EM("vector", lambda e, qcnt=qcnt, slot=slot: e.tensor_copy(
            out=raw1_sb[0][0:qcnt, :], in_=BANK(6)[0:96, 0:H * NPOS][0:qcnt, :]),
           reads=[r_bank[6]], writes=[rr], bump=True)
        EM("sync", lambda e, qcnt=qcnt, slot=slot: e.dma_start(
            out=bdscr[slot].rearrange("h nb (c p) -> (nb c) h p", c=CH, p=NPOS)[0:qcnt, :, :],
            in_=raw1_sb[0][0:qcnt, :].rearrange("j (h p) -> j h p", h=H)),
           reads=[rr], writes=[rsc], dma_sem=("sBW0", "sBW1")[slot])
        for b in range(nb):
            EM("sync", lambda e, b=b, slot=slot: e.dma_start(
                out=bdm[slot][b * CH:(b + 1) * CH, :].rearrange("c (h q) -> c h q", h=H)[:, :, 12 * b:12 * b + CTX],
                in_=bdscr[slot][:, b, 0:288].rearrange("h (c k) -> c h k", c=CH)),
               reads=[rsc], writes=[rb],
               dma_sem=("sBM0", "sBM1", "sBM2", "sBM3", "sBM4", "sBM5", "sBM6", "sBM7")[slot * 4 + b % 4])

    def emit_P3(mi, part=None):
        mm = MACROS[mi]
        xn = xn_sb[mi % 2]; rxn = r_xn[mi % 2]
        ntok = mm["ntok"]; npad = 12 * mm["nblk"] - ntok
        xnv = xn[:].rearrange("p (c t) -> p c t", c=8)
        if part in (None, 0):
          if npad:
            EM("vector", lambda e, xnv=xnv, ntok=ntok, npad=npad: e.memset(
                xnv[:, :, 12 + ntok:12 + ntok + npad], 0.0), writes=[rxn], bump=True)
          for h in range(H):
            bq = r_bank[6 + h % 2]
            for dc in range(8):
                EM("tensor", lambda e, h=h, dc=dc, xnv=xnv, ntok=ntok: e.matmul(
                    out=BANK(6 + h % 2)[0:128, 0:ntok],
                    lhsT=wq_sb[:, dc * D + h * 128: dc * D + (h + 1) * 128],
                    rhs=xnv[:, dc, 12:12 + ntok],
                    start=(dc == 0), stop=(dc == 7)),
                   reads=[rxn, r_wres], acq=[bq] if dc == 0 else [],
                   writes=[bq] if dc == 7 else [], bump=(dc == 7))
            EM("vector", lambda e, h=h, ntok=ntok: e.tensor_scalar_mul(
                out=q_sb[:, h * 384:h * 384 + ntok], in0=BANK(6 + h % 2)[0:128, 0:ntok],
                scalar1=qscale[:, 0:1]),
               reads=[bq, r_wres], writes=[r_q], bump=True)
        if part in (None, 0):
            bd_emit(mi, 0)
        if part in (None, 1):
          for h in range(H):
            bk = r_bank[(6, 7, 0, 1)[h % 4]]
            for dc in range(8):
                EM("tensor", lambda e, h=h, dc=dc, xnv=xnv, ntok=ntok: e.matmul(
                    out=BANK((6, 7, 0, 1)[h % 4])[0:128, 0:ntok],
                    lhsT=wk_sb[:, dc * D + h * 128: dc * D + (h + 1) * 128],
                    rhs=xnv[:, dc, 12:12 + ntok],
                    start=(dc == 0), stop=(dc == 7)),
                   reads=[rxn, r_wres], acq=[bk] if dc == 0 else [],
                   writes=[bk] if dc == 7 else [], bump=(dc == 7))
            EM("vector", lambda e, h=h, ntok=ntok: e.tensor_scalar_mul(
                out=k_sb[:, h * 396 + 12:h * 396 + 12 + ntok], in0=BANK((6, 7, 0, 1)[h % 4])[0:128, 0:ntok],
                scalar1=K_SCALE),
               reads=[bk], writes=[r_k], bump=True)
        if part in (None, 1):
            bd_emit(mi, 1)
        if part not in (None, 2):
            return
        if npad:
            EM("vector", lambda e, ntok=ntok, npad=npad: e.memset(
                k_sb[:].rearrange("p (h t) -> p h t", h=H)[:, :, 12 + ntok:12 + ntok + npad], 0.0),
               writes=[r_k], bump=True)
        for si, (q0, nb) in enumerate(mm["subs"]):
            kw = 12 * nb + 12
            b0, b1 = 6, 7
            for dc in range(8):
                for half in range(2):
                    b = (b0, b1)[half]
                    EM("tensor", lambda e, dc=dc, half=half, q0=q0, kw=kw, b=b, xnv=xnv: e.matmul(
                        out=BANK(b)[0:kw, 0:512],
                        lhsT=xnv[:, dc, q0:q0 + kw],
                        rhs=wv_sb[:, dc * D + half * 512: dc * D + half * 512 + 512],
                        start=(dc == 0), stop=(dc == 7)),
                       reads=[rxn, r_wres],
                       acq=[r_bank[b]] if dc == 0 else [],
                       writes=[r_bank[b]] if dc == 7 else [],
                       bump=(dc == 7 and half == 1))
            EM("vector", lambda e, si=si, kw=kw, b0=b0: e.tensor_copy(
                out=v_sb[si][0:kw, 0:512], in_=BANK(b0)[0:kw, 0:512]),
               reads=[r_bank[b0]], writes=[r_v[si]])
            EM("vector", lambda e, si=si, kw=kw, b1=b1: e.tensor_copy(
                out=v_sb[si][0:kw, 512:1024], in_=BANK(b1)[0:kw, 0:512]),
               reads=[r_bank[b1]], writes=[r_v[si]], bump=True)

    def emit_P4(mi):
        mm = MACROS[mi]
        tbufs = [tbuf, scr34]

        def stage1(si):
            q0, nb = mm["subs"][si]
            qcnt = 12 * nb
            kw = qcnt + 12
            slot = si % 2
            pb = (0, 2)[si % 2]
            tb = tbufs[si % 2]
            for h in range(H):
                bi = pb + h // 4
                EM("tensor", lambda e, h=h, bi=bi, q0=q0, qcnt=qcnt, kw=kw: e.matmul(
                    out=BANK(bi)[0:qcnt, (h % 4) * 108:(h % 4) * 108 + kw],
                    lhsT=q_sb[:, h * 384 + q0:h * 384 + q0 + qcnt],
                    rhs=k_sb[:, h * 396 + q0:h * 396 + q0 + kw],
                    start=True, stop=True),
                   reads=[r_q, r_k],
                   acq=[r_bank[bi]] if h % 4 == 0 else [],
                   writes=[r_bank[bi]] if h % 4 == 3 else [], bump=(h % 4 == 3))
            for j in range(2):
                bi = pb + j
                EM("vector", lambda e, j=j, bi=bi, qcnt=qcnt, kw=kw, slot=slot: e.scalar_tensor_tensor(
                    out=ubuf[0:qcnt, :].rearrange("p (h q) -> p h q", h=H)[:, j * 4:j * 4 + 4, 0:kw],
                    in0=BANK(bi)[0:qcnt, 0:432].rearrange("p (h q) -> p h q", h=4)[:, :, 0:kw],
                    scalar=1.0 / SCAP,
                    in1=bdm[slot][0:qcnt, :].rearrange("p (h q) -> p h q", h=H)[:, j * 4:j * 4 + 4, 0:kw],
                    op0=ALU.mult, op1=ALU.add),
                   reads=[r_bank[bi], r_bdm[slot]], writes=[r_ubufh[j]], bump=True)
            bd_emit(mi, si + 2)
            for j in range(2):
                hs = slice(j * 432, j * 432 + 432)
                EM("scalar", lambda e, qcnt=qcnt, hs=hs, tb=tb: e.activation(
                    out=tb[0:qcnt, hs] if not hasattr(tb, 'tensor') else tb[0:qcnt, hs],
                    in_=ubuf[0:qcnt, hs], func=AF.Tanh),
                   reads=[r_ubufh[j]], writes=[r_tbh[si % 2][j]])
                EM("scalar", lambda e, qcnt=qcnt, hs=hs, tb=tb: e.activation(
                    out=ubuf[0:qcnt, hs], in_=tb[0:qcnt, hs], func=AF.Exp, scale=SCAP),
                   reads=[r_tbh[si % 2][j]], writes=[r_ubufh[j]], bump=True)
                EM("vector", lambda e, qcnt=qcnt, kw=kw, j=j: e.reduce_sum(
                    out=sums[0:qcnt, j * 4:j * 4 + 4],
                    in_=ubuf[0:qcnt, j * 432:j * 432 + 432].rearrange("p (h q) -> p h q", h=4)[:, :, 0:kw],
                    axis=AX.X),
                   reads=[r_ubufh[j]], writes=[r_sums], small=True)
                EM("vector", lambda e, qcnt=qcnt, j=j: e.reciprocal(
                    out=inv[0:qcnt, j * 4:j * 4 + 4], in_=sums[0:qcnt, j * 4:j * 4 + 4]),
                   reads=[r_sums], writes=[r_inv], small=True, bump=True)
                EM("vector", lambda e, qcnt=qcnt, kw=kw, j=j, tb=tb: e.tensor_mul(
                    out=tb[0:qcnt, j * 432:j * 432 + 432].rearrange("p (h q) -> p h q", h=4)[:, :, 0:kw],
                    in0=ubuf[0:qcnt, j * 432:j * 432 + 432].rearrange("p (h q) -> p h q", h=4)[:, :, 0:kw],
                    in1=inv[0:qcnt, j * 4:j * 4 + 4].rearrange("p (h o) -> p h o", o=1).broadcast_to([qcnt, 4, kw])),
                   reads=[r_ubufh[j], r_inv], writes=[r_tbh[si % 2][j]], bump=True)

        def stage2(si):
            q0, nb = mm["subs"][si]
            qcnt = 12 * nb
            kw = qcnt + 12
            pb = (0, 2)[si % 2]
            tb = tbufs[si % 2]
            for h in range(H):
                bi = pb + h // 4
                EM("tensor", lambda e, h=h, bi=bi, qcnt=qcnt, kw=kw, tb=tb: e.transpose(
                    out=BANK(bi)[0:kw, (h % 4) * 96:(h % 4) * 96 + qcnt],
                    in_=tb[0:qcnt, h * 108:h * 108 + kw], identity=idf[0:qcnt, 0:qcnt]),
                   reads=[r_tbh[si % 2][h // 4], r_wres],
                   acq=[r_bank[bi]] if h % 4 == 0 else [],
                   writes=[r_bank[bi]] if h % 4 == 3 else [], bump=(h % 4 == 3))
            for j in range(2):
                bi = pb + j
                EM("vector", lambda e, bi=bi, j=j, kw=kw: e.tensor_copy(
                    out=ptbf[0:kw, j * 384:j * 384 + 384], in_=BANK(bi)[0:kw, 0:384]),
                   reads=[r_bank[bi]], writes=[r_ptbf], bump=True)
            for h in range(H):
                bi = 4 + h // 4
                EM("tensor", lambda e, h=h, bi=bi, si=si, qcnt=qcnt, kw=kw: e.matmul(
                    out=BANK(bi)[0:128, (h % 4) * 96:(h % 4) * 96 + qcnt],
                    lhsT=v_sb[si][0:kw, h * 128:(h + 1) * 128],
                    rhs=ptbf[0:kw, h * 96:h * 96 + qcnt],
                    start=True, stop=True),
                   reads=[r_v[si], r_ptbf],
                   acq=[r_bank[bi]] if h % 4 == 0 else [],
                   writes=[r_bank[bi]] if h % 4 == 3 else [], bump=(h % 4 == 3))
            for bi in (4, 5):
                h0 = (bi - 4) * 4
                EM("vector", lambda e, bi=bi, h0=h0, q0=q0, qcnt=qcnt: e.tensor_copy(
                    out=attn_sb[:].rearrange("p (h t) -> p h t", h=H)[:, h0:h0 + 4, q0:q0 + qcnt],
                    in_=BANK(bi)[0:128, 0:384].rearrange("p (h t) -> p h t", h=4)[:, :, 0:qcnt]),
                   reads=[r_bank[bi]], writes=[r_attn], bump=True)

        for si in range(len(mm["subs"])):
            stage1(si)
            if si > 0:
                stage2(si - 1)
        stage2(len(mm["subs"]) - 1)

    def emit_P5_group(mi, g):
        mm = MACROS[mi]
        if g >= mm["ngrp"]:
            return
        for half in range(2):
            b = 2 * g + half
            for dc in range(8):
                EM("tensor", lambda e, g=g, dc=dc, half=half, b=b: e.matmul(
                    out=BANK(b)[0:128, 0:512],
                    lhsT=attn_sb[:, dc * 384 + g * 128: dc * 384 + g * 128 + 128],
                    rhs=wp_sb[:, dc * D + half * 512: dc * D + half * 512 + 512],
                    start=(dc == 0), stop=(dc == 7)),
                   reads=[r_attn, r_wres], acq=[r_bank[b]] if dc == 0 else [],
                   writes=[r_bank[b]] if dc == 7 else [], bump=(dc == 7))
        for half in range(2):
            b = 2 * g + half
            EM("vector", lambda e, g=g, half=half, b=b: e.tensor_copy(
                out=s_sb[:, g * D + half * 512:g * D + half * 512 + 512],
                in_=BANK(b)[0:128, 0:512]),
               reads=[r_bank[b]], writes=[r_s], bump=True)

    def emit_P5(mi):
        mm = MACROS[mi]
        for g in range(mm["ngrp"]):
            emit_P5_group(mi, g)
        if mi + 1 < n_macro:
            po = mm["ntok"]
            EM("vector", lambda e, po=po: e.tensor_copy(
                out=k_sb[:].rearrange("p (h t) -> p h t", h=H)[:, :, 0:12],
                in_=k_sb[:].rearrange("p (h t) -> p h t", h=H)[:, :, po:po + 12]),
               reads=[r_k], writes=[r_k], bump=True)

    def emit_P6(mi):
        mm = MACROS[mi]
        ntok = mm["ntok"]
        xn = xn_sb[mi % 2]; rxn = r_xn[mi % 2]
        xnv = xn[:].rearrange("p (c t) -> p c t", c=8)
        for lc in range(4):
            b = (2, 3, 4, 5)[lc]
            for dc in range(8):
                EM("tensor", lambda e, lc=lc, dc=dc, b=b, xnv=xnv, ntok=ntok: e.matmul(
                    out=BANK(b)[0:128, 0:ntok],
                    lhsT=fc1_sb[:, dc * LAT + lc * 128: dc * LAT + (lc + 1) * 128],
                    rhs=xnv[:, dc, 12:12 + ntok],
                    start=(dc == 0), stop=(dc == 7)),
                   reads=[rxn, r_wres], acq=[r_bank[b]] if dc == 0 else [],
                   writes=[r_bank[b]] if dc == 7 else [], bump=(dc == 7))
            EM("vector", lambda e, lc=lc, b=b, ntok=ntok: e.tensor_copy(
                out=lat_sb[:, lc * 384:lc * 384 + ntok], in_=BANK(b)[0:128, 0:ntok]),
               reads=[r_bank[b]], writes=[r_lat], bump=True)

    def emit_stream_w12(ei):
        sl = ei % 2
        EM("sync", lambda e, ei=ei, sl=sl: e.dma_start(
            out=w1b[sl][:].rearrange("p (c i) -> p c i", c=4),
            in_=w1s[ei].rearrange("(c p) i -> p c i", p=128)),
           reads=[r_wscr], writes=[r_w1b[sl]], dma_sem=("sW10", "sW11")[sl])
        EM("sync", lambda e, ei=ei, sl=sl: e.dma_start(
            out=w2b[sl][:].rearrange("p (c i) -> p c i", c=4),
            in_=w2s[ei].rearrange("(c p) i -> p c i", p=128)),
           reads=[r_wscr], writes=[r_w2b[sl]], dma_sem=("sW20", "sW21")[sl])

    def emit_stream_up(oc):
        sl = oc % 2
        EM("scalar", lambda e, oc=oc, sl=sl: e.dma_start(
            out=upb[sl][:].rearrange("p (c o) -> p c o", c=8),
            in_=ups.rearrange("(c p) o -> p c o", p=128)[:, :, oc * 128:(oc + 1) * 128]),
           reads=[r_wscr], writes=[r_upb[sl]], dma_sem=("sU0", "sU1")[sl])

    def emit_stream_dn(oc):
        sl = oc % 2
        EM("scalar", lambda e, oc=oc, sl=sl: e.dma_start(
            out=dnb[sl][:], in_=dns.rearrange("(g p) o -> p g o", p=128)[:, oc, :]),
           reads=[r_wscr], writes=[r_dnb[sl]], dma_sem=("sD0", "sD1")[sl])

    def emit_stream_updn(oc):
        emit_stream_up(oc)
        emit_stream_dn(oc)

    def emit_P7_expert(mi, ei):
        mm = MACROS[mi]
        ntok = mm["ntok"]
        sl = ei % 2
        wf = wfm[mi % 2]; rwf = r_wfm[mi % 2]
        EM("gpsimd", lambda e, ei=ei, wf=wf, ntok=ntok: e.dma_start(
            out=ubuf[0:1, 0:ntok], in_=wf[ei:ei + 1, 0:ntok]),
           reads=[rwf], writes=[r_ubuf], dma_sem="sRW")
        EM("tensor", lambda e, ntok=ntok: e.matmul(
            out=BANK(6)[0:128, 0:ntok], lhsT=ones1[:], rhs=ubuf[0:1, 0:ntok],
            start=True, stop=True),
           reads=[r_ubuf, r_wres], acq=[r_bank[6]], writes=[r_bank[6]], bump=True)
        for ic in range(4):
            b = 4 + ic % 2
            for lc in range(4):
                EM("tensor", lambda e, ic=ic, lc=lc, b=b, sl=sl, ntok=ntok: e.matmul(
                    out=BANK(b)[0:128, 0:ntok],
                    lhsT=w1b[sl][:, lc * 512 + ic * 128: lc * 512 + (ic + 1) * 128],
                    rhs=lat_sb[:, lc * 384:lc * 384 + ntok],
                    start=(lc == 0), stop=(lc == 3)),
                   reads=[r_w1b[sl], r_lat], acq=[r_bank[b]] if lc == 0 else [],
                   writes=[r_bank[b]] if lc == 3 else [], bump=(lc == 3))
            tr = trelu[0]; rtr = r_trelu[0]
            EM("vector", lambda e, b=b, tr=tr, ntok=ntok: e.tensor_scalar_max(
                out=tr[:, 0:ntok], in0=BANK(b)[0:128, 0:ntok], scalar1=0.0),
               reads=[r_bank[b]], writes=[rtr])
            EM("vector", lambda e, tr=tr, ntok=ntok: e.tensor_mul(
                out=ttw[:, 0:ntok], in0=tr[:, 0:ntok], in1=BANK(6)[0:128, 0:ntok]),
               reads=[rtr, r_bank[6]], writes=[r_ttw])
            EM("vector", lambda e, ic=ic, tr=tr, ntok=ntok: e.tensor_mul(
                out=interbf[:, ic * 384:ic * 384 + ntok], in0=tr[:, 0:ntok], in1=ttw[:, 0:ntok]),
               reads=[rtr, r_ttw], writes=[r_interbf], bump=True)
        for lc in range(4):
            for ic in range(4):
                first = (ei == 0 and ic == 0)
                last = (ei == E - 1 and ic == 3)
                EM("tensor", lambda e, lc=lc, ic=ic, sl=sl, first=first, last=last, ntok=ntok: e.matmul(
                    out=BANK(lc)[0:128, 0:ntok],
                    lhsT=w2b[sl][:, ic * 512 + lc * 128: ic * 512 + (lc + 1) * 128],
                    rhs=interbf[:, ic * 384:ic * 384 + ntok],
                    start=first, stop=last),
                   reads=[r_w2b[sl], r_interbf],
                   acq=[r_bank[lc]] if first else [],
                   writes=[r_bank[lc]] if last else [], bump=last)

    def emit_P8(mi):
        mm = MACROS[mi]
        ntok = mm["ntok"]
        for lc in range(4):
            EM("vector", lambda e, lc=lc, ntok=ntok: e.tensor_copy(
                out=combbf[:, lc * 384:lc * 384 + ntok], in_=BANK(lc)[0:128, 0:ntok]),
               reads=[r_bank[lc]], writes=[r_combbf], bump=True)
        for g in range(mm["ngrp"]):
            for half in range(2):
                b = 2 * g + half
                for lc in range(4):
                    EM("tensor", lambda e, g=g, half=half, lc=lc, b=b: e.matmul(
                        out=BANK(b)[0:128, 0:512],
                        lhsT=combbf[:, lc * 384 + g * 128: lc * 384 + g * 128 + 128],
                        rhs=fc2_sb[:, lc * D + half * 512: lc * D + half * 512 + 512],
                        start=(lc == 0), stop=False),
                       reads=[r_combbf, r_wres],
                       acq=[r_bank[b]] if lc == 0 else [], writes=[], bump=False)

    def emit_P9(mi):
        mm = MACROS[mi]
        ntok = mm["ntok"]
        xn = xn_sb[mi % 2]; rxn = r_xn[mi % 2]
        xnv = xn[:].rearrange("p (c t) -> p c t", c=8)
        def emit_down(oc):
            sl = oc % 2
            for g in range(mm["ngrp"]):
                for half in range(2):
                    fb = 2 * g + half
                    last = (oc == 31)
                    EM("tensor", lambda e, oc=oc, half=half, fb=fb, sl=sl, g=g, last=last: e.matmul(
                        out=BANK(fb)[0:128, 0:512],
                        lhsT=upbf[oc % 2][:, g * 128:g * 128 + 128],
                        rhs=dnb[sl][:, half * 512:half * 512 + 512],
                        start=False, stop=last),
                       reads=[r_upbf[oc % 2], r_dnb[sl]],
                       writes=[r_bank[fb]] if last else [], bump=last)
        for oc in range(32):
            sl = oc % 2
            b = 6 + oc % 2
            for dc in range(8):
                EM("tensor", lambda e, dc=dc, b=b, sl=sl, xnv=xnv, ntok=ntok: e.matmul(
                    out=BANK(b)[0:128, 0:ntok],
                    lhsT=upb[sl][:, dc * 128:(dc + 1) * 128],
                    rhs=xnv[:, dc, 12:12 + ntok],
                    start=(dc == 0), stop=(dc == 7)),
                   reads=[r_upb[sl], rxn], acq=[r_bank[b]] if dc == 0 else [],
                   writes=[r_bank[b]] if dc == 7 else [], bump=(dc == 7))
            tr = trelu[0]; rtr = r_trelu[0]
            EM("vector", lambda e, b=b, tr=tr, ntok=ntok: e.tensor_scalar_max(
                out=tr[:, 0:ntok], in0=BANK(b)[0:128, 0:ntok], scalar1=0.0),
               reads=[r_bank[b]], writes=[rtr])
            EM("scalar", lambda e, oc=oc, tr=tr, ntok=ntok: e.activation(
                out=upbf[oc % 2][:, 0:ntok], in_=tr[:, 0:ntok], func=AF.Square),
               reads=[rtr], writes=[r_upbf[oc % 2]], bump=True)
            sched.bump("vector")
            if oc > 0:
                emit_down(oc - 1)
            if oc + 2 < 32:
                emit_stream_up(oc + 2)
            if 2 <= oc + 1 < 32:
                emit_stream_dn(oc + 1)
        emit_down(31)

    def emit_P10(mi, groups=None):
        mm = MACROS[mi]
        if groups is None:
            groups = list(range(mm["ngrp"]))
        for g in groups:
            t0 = mm["t0"] + g * 128
            xr = x_res[0]; rxr = r_xres[0]
            EM("scalar", lambda e, t0=t0, xr=xr: e.dma_start(out=xr[:], in_=x_in[t0:t0 + 128, :]),
               reads=[rxr], writes=[rxr], dma_sem="sX")
            ob = outb[0]; rob = r_outb[0]
            for half in range(2):
                fb = 2 * g + half
                hs = slice(half * 512, half * 512 + 512)
                EM("vector", lambda e, fb=fb, hs=hs, ob=ob: e.tensor_copy(
                    out=ob[:, hs], in_=BANK(fb)),
                   reads=[r_bank[fb]], writes=[rob, r_bank[fb]], bump=True)
                EM("scalar", lambda e, hs=hs, ob=ob: e.activation(
                    out=ttn[:, hs], in_=ob[:, hs], func=AF.Tanh, scale=0.5),
                   reads=[rob], writes=[r_ttn], bump=True)
                EM("vector", lambda e, hs=hs, ob=ob: e.scalar_tensor_tensor(
                    out=g2b[:, hs], in0=ttn[:, hs], scalar=1.0, in1=ob[:, hs],
                    op0=ALU.add, op1=ALU.mult),
                   reads=[r_ttn, rob], writes=[r_g2], bump=True)
                EM("vector", lambda e, g=g, half=half, hs=hs: e.scalar_tensor_tensor(
                    out=ttn[:, hs], in0=g2b[:, hs], scalar=0.5,
                    in1=s_sb[:, g * D + half * 512:g * D + half * 512 + 512],
                    op0=ALU.mult, op1=ALU.mult),
                   reads=[r_g2, r_s], writes=[r_ttn])
                EM("vector", lambda e, hs=hs, ob=ob, xr=xr: e.tensor_add(
                    out=ob[:, hs], in0=ttn[:, hs], in1=xr[:, hs]),
                   reads=[r_ttn, rxr], writes=[rob], bump=True)
            EM("scalar", lambda e, t0=t0, ob=ob: e.dma_start(out=out_d[t0:t0 + 128, :], in_=ob[:]),
               reads=[rob], writes=[r_out], dma_sem="sO")

    ORDER = ["init", "P012", "P3", "P4", "P5", "P6", "P7", "P8", "P9", "full"]
    lvl = ORDER.index(stop_after)
    if lvl >= 1:
        emit_P012(0)
        for g0 in range(3):
            emit_wtrans(0, g0)
    if lvl >= 2:
        emit_P3(0)
    for mi in range(n_macro):
        if lvl >= 3:
            emit_P4(mi)
        if lvl >= 4:
            emit_P5(mi)
        if lvl >= 5:
            emit_P6(mi)
        if lvl >= 6:
            emit_stream_w12(0)
            emit_stream_w12(1)
            for ei in range(E):
                emit_P7_expert(mi, ei)
                if ei + 2 < E:
                    emit_stream_w12(ei + 2)
                if mi + 1 < n_macro and ei in (1, 3, 5):
                    emit_P012(mi + 1, groups=[(ei - 1) // 2], phase="ln")
                if mi + 1 < n_macro and ei in (2, 4, 6):
                    emit_P012(mi + 1, groups=[ei // 2 - 1], phase="rt")
                if mi + 1 < n_macro and ei in (3, 5, 7):
                    emit_wtrans(mi + 1, ei // 2 - 1)
        elif mi + 1 < n_macro and lvl >= 2:
            emit_P012(mi + 1)
        if lvl >= 7:
            emit_P8(mi)
        if lvl >= 8:
            emit_stream_updn(0)
            emit_stream_updn(1)
            emit_P9(mi)
        if lvl >= 9 and lvl >= 2 and mi + 1 < n_macro:
            emit_P3(mi + 1, part=0)
            emit_P10(mi, groups=[0])
            emit_P3(mi + 1, part=1)
            emit_P10(mi, groups=[1])
            emit_P3(mi + 1, part=2)
            emit_P10(mi, groups=[g for g in range(2, MACROS[mi]["ngrp"])])
        else:
            if lvl >= 9:
                emit_P10(mi)
            if lvl >= 2 and mi + 1 < n_macro:
                emit_P3(mi + 1)

    if debug_dump:
        dbg_specs = [("xn_sb0", xn_sb[0], r_xn[0]), ("q_sb", q_sb, r_q), ("k_sb", k_sb, r_k),
                     ("lat_sb", lat_sb, r_lat), ("s_sb", s_sb, r_s), ("attn_sb", attn_sb, r_attn),
                     ("combbf", combbf, r_combbf), ("v_sb0", v_sb[0], r_v[0]),
                     ("bdm0", bdm[0], r_bdm[0]), ("bdm1", bdm[1], r_bdm[1]),
                     ("ubuf_d", ubuf, r_ubuf), ("wfm0", wfm[0], r_wfm[0])]
        for nm, tt, rr_ in dbg_specs:
            shp = list(tt[:].shape)
            dt_ = tt[:].dtype
            od = nc.dram_tensor("dbg_" + nm, shp, dt_, kind="ExternalOutput").ap()
            EM("sync", lambda e, od=od, tt=tt: e.dma_start(out=od, in_=tt[:]),
               reads=[rr_], writes=[], dma_sem="sO")
    for sname in ("sO", "sW10", "sW11", "sW20", "sW21", "sU0", "sU1", "sD0", "sD1",
                  "sX", "sBW0", "sBW1", "sBM0", "sBM1", "sRW", "sI", "sY0", "sG0", "sG1"):
        if sched.cnt[sname]:
            sched.prog["sync"].append(("wait", sname, sched.cnt[sname]))

    print("prog sizes", {k: len(v) for k, v in sched.prog.items()},
          "sems", {k: v for k, v in sched.cnt.items()}, "spacers", sched.nops)
    with nc.Block() as block:
        sched.replay(block)
    if want_handles:
        h = dict(xn_sb=xn_sb, q_sb=q_sb, k_sb=k_sb, v_sb=v_sb, attn_sb=attn_sb,
                 s_sb=s_sb, lat_sb=lat_sb, combbf=combbf, wfm=wfm, relk_sb=relk_sb,
                 qscale=qscale, bdm=bdm, ubuf=ubuf, tbuf=tbuf, sums=sums, inv=inv,
                 ptbf=ptbf, rt_sig=rt_sig, rt_W=rt_W, rt_idx=rt_idx, biasb=biasb,
                 interbf=interbf, posfm=posfm, rt_sfc=rt_sfc, rt_m=rt_m, rt_sel=rt_sel,
                 pbank=pbank, upbf=upbf, trelu=trelu)
        return nc, h, es
    es.close()
    n_ins = sum(len(v) for v in sched.prog.values())
    return nc


_NC_CACHE = {}

def _get_nc():
    if "nc" not in _NC_CACHE:
        _NC_CACHE["nc"] = build_nc()
    return _NC_CACHE["nc"]


def kernel(L0_attn_hidden_states, L0_attn_position_embeddings, L0_attn_Wq, L0_attn_Wk,
           L0_attn_Wv, L0_attn_Wpost, L0_attn_Wrel, L0_attn_per_dim_scale,
           L0_ffn_gate, L0_ffn_corr_bias, L0_ffn_w1, L0_ffn_w2, L0_ffn_fc1_w,
           L0_ffn_fc2_w, L0_ffn_up_w, L0_ffn_down_w):
    f32 = np.float32
    Ct = lambda a: np.ascontiguousarray(np.asarray(a, dtype=f32))
    x = Ct(L0_attn_hidden_states)
    base = {
        "wqT": Ct(np.asarray(L0_attn_Wq, f32).T),
        "wkT": Ct(np.asarray(L0_attn_Wk, f32).T),
        "wvT": Ct(np.asarray(L0_attn_Wv, f32).T),
        "wpT": Ct(np.asarray(L0_attn_Wpost, f32).T),
        "wrT": Ct(np.asarray(L0_attn_Wrel, f32).T),
        "posemb": Ct(L0_attn_position_embeddings),
        "pds": Ct(L0_attn_per_dim_scale),
        "gateT": Ct(np.asarray(L0_ffn_gate, f32).T),
        "corrb": Ct(L0_ffn_corr_bias),
        "w1T": Ct(np.asarray(L0_ffn_w1, f32).transpose(0, 2, 1)),
        "w2T": Ct(np.asarray(L0_ffn_w2, f32).transpose(0, 2, 1)),
        "fc1T": Ct(np.asarray(L0_ffn_fc1_w, f32).T),
        "fc2T": Ct(np.asarray(L0_ffn_fc2_w, f32).T),
        "upT": Ct(np.asarray(L0_ffn_up_w, f32).T),
        "dnT": Ct(np.asarray(L0_ffn_down_w, f32).T),
        "idbf": np.eye(128, dtype=ml_dtypes.bfloat16),
        "idf": np.eye(128, dtype=np.float32),
        "iota8": np.tile(np.arange(8, dtype=np.float32), (128, 1)),
    }
    in_maps = [dict(base, x=np.ascontiguousarray(x[b])) for b in range(8)]
    nc = _get_nc()
    res = run_bass_kernel_spmd(nc, in_maps, core_ids=list(range(8)))
    return np.stack([np.asarray(r["out"]) for r in res.results], axis=0)


if __name__ == "__main__":
    print("building nc...")
    import time
    t = time.time()
    nc = build_nc()
    print("built in", time.time() - t, "s")


# version 24
# speedup vs baseline: 1.1132x; 1.0097x over previous
import math
import numpy as np
import ml_dtypes
from contextlib import ExitStack

import concourse.bass as bass
import concourse.mybir as mybir
from concourse.bass_utils import run_bass_kernel_spmd

F32 = mybir.dt.float32
BF16 = mybir.dt.bfloat16
U32 = mybir.dt.uint32
AF = mybir.ActivationFunctionType
ALU = mybir.AluOpType
AX = mybir.AxisListType

S = 4096
D = 1024
H = 8
CH = 12
CTX = 24
NPOS = 25
SCAP = 50.0
Q_SCALE = (128 ** -0.5) / math.log(2)
K_SCALE = math.log(1.0 + math.e) / math.log(2)
E = 8
LAT = 512

MACRO = 384
N_MACRO = 11


def _macros():
    ms = []
    for m in range(N_MACRO):
        ntok = MACRO if m < 10 else 256
        nblk = 32 if m < 10 else 22
        subs = []
        b = 0
        while b < nblk:
            nb = min(8, nblk - b)
            subs.append((12 * b, nb))
            b += nb
        ms.append(dict(m=m, t0=m * MACRO, ntok=ntok, ngrp=ntok // 128, nblk=nblk, subs=subs))
    return ms

MACROS = _macros()


class Res:
    __slots__ = ("name", "wr", "rd")
    def __init__(self, name):
        self.name = name
        self.wr = []
        self.rd = []


class Sched:
    ENGINES = ("tensor", "vector", "scalar", "sync", "gpsimd")

    def __init__(self, nc):
        self.nc = nc
        self.prog = {e: [] for e in self.ENGINES}
        self.cnt = {}
        self.sem = {}
        self.last_wait = {e: {} for e in self.ENGINES}
        self.pend_wr = {e: [] for e in self.ENGINES}
        self.pend_rd = {e: [] for e in self.ENGINES}
        self.pw_eng = {}
        self.pr_eng = {}
        self.esem = {"tensor": "cT", "vector": "cV", "scalar": "cS",
                     "sync": "cY", "gpsimd": "cG"}
        self.lastsmall = {e: set() for e in self.ENGINES}
        self.junk = None
        self.nops = 0

    def add_sem(self, name, handle):
        self.sem[name] = handle
        self.cnt[name] = 0

    def _wait(self, eng, sem, val):
        if val <= 0:
            return
        assert val <= self.cnt[sem], \
            f"wait {sem}>={val} but only {self.cnt[sem]} emitted (engine {eng})"
        lw = self.last_wait[eng]
        if lw.get(sem, 0) >= val:
            return
        lw[sem] = val
        self.prog[eng].append(("wait", sem, val))

    def bump(self, eng):
        sname = self.esem[eng]
        prog = self.prog[eng]
        for i in range(len(prog) - 1, -1, -1):
            if prog[i][0] == "op":
                incs = prog[i][2]
                for (s, _v) in incs:
                    if s == sname:
                        return self.cnt[sname]
                self.cnt[sname] += 1
                incs.append((sname, 1))
                c = self.cnt[sname]
                for r in self.pend_wr[eng]:
                    r.wr = [(sname, c)]
                    r.rd = []
                    self.pw_eng.pop(id(r), None)
                for r in self.pend_rd[eng]:
                    r.rd.append((sname, c))
                    s_ = self.pr_eng.get(id(r))
                    if s_:
                        s_.discard(eng)
                self.pend_wr[eng] = []
                self.pend_rd[eng] = []
                return c
        raise RuntimeError(f"bump with no op on {eng}")

    def _resolve_cross(self, eng, reses):
        for r in reses:
            pe = self.pw_eng.get(id(r))
            if pe is not None and pe != eng:
                self.bump(pe)
            for pe2 in list(self.pr_eng.get(id(r), ())):
                if pe2 != eng:
                    self.bump(pe2)

    def emit(self, eng, fn, reads=(), writes=(), acq=(), dma_sem=None, small=False, bump=False,
             dma_wait_prior=True):
        reads = list(reads); writes = list(writes); acq = list(acq)
        self._resolve_cross(eng, reads + writes + acq)
        if eng in ("vector", "scalar"):
            if any(r.name in self.lastsmall[eng] for r in reads):
                j0, j1 = self.junk[eng]
                self.prog[eng].append(("op", j0, []))
                self.prog[eng].append(("op", j1, []))
                self.nops += 2
                self.lastsmall[eng] = set()
        for r in reads:
            for (s, v) in r.wr:
                self._wait(eng, s, v)
        for w in writes + acq:
            for (s, v) in w.rd:
                self._wait(eng, s, v)
            for (s, v) in w.wr:
                if s != self.esem[eng]:
                    self._wait(eng, s, v)
        if dma_sem is not None and dma_wait_prior:
            self._wait(eng, dma_sem, self.cnt[dma_sem])
        incs = []
        self.prog[eng].append(("op", fn, incs))
        if dma_sem is not None:
            self.cnt[dma_sem] += 16
            incs.append((dma_sem, 16))
            c = self.cnt[dma_sem]
            for w in writes:
                w.wr = [(dma_sem, c)]
                w.rd = []
            for r in reads:
                r.rd.append((dma_sem, c))
        else:
            for w in writes:
                self.pend_wr[eng].append(w)
                self.pw_eng[id(w)] = eng
            for r in reads:
                self.pend_rd[eng].append(r)
                self.pr_eng.setdefault(id(r), set()).add(eng)
            if bump:
                self.bump(eng)
        if eng in ("vector", "scalar"):
            self.lastsmall[eng] = set(w.name for w in writes) if small else set()

    def replay(self, block):
        def make(eng_name):
            def run(eng):
                for item in self.prog[eng_name]:
                    if item[0] == "wait":
                        eng.wait_ge(self.sem[item[1]], item[2])
                    else:
                        ins = item[1](eng)
                        for (s, v) in item[2]:
                            ins.then_inc(self.sem[s], v)
            return run
        block.tensor(make("tensor"))
        block.vector(make("vector"))
        block.scalar(make("scalar"))
        block.sync(make("sync"))
        block.gpsimd(make("gpsimd"))


def build_nc(debug_stage="full", n_macro=N_MACRO, want_handles=False, stop_after="full", debug_dump=False):
    nc = bass.Bass("TRN2", target_bir_lowering=False, debug=False,
                   detect_race_conditions=False)

    x_in = nc.dram_tensor("x", [S, D], F32, kind="ExternalInput").ap()
    wqT = nc.dram_tensor("wqT", [D, D], F32, kind="ExternalInput").ap()
    wkT = nc.dram_tensor("wkT", [D, D], F32, kind="ExternalInput").ap()
    wvT = nc.dram_tensor("wvT", [D, D], F32, kind="ExternalInput").ap()
    wpT = nc.dram_tensor("wpT", [D, D], F32, kind="ExternalInput").ap()
    wrT = nc.dram_tensor("wrT", [D, D], F32, kind="ExternalInput").ap()
    posemb = nc.dram_tensor("posemb", [NPOS, D], F32, kind="ExternalInput").ap()
    pds = nc.dram_tensor("pds", [128], F32, kind="ExternalInput").ap()
    gateT = nc.dram_tensor("gateT", [D, E], F32, kind="ExternalInput").ap()
    corrb = nc.dram_tensor("corrb", [E], F32, kind="ExternalInput").ap()
    w1T = nc.dram_tensor("w1T", [E, LAT, 512], F32, kind="ExternalInput").ap()
    w2T = nc.dram_tensor("w2T", [E, 512, LAT], F32, kind="ExternalInput").ap()
    fc1T = nc.dram_tensor("fc1T", [D, LAT], F32, kind="ExternalInput").ap()
    fc2T = nc.dram_tensor("fc2T", [LAT, D], F32, kind="ExternalInput").ap()
    upT = nc.dram_tensor("upT", [D, 4096], F32, kind="ExternalInput").ap()
    dnT = nc.dram_tensor("dnT", [4096, D], F32, kind="ExternalInput").ap()
    idbf_in = nc.dram_tensor("idbf", [128, 128], BF16, kind="ExternalInput").ap()
    idf_in = nc.dram_tensor("idf", [128, 128], F32, kind="ExternalInput").ap()
    iota_in = nc.dram_tensor("iota8", [128, E], F32, kind="ExternalInput").ap()

    out_d = nc.dram_tensor("out", [S, D], F32, kind="ExternalOutput").ap()

    bdscr = nc.dram_tensor("bdscr", [2, H, 8, 300], F32, kind="Internal").ap()
    w1s = nc.dram_tensor("w1s", [E, LAT, 512], BF16, kind="Internal").ap()
    w2s = nc.dram_tensor("w2s", [E, 512, LAT], BF16, kind="Internal").ap()
    ups = nc.dram_tensor("ups", [D, 4096], BF16, kind="Internal").ap()
    dns = nc.dram_tensor("dns", [4096, D], BF16, kind="Internal").ap()

    es = ExitStack()
    def sb(name, shape, dtype=F32):
        return es.enter_context(nc.sbuf_tensor(name, shape, dtype))

    wq_sb = sb("wq_sb", [128, 8 * D], BF16)
    wk_sb = sb("wk_sb", [128, 8 * D], BF16)
    wv_sb = sb("wv_sb", [128, 8 * D], BF16)
    wp_sb = sb("wp_sb", [128, 8 * D], BF16)
    fc1_sb = sb("fc1_sb", [128, 8 * LAT], BF16)
    fc2_sb = sb("fc2_sb", [128, 4 * D], BF16)
    gate_sb = sb("gate_sb", [128, 8 * E], F32)
    relk_sb = sb("relk_sb", [128, H * NPOS], BF16)
    posfm = sb("posfm", [128, 8 * NPOS], BF16)
    idbf = sb("idbf_s", [128, 128], BF16)
    idf = sb("idf_s", [128, 128], F32)
    iota8 = sb("iota8_s", [128, E], F32)
    ones1 = sb("ones1", [1, 128], F32)
    corrb_sb = sb("corrb_sb", [1, E], F32)
    biasb = sb("biasb", [128, E], F32)
    qscale = sb("qscale", [128, 1], F32)
    pds_sb = sb("pds_sb", [128, 4], F32)

    x_ld = sb("x_ld", [128, D], F32)
    x_res = [x_ld]
    xn32 = sb("xn32", [128, D], F32)
    xnfm32 = sb("xnfm32", [128, D], F32)
    xn_sb = [sb(f"xn_sb{i}", [128, 8 * 396], BF16) for i in range(2)]
    q_sb = sb("q_sb", [128, H * 384], BF16)
    k_sb = sb("k_sb", [128, H * 396], BF16)
    v_sb = [sb(f"v_sb{i}", [120, 1024], BF16) for i in range(4)]
    attn_sb = sb("attn_sb", [128, H * 384], BF16)
    s_sb = sb("s_sb", [128, 3 * D], F32)
    lat_sb = sb("lat_sb", [128, 4 * 384], BF16)

    raw1_sb = [sb("raw1_sb0", [96, H * NPOS], F32)]
    bdm = [sb(f"bdm{i}", [96, H * 108], F32) for i in range(2)]
    ubuf = sb("ubuf", [96, H * 108], F32)
    tbuf = sb("tbuf", [96, H * 108], F32)
    sums = sb("sums", [96, H], F32)
    inv = sb("inv", [96, H], F32)
    ptbf = sb("ptbf", [108, H * 96], BF16)

    rt_sig = sb("rt_sig", [128, E], F32)
    rt_tt = sb("rt_tt", [128, E], F32)
    rt_sfc = sb("rt_sfc", [128, E], F32)
    rt_pad = sb("rt_pad", [128, 16], F32)
    rt_srt = sb("rt_srt", [128, 16], F32)
    rt_gs = sb("rt_gs", [128, 4], F32)
    rt_m = sb("rt_m", [128, E], F32)
    rt_sm = sb("rt_sm", [128, E], F32)
    rt_idx = sb("rt_idx", [128, E], U32)
    rt_idxf = sb("rt_idxf", [128, 2], F32)
    rt_sel = sb("rt_sel", [128, 8], F32)
    rt_eq = sb("rt_eq", [128, 2 * E], F32)
    rt_W = sb("rt_W", [128, E], F32)
    rt_Wg = sb("rt_Wg", [128, 3 * E], F32)
    wfm = [sb(f"wfm{i}", [8, 384], F32) for i in range(2)]

    lnst = sb("lnst", [128, 36], F32)
    lnmv = sb("lnmv", [128, 6], F32)
    lnv = sb("lnv", [128, 3], F32)
    lny = sb("lny", [128, 6], F32)
    lnt = sb("lnt", [128, 6], F32)

    scr34 = sb("scr34", [128, 864], F32)
    trelu = [scr34[:, 0:384]]
    ttw = scr34[:, 384:768]
    interbf = sb("interbf", [128, 4 * 384], BF16)
    combbf = sb("combbf", [128, 4 * 384], BF16)
    upbf = [sb(f"upbf{i}", [128, 384], BF16) for i in range(2)]
    g2b = xnfm32
    ttn = xn32
    outb = [sb("outb0", [128, D], F32)]

    w1b = [sb(f"w1b{i}", [128, 4 * 512], BF16) for i in range(2)]
    w2b = [sb(f"w2b{i}", [128, 4 * 512], BF16) for i in range(2)]
    upb = [sb(f"upb{i}", [128, 8 * 128], BF16) for i in range(2)]
    dnb = [sb(f"dnb{i}", [128, D], BF16) for i in range(2)]
    scr_sb = sb("scr_sb", [64, 300], F32)
    junkv = sb("junkv", [128, 8], F32)
    junks = junkv

    pbank = [es.enter_context(nc.psum_tensor(f"pbank{i}", [128, 512], F32)) for i in range(8)]
    def BANK(i):
        return pbank[i][:]

    sched = Sched(nc)
    for sname in ("cT", "cV", "cS", "cG",
                  "sW10", "sW11", "sW20", "sW21", "sU0", "sU1", "sD0", "sD1",
                  "sX", "sO", "sBW0", "sBW1", "sBM0", "sBM1", "sRW", "sI", "sY0", "sG0", "sG1", "sC0", "sC1", "sC2", "sC3", "sBM2", "sBM3", "sBM4", "sBM5", "sBM6", "sBM7"):
        sched.add_sem(sname, es.enter_context(nc.semaphore(sname)))
    sched.junk = {
        "vector": [(lambda v: v.reduce_sum(out=junkv[:, 0:1], in_=wq_sb[:, 0:32], axis=AX.X)),
                   (lambda v: v.reduce_sum(out=junkv[:, 4:5], in_=wq_sb[:, 32:64], axis=AX.X))],
        "scalar": [(lambda s_: s_.copy(out=junks[0:64, 0:4], in_=iota8[0:64, 0:4])),
                   (lambda s_: s_.copy(out=junks[0:64, 4:8], in_=iota8[0:64, 4:8]))],
    }

    r_bank = [Res(f"bank{i}") for i in range(8)]
    r_wres = Res("wres")
    r_xld = Res("x_ld")
    r_xn32 = Res("xn32")
    r_xnfm32 = Res("xnfm32")
    r_xn = [Res(f"xn{i}") for i in range(2)]
    r_q = Res("q"); r_k = Res("k")
    r_v = [Res(f"v{i}") for i in range(4)]
    r_attn = Res("attn"); r_s = Res("s"); r_lat = Res("lat")
    r_raw1 = [Res("raw1_0")]
    r_bdscr = [Res(f"bdscr{i}") for i in range(2)]
    r_scrsb = Res("scr_sb")
    r_bdm = [Res(f"bdm{i}") for i in range(2)]
    r_scr = Res("scr34")
    r_ubufh = [Res("ubuf0"), Res("ubuf1")]
    r_tbh = [[Res("tb00"), Res("tb01")], [r_scr, r_scr]]
    r_ubuf = r_ubufh[0]
    r_sums = Res("sums"); r_inv = Res("inv")
    r_ptbf = Res("ptbf")
    r_rt = Res("rt"); r_idx = Res("rt_idx"); r_W = Res("rt_W"); r_Wg = Res("rt_Wg")
    r_wfm = [Res(f"wfm{i}") for i in range(2)]
    r_ln = Res("ln")
    r_trelu = [r_scr]
    r_ttw = r_scr
    r_interbf = Res("interbf"); r_combbf = Res("combbf")
    r_upbf = [Res(f"upbf{i}") for i in range(2)]
    r_outb = [Res("outb0")]
    r_w1b = [Res(f"w1b{i}") for i in range(2)]
    r_w2b = [Res(f"w2b{i}") for i in range(2)]
    r_upb = [Res(f"upb{i}") for i in range(2)]
    r_dnb = [Res(f"dnb{i}") for i in range(2)]
    r_wscr = Res("wscr")
    r_postm = Res("pos_tm"); r_posfm = Res("posfm")
    r_out = Res("out")
    r_g2 = r_xnfm32
    r_ttn = r_xn32
    r_xres = [r_xld]

    EM = sched.emit

    EM("sync", lambda e: e.dma_start(out=idbf[:], in_=idbf_in), writes=[r_wres], dma_sem="sY0")
    EM("sync", lambda e: e.dma_start(out=idf[:], in_=idf_in), writes=[r_wres], dma_sem="sY0")
    EM("sync", lambda e: e.dma_start(out=iota8[:], in_=iota_in), writes=[r_wres], dma_sem="sY0")
    EM("sync", lambda e: e.dma_start(out=pds_sb[:, 0:1], in_=pds.rearrange("(p o) -> p o", o=1)),
       writes=[r_wres], dma_sem="sY0")
    EM("sync", lambda e: e.dma_start(out=corrb_sb[:], in_=corrb.rearrange("(o e) -> o e", o=1)),
       writes=[r_wres], dma_sem="sY0")
    EM("sync", lambda e: e.dma_start(out=gate_sb[:].rearrange("p (c e) -> p c e", c=8),
                                     in_=gateT.rearrange("(c p) e -> p c e", p=128)),
       writes=[r_wres], dma_sem="sY0")
    EM("sync", lambda e: e.dma_start(out=x_ld[0:25, :], in_=posemb), writes=[r_postm], dma_sem="sY0")

    for (dst, src) in ((wq_sb, wqT), (wk_sb, wkT), (wv_sb, wvT), (wp_sb, wpT)):
        EM("gpsimd", lambda e, d=dst, s_=src: e.dma_start(
            out=d[:].rearrange("p (c o) -> p c o", c=8),
            in_=s_.rearrange("(c p) o -> p c o", p=128)),
           writes=[r_wres], dma_sem="sI")
    EM("gpsimd", lambda e: e.dma_start(out=fc1_sb[:].rearrange("p (c o) -> p c o", c=8),
                                       in_=fc1T.rearrange("(c p) o -> p c o", p=128)),
       writes=[r_wres], dma_sem="sI")
    EM("gpsimd", lambda e: e.dma_start(out=fc2_sb[:].rearrange("p (c o) -> p c o", c=4),
                                       in_=fc2T.rearrange("(c p) o -> p c o", p=128)),
       writes=[r_wres], dma_sem="sI")

    sched._wait("vector", "sI", sched.cnt["sI"])
    sched._wait("vector", "sY0", sched.cnt["sY0"])
    EM("vector", lambda e: e.memset(ones1[:], 1.0), writes=[r_wres, r_postm], bump=True)
    EM("scalar", lambda e: e.activation(out=pds_sb[:, 1:2], in_=pds_sb[:, 0:1], func=AF.Abs),
       reads=[r_wres], writes=[r_wres], small=True)
    EM("scalar", lambda e: e.activation(out=pds_sb[:, 2:3], in_=pds_sb[:, 1:2], func=AF.Exp, scale=-1.0),
       reads=[r_wres], writes=[r_wres], small=True)
    EM("vector", lambda e: e.tensor_scalar_add(out=pds_sb[:, 3:4], in0=pds_sb[:, 2:3], scalar1=1.0),
       reads=[r_wres], writes=[r_wres], small=True, bump=True)
    EM("scalar", lambda e: e.activation(out=pds_sb[:, 2:3], in_=pds_sb[:, 3:4], func=AF.Ln),
       reads=[r_wres], writes=[r_wres], small=True)
    EM("vector", lambda e: e.tensor_scalar_max(out=pds_sb[:, 1:2], in0=pds_sb[:, 0:1], scalar1=0.0),
       reads=[r_wres], writes=[r_wres], small=True, bump=True)
    EM("vector", lambda e: e.tensor_add(out=pds_sb[:, 3:4], in0=pds_sb[:, 1:2], in1=pds_sb[:, 2:3]),
       reads=[r_wres], writes=[r_wres], small=True)
    EM("vector", lambda e: e.tensor_scalar_mul(out=qscale[:], in0=pds_sb[:, 3:4], scalar1=Q_SCALE),
       reads=[r_wres], writes=[r_wres], small=True, bump=True)
    EM("tensor", lambda e: e.matmul(out=BANK(7)[0:128, 0:E], lhsT=ones1[:],
                                    rhs=corrb_sb[:], start=True, stop=True),
       reads=[r_wres], acq=[r_bank[7]], writes=[r_bank[7]], bump=True)
    EM("vector", lambda e: e.tensor_copy(out=biasb[:], in_=BANK(7)[0:128, 0:E]),
       reads=[r_bank[7]], writes=[r_wres], bump=True)


    for dc in range(8):
        EM("tensor", lambda e, dc=dc: e.transpose(out=BANK(6)[0:128, 0:NPOS],
                                                  in_=x_ld[0:25, dc * 128:(dc + 1) * 128],
                                                  identity=idf[0:25, 0:25]),
           reads=[r_postm, r_wres], acq=[r_bank[6]], writes=[r_bank[6]], bump=True)
        EM("vector", lambda e, dc=dc: e.tensor_copy(out=posfm[:, dc * NPOS:(dc + 1) * NPOS],
                                                    in_=BANK(6)[0:128, 0:NPOS]),
           reads=[r_bank[6]], writes=[r_posfm], bump=True)
    for quarter in range(4):
        buf = (w1b[0], w2b[0])[quarter % 2]
        rbuf = (r_w1b[0], r_w2b[0])[quarter % 2]
        ssem = ("sG0", "sG1")[quarter % 2]
        EM("gpsimd", lambda e, q=quarter, buf=buf: e.dma_start(
            out=buf[:, 0:2048].rearrange("p (c o) -> p c o", c=2),
            in_=wrT[q * 256:(q + 1) * 256, :].rearrange("(c p) o -> p c o", p=128)),
           reads=[rbuf], writes=[rbuf], dma_sem=ssem)
        for j in range(2):
            dc = quarter * 2 + j
            for h in range(H):
                EM("tensor", lambda e, j=j, dc=dc, h=h, buf=buf: e.matmul(
                    out=BANK(6)[0:128, h * NPOS:(h + 1) * NPOS],
                    lhsT=buf[:, j * 1024 + h * 128: j * 1024 + (h + 1) * 128],
                    rhs=posfm[:, dc * NPOS:(dc + 1) * NPOS],
                    start=(dc == 0 and h == 0), stop=(dc == 7 and h == H - 1)),
                   reads=[rbuf, r_posfm],
                   acq=[r_bank[6]] if (dc == 0 and h == 0) else [],
                   writes=[r_bank[6]] if (dc == 7 and h == H - 1) else [],
                   bump=(dc == 7 and h == H - 1))
    EM("vector", lambda e: e.tensor_scalar_mul(out=relk_sb[:],
                                               in0=BANK(6)[0:128, 0:H * NPOS], scalar1=1.0 / SCAP),
       reads=[r_bank[6]], writes=[r_wres], bump=True)

    for (dst_d, src_d, ssem) in ((ups, upT, "sC2"), (dns, dnT, "sC3"),
                                 (w1s.rearrange("e l i -> (e l) i"), w1T.rearrange("e l i -> (e l) i"), "sC0"),
                                 (w2s.rearrange("e i l -> (e i) l"), w2T.rearrange("e i l -> (e i) l"), "sC1")):
        EM("gpsimd", lambda e, d=dst_d, s_=src_d: e.dma_start(out=d, in_=s_),
           writes=[r_wscr], dma_sem=ssem, dma_wait_prior=False)
    for _s in ("sC0", "sC1", "sC2", "sC3"):
        sched._wait("gpsimd", _s, sched.cnt[_s])
    EM("gpsimd", lambda e: e.memset(junkv[:, 0:4], 0.0), writes=[r_wscr], bump=True)
    for i in range(2):
        EM("vector", lambda e, i=i: e.memset(bdm[i][:], -1000.0 / SCAP), writes=[r_bdm[i]], bump=True)
    EM("vector", lambda e: e.memset(rt_pad[:, 4:8], -1e30), writes=[r_rt], bump=True)
    EM("vector", lambda e: e.memset(rt_pad[:, 12:16], -1e30), writes=[r_rt], bump=True)
    EM("vector", lambda e: e.memset(k_sb[:].rearrange("p (h t) -> p h t", h=H)[:, :, 0:12], 0.0),
       writes=[r_k], bump=True)
    EM("vector", lambda e: e.memset(xn_sb[0][:].rearrange("p (c t) -> p c t", c=8)[:, :, 0:12], 0.0),
       writes=[r_xn[0]], bump=True)


    def emit_P012(mi, groups=None, phase=None):
        import os
        p012_lvl = int(os.environ.get("P012_LVL", "9"))
        mm = MACROS[mi]
        if groups is None:
            groups = list(range(mm["ngrp"]))
        xn = xn_sb[mi % 2]; rxn = r_xn[mi % 2]
        xnv = xn[:].rearrange("p (c t) -> p c t", c=8)
        if mi > 0 and 0 in groups:
            prev = xn_sb[(mi - 1) % 2]; rprev = r_xn[(mi - 1) % 2]
            po = MACROS[mi - 1]["ntok"]
            EM("vector", lambda e, prev=prev, xnv=xnv, po=po: e.tensor_copy(
                out=xnv[:, :, 0:12],
                in_=prev[:].rearrange("p (c t) -> p c t", c=8)[:, :, po:po + 12]),
               reads=[rprev], writes=[rxn], bump=True)
        wf = wfm[mi % 2]; rwf = r_wfm[mi % 2]
        for g in groups:
            if g >= mm["ngrp"]:
                continue
            t0 = mm["t0"] + g * 128
            if phase != "rt":
              EM("scalar", lambda e, t0=t0: e.dma_start(out=x_ld[:], in_=x_in[t0:t0 + 128, :]),
               reads=[r_xld], writes=[r_xld, r_postm], dma_sem="sX")
              EM("vector", lambda e, g=g: e.bn_stats(out=lnst[:, 12 * g:12 * g + 6], in_=x_ld[:, 0:512]),
                 reads=[r_xld], writes=[r_ln], small=True)
              EM("vector", lambda e, g=g: e.bn_stats(out=lnst[:, 12 * g + 6:12 * g + 12],
                                                     in_=x_ld[:, 512:1024]),
                 reads=[r_xld], writes=[r_ln], small=True)
              EM("vector", lambda e, g=g: e.bn_aggr(out=lnmv[:, 2 * g:2 * g + 2],
                                                    in_=lnst[:, 12 * g:12 * g + 12]),
                 reads=[r_ln], writes=[r_ln], small=True)
              EM("vector", lambda e, g=g: e.tensor_scalar_add(out=lnv[:, g:g + 1],
                                                              in0=lnmv[:, 2 * g + 1:2 * g + 2], scalar1=1e-5),
                 reads=[r_ln], writes=[r_ln], small=True)
              EM("vector", lambda e, g=g: e.reciprocal(out=lny[:, 2 * g:2 * g + 1], in_=lnv[:, g:g + 1]),
                 reads=[r_ln], writes=[r_ln], small=True)
              for it in range(4):
                  a, b = 2 * g + (it % 2), 2 * g + ((it + 1) % 2)
                  EM("vector", lambda e, a=a: e.tensor_mul(out=lnt[:, 0:1], in0=lny[:, a:a + 1],
                                                           in1=lny[:, a:a + 1]),
                     reads=[r_ln], writes=[r_ln], small=True)
                  EM("vector", lambda e, g=g: e.tensor_mul(out=lnt[:, 1:2], in0=lnt[:, 0:1],
                                                           in1=lnv[:, g:g + 1]),
                     reads=[r_ln], writes=[r_ln], small=True)
                  EM("vector", lambda e: e.tensor_scalar(out=lnt[:, 2:3], in0=lnt[:, 1:2], scalar1=-0.5,
                                                         scalar2=1.5, op0=ALU.mult, op1=ALU.add),
                     reads=[r_ln], writes=[r_ln], small=True)
                  EM("vector", lambda e, a=a, b=b: e.tensor_mul(out=lny[:, b:b + 1],
                                                                in0=lny[:, a:a + 1], in1=lnt[:, 2:3]),
                     reads=[r_ln], writes=[r_ln], small=True)
              if p012_lvl < 1:
                  continue
              EM("vector", lambda e, g=g: e.tensor_scalar(out=xn32[:], in0=x_ld[:],
                                                          scalar1=lnmv[:, 2 * g:2 * g + 1],
                                                          scalar2=lny[:, 2 * g:2 * g + 1],
                                                          op0=ALU.subtract, op1=ALU.mult),
                 reads=[r_xld, r_ln], writes=[r_xn32], bump=True)
            if p012_lvl < 2:
                continue
            if phase == "ln":
                continue
            for half in range(2):
                for j in range(4):
                    dc = half * 4 + j
                    EM("tensor", lambda e, dc=dc, j=j: e.transpose(
                        out=BANK(7)[0:128, j * 128:(j + 1) * 128],
                        in_=xn32[:, dc * 128:(dc + 1) * 128], identity=idf[:]),
                       reads=[r_xn32, r_wres],
                       acq=[r_bank[7]] if j == 0 else [],
                       writes=[r_bank[7]] if j == 3 else [], bump=(j == 3))
                EM("vector", lambda e, half=half: e.tensor_copy(
                    out=xnfm32[:, half * 512:(half + 1) * 512], in_=BANK(7)[:]),
                   reads=[r_bank[7]], writes=[r_xnfm32])
                EM("vector", lambda e, half=half, g=g, xnv=xnv: e.tensor_copy(
                    out=xnv[:, half * 4:half * 4 + 4, 12 + g * 128:12 + g * 128 + 128],
                    in_=BANK(7)[:].rearrange("p (c t) -> p c t", c=4)),
                   reads=[r_bank[7]], writes=[rxn, r_bank[7]], bump=True)
            if p012_lvl < 3:
                continue
            for dc in range(8):
                EM("tensor", lambda e, dc=dc: e.matmul(
                    out=BANK(7)[0:128, 0:E],
                    lhsT=xnfm32[:, dc * 128:(dc + 1) * 128],
                    rhs=gate_sb[:, dc * E:(dc + 1) * E],
                    start=(dc == 0), stop=(dc == 7)),
                   reads=[r_xnfm32, r_wres],
                   acq=[r_bank[7]] if dc == 0 else [],
                   writes=[r_bank[7]] if dc == 7 else [], bump=(dc == 7))
            if p012_lvl < 4:
                continue
            EM("vector", lambda e: e.tensor_copy(out=rt_m[:], in_=BANK(7)[0:128, 0:E]),
               reads=[r_bank[7]], writes=[r_rt, r_bank[7]], small=True, bump=True)
            EM("scalar", lambda e: e.activation(out=rt_tt[:], in_=rt_m[:],
                                                func=AF.Tanh, scale=0.5),
               reads=[r_rt], writes=[r_rt], small=True, bump=True)
            EM("vector", lambda e: e.tensor_scalar(out=rt_sig[:], in0=rt_tt[:], scalar1=0.5,
                                                   scalar2=0.5, op0=ALU.mult, op1=ALU.add),
               reads=[r_rt], writes=[r_rt], small=True)
            EM("vector", lambda e: e.tensor_add(out=rt_sfc[:], in0=rt_sig[:], in1=biasb[:]),
               reads=[r_rt, r_wres], writes=[r_rt], small=True)
            EM("vector", lambda e: e.tensor_copy(out=rt_pad[:, 0:4], in_=rt_sfc[:, 0:4]),
               reads=[r_rt], writes=[r_rt], small=True)
            EM("vector", lambda e: e.tensor_copy(out=rt_pad[:, 8:12], in_=rt_sfc[:, 4:8]),
               reads=[r_rt], writes=[r_rt], small=True)
            EM("vector", lambda e: e.max(out=rt_srt[:, 0:8], in_=rt_pad[:, 0:8]),
               reads=[r_rt], writes=[r_rt], small=True)
            EM("vector", lambda e: e.max(out=rt_srt[:, 8:16], in_=rt_pad[:, 8:16]),
               reads=[r_rt], writes=[r_rt], small=True)
            EM("vector", lambda e: e.tensor_add(out=rt_gs[:, 0:2], in0=rt_srt[:, 0:16:8],
                                                in1=rt_srt[:, 1:16:8]),
               reads=[r_rt], writes=[r_rt], small=True)
            EM("vector", lambda e: e.tensor_tensor(out=rt_gs[:, 2:3], in0=rt_gs[:, 0:1],
                                                   in1=rt_gs[:, 1:2], op=ALU.is_ge),
               reads=[r_rt], writes=[r_rt], small=True)
            EM("vector", lambda e: e.tensor_scalar(out=rt_gs[:, 3:4], in0=rt_gs[:, 2:3], scalar1=-1.0,
                                                   scalar2=1.0, op0=ALU.mult, op1=ALU.add),
               reads=[r_rt], writes=[r_rt], small=True)
            EM("vector", lambda e: e.tensor_scalar_mul(out=rt_m[:, 0:4], in0=rt_sfc[:, 0:4],
                                                       scalar1=rt_gs[:, 2:3]),
               reads=[r_rt], writes=[r_rt], small=True)
            EM("vector", lambda e: e.tensor_scalar_mul(out=rt_m[:, 4:8], in0=rt_sfc[:, 4:8],
                                                       scalar1=rt_gs[:, 3:4]),
               reads=[r_rt], writes=[r_rt], small=True)
            EM("vector", lambda e: e.max(out=rt_sm[:], in_=rt_m[:]),
               reads=[r_rt], writes=[r_rt], small=True)
            EM("vector", lambda e: e.tensor_copy(out=junkv[:, 0:8], in_=rt_sm[:]),
               reads=[r_rt], writes=[], small=True)
            if p012_lvl < 5:
                continue
            EM("vector", lambda e: e.max_index(out=rt_idx[:], in_max=rt_sm[:], in_values=rt_m[:]),
               reads=[r_rt], writes=[r_idx], small=True)
            EM("vector", lambda e: e.tensor_copy(out=rt_idxf[:], in_=rt_idx[:, 0:2]),
               reads=[r_idx], writes=[r_idx], small=True)
            EM("vector", lambda e: e.scalar_tensor_tensor(out=rt_eq[:, 0:8], in0=iota8[:],
                                                          scalar=rt_idxf[:, 0:1], in1=rt_sig[:],
                                                          op0=ALU.is_equal, op1=ALU.mult,
                                                          accum_out=rt_sel[:, 0:1]),
               reads=[r_idx, r_rt, r_wres], writes=[r_rt], small=True)
            EM("vector", lambda e: e.scalar_tensor_tensor(out=rt_eq[:, 8:16], in0=iota8[:],
                                                          scalar=rt_idxf[:, 1:2], in1=rt_sig[:],
                                                          op0=ALU.is_equal, op1=ALU.mult,
                                                          accum_out=rt_sel[:, 1:2]),
               reads=[r_idx, r_rt, r_wres], writes=[r_rt], small=True)
            EM("vector", lambda e: e.tensor_add(out=rt_sel[:, 2:3], in0=rt_sel[:, 0:1],
                                                in1=rt_sel[:, 1:2]),
               reads=[r_rt], writes=[r_rt], small=True)
            EM("vector", lambda e: e.tensor_scalar_add(out=rt_sel[:, 3:4], in0=rt_sel[:, 2:3],
                                                       scalar1=1e-20),
               reads=[r_rt], writes=[r_rt], small=True)
            EM("vector", lambda e: e.reciprocal(out=rt_sel[:, 4:5], in_=rt_sel[:, 3:4]),
               reads=[r_rt], writes=[r_rt], small=True)
            EM("vector", lambda e: e.tensor_mul(out=rt_sel[:, 5:6], in0=rt_sel[:, 0:1],
                                                in1=rt_sel[:, 4:5]),
               reads=[r_rt], writes=[r_rt], small=True)
            EM("vector", lambda e: e.tensor_mul(out=rt_sel[:, 6:7], in0=rt_sel[:, 1:2],
                                                in1=rt_sel[:, 4:5]),
               reads=[r_rt], writes=[r_rt], small=True)
            EM("vector", lambda e: e.tensor_single_scalar(out=rt_eq[:, 0:8], in_=iota8[:],
                                                          scalar=rt_idxf[:, 0:1], op=ALU.is_equal),
               reads=[r_idx, r_wres], writes=[r_rt], small=True)
            EM("vector", lambda e: e.tensor_single_scalar(out=rt_eq[:, 8:16], in_=iota8[:],
                                                          scalar=rt_idxf[:, 1:2], op=ALU.is_equal),
               reads=[r_idx, r_wres], writes=[r_rt], small=True)
            EM("vector", lambda e: e.tensor_scalar_mul(out=rt_eq[:, 0:8], in0=rt_eq[:, 0:8],
                                                       scalar1=rt_sel[:, 5:6]),
               reads=[r_rt], writes=[r_rt], small=True)
            EM("vector", lambda e: e.tensor_scalar_mul(out=rt_eq[:, 8:16], in0=rt_eq[:, 8:16],
                                                       scalar1=rt_sel[:, 6:7]),
               reads=[r_rt], writes=[r_rt], small=True)
            EM("vector", lambda e: e.tensor_add(out=rt_W[:], in0=rt_eq[:, 0:8], in1=rt_eq[:, 8:16]),
               reads=[r_rt], writes=[r_W], small=True, bump=True)
            if p012_lvl < 6:
                continue
            EM("vector", lambda e, g=g: e.tensor_copy(out=rt_Wg[:, g * E:(g + 1) * E], in_=rt_W[:]),
               reads=[r_W], writes=[r_Wg], small=True, bump=True)

    def emit_wtrans(mi, g):
        mm = MACROS[mi]
        if g >= mm["ngrp"]:
            return
        wf = wfm[mi % 2]; rwf = r_wfm[mi % 2]
        EM("tensor", lambda e, g=g: e.transpose(out=BANK(7)[0:8, 0:128],
                                                in_=rt_Wg[:, g * E:(g + 1) * E], identity=idf[:]),
           reads=[r_Wg, r_wres], acq=[r_bank[7]], writes=[r_bank[7]], bump=True)
        EM("vector", lambda e, wf=wf, g=g: e.tensor_copy(out=wf[0:8, g * 128:(g + 1) * 128],
                                                         in_=BANK(7)[0:8, 0:128]),
           reads=[r_bank[7]], writes=[rwf], bump=True)

    def bd_emit(mi, si):
        mm = MACROS[mi]
        if si >= len(mm["subs"]):
            return
        q0, nb = mm["subs"][si]
        qcnt = 12 * nb
        slot = si % 2
        rr = r_raw1[0]; rsc = r_bdscr[slot]; rb = r_bdm[slot]
        for h in range(H):
            EM("tensor", lambda e, h=h, q0=q0, qcnt=qcnt: e.matmul(
                out=BANK(6)[0:96, h * NPOS:(h + 1) * NPOS][0:qcnt, :],
                lhsT=q_sb[:, h * 384 + q0:h * 384 + q0 + qcnt],
                rhs=relk_sb[:, h * NPOS:(h + 1) * NPOS],
                start=True, stop=True),
               reads=[r_q, r_wres],
               acq=[r_bank[6]] if h == 0 else [],
               writes=[r_bank[6]] if h == H - 1 else [], bump=(h == H - 1))
        EM("vector", lambda e, qcnt=qcnt, slot=slot: e.tensor_copy(
            out=raw1_sb[0][0:qcnt, :], in_=BANK(6)[0:96, 0:H * NPOS][0:qcnt, :]),
           reads=[r_bank[6]], writes=[rr], bump=True)
        EM("sync", lambda e, qcnt=qcnt, slot=slot: e.dma_start(
            out=bdscr[slot].rearrange("h nb (c p) -> (nb c) h p", c=CH, p=NPOS)[0:qcnt, :, :],
            in_=raw1_sb[0][0:qcnt, :].rearrange("j (h p) -> j h p", h=H)),
           reads=[rr], writes=[rsc], dma_sem=("sBW0", "sBW1")[slot])
        for b in range(nb):
            EM("sync", lambda e, b=b, slot=slot: e.dma_start(
                out=bdm[slot][b * CH:(b + 1) * CH, :].rearrange("c (h q) -> c h q", h=H)[:, :, 12 * b:12 * b + CTX],
                in_=bdscr[slot][:, b, 0:288].rearrange("h (c k) -> c h k", c=CH)),
               reads=[rsc], writes=[rb],
               dma_sem=("sBM0", "sBM1", "sBM2", "sBM3", "sBM4", "sBM5", "sBM6", "sBM7")[slot * 4 + b % 4])

    def emit_P3(mi, part=None):
        mm = MACROS[mi]
        xn = xn_sb[mi % 2]; rxn = r_xn[mi % 2]
        ntok = mm["ntok"]; npad = 12 * mm["nblk"] - ntok
        xnv = xn[:].rearrange("p (c t) -> p c t", c=8)
        if part in (None, 0):
          if npad:
            EM("vector", lambda e, xnv=xnv, ntok=ntok, npad=npad: e.memset(
                xnv[:, :, 12 + ntok:12 + ntok + npad], 0.0), writes=[rxn], bump=True)
          for h in range(H):
            bq = r_bank[6 + h % 2]
            for dc in range(8):
                EM("tensor", lambda e, h=h, dc=dc, xnv=xnv, ntok=ntok: e.matmul(
                    out=BANK(6 + h % 2)[0:128, 0:ntok],
                    lhsT=wq_sb[:, dc * D + h * 128: dc * D + (h + 1) * 128],
                    rhs=xnv[:, dc, 12:12 + ntok],
                    start=(dc == 0), stop=(dc == 7)),
                   reads=[rxn, r_wres], acq=[bq] if dc == 0 else [],
                   writes=[bq] if dc == 7 else [], bump=(dc == 7))
            EM("vector", lambda e, h=h, ntok=ntok: e.tensor_scalar_mul(
                out=q_sb[:, h * 384:h * 384 + ntok], in0=BANK(6 + h % 2)[0:128, 0:ntok],
                scalar1=qscale[:, 0:1]),
               reads=[bq, r_wres], writes=[r_q], bump=True)
        if part in (None, 0):
            bd_emit(mi, 0)
        if part in (None, 1):
          for h in range(H):
            bk = r_bank[(6, 7, 0, 1)[h % 4]]
            for dc in range(8):
                EM("tensor", lambda e, h=h, dc=dc, xnv=xnv, ntok=ntok: e.matmul(
                    out=BANK((6, 7, 0, 1)[h % 4])[0:128, 0:ntok],
                    lhsT=wk_sb[:, dc * D + h * 128: dc * D + (h + 1) * 128],
                    rhs=xnv[:, dc, 12:12 + ntok],
                    start=(dc == 0), stop=(dc == 7)),
                   reads=[rxn, r_wres], acq=[bk] if dc == 0 else [],
                   writes=[bk] if dc == 7 else [], bump=(dc == 7))
            EM("vector", lambda e, h=h, ntok=ntok: e.tensor_scalar_mul(
                out=k_sb[:, h * 396 + 12:h * 396 + 12 + ntok], in0=BANK((6, 7, 0, 1)[h % 4])[0:128, 0:ntok],
                scalar1=K_SCALE),
               reads=[bk], writes=[r_k], bump=True)
        if part in (None, 1):
            bd_emit(mi, 1)
        if part not in (None, 2):
            return
        if npad:
            EM("vector", lambda e, ntok=ntok, npad=npad: e.memset(
                k_sb[:].rearrange("p (h t) -> p h t", h=H)[:, :, 12 + ntok:12 + ntok + npad], 0.0),
               writes=[r_k], bump=True)
        for si, (q0, nb) in enumerate(mm["subs"]):
            kw = 12 * nb + 12
            b0, b1 = 6, 7
            for dc in range(8):
                for half in range(2):
                    b = (b0, b1)[half]
                    EM("tensor", lambda e, dc=dc, half=half, q0=q0, kw=kw, b=b, xnv=xnv: e.matmul(
                        out=BANK(b)[0:kw, 0:512],
                        lhsT=xnv[:, dc, q0:q0 + kw],
                        rhs=wv_sb[:, dc * D + half * 512: dc * D + half * 512 + 512],
                        start=(dc == 0), stop=(dc == 7)),
                       reads=[rxn, r_wres],
                       acq=[r_bank[b]] if dc == 0 else [],
                       writes=[r_bank[b]] if dc == 7 else [],
                       bump=(dc == 7 and half == 1))
            EM("vector", lambda e, si=si, kw=kw, b0=b0: e.tensor_copy(
                out=v_sb[si][0:kw, 0:512], in_=BANK(b0)[0:kw, 0:512]),
               reads=[r_bank[b0]], writes=[r_v[si]])
            EM("vector", lambda e, si=si, kw=kw, b1=b1: e.tensor_copy(
                out=v_sb[si][0:kw, 512:1024], in_=BANK(b1)[0:kw, 0:512]),
               reads=[r_bank[b1]], writes=[r_v[si]], bump=True)

    def emit_P4(mi):
        mm = MACROS[mi]
        tbufs = [tbuf, scr34]

        def stage1(si):
            q0, nb = mm["subs"][si]
            qcnt = 12 * nb
            kw = qcnt + 12
            slot = si % 2
            pb = (0, 2)[si % 2]
            tb = tbufs[si % 2]
            for h in range(H):
                bi = pb + h // 4
                EM("tensor", lambda e, h=h, bi=bi, q0=q0, qcnt=qcnt, kw=kw: e.matmul(
                    out=BANK(bi)[0:qcnt, (h % 4) * 108:(h % 4) * 108 + kw],
                    lhsT=q_sb[:, h * 384 + q0:h * 384 + q0 + qcnt],
                    rhs=k_sb[:, h * 396 + q0:h * 396 + q0 + kw],
                    start=True, stop=True),
                   reads=[r_q, r_k],
                   acq=[r_bank[bi]] if h % 4 == 0 else [],
                   writes=[r_bank[bi]] if h % 4 == 3 else [], bump=(h % 4 == 3))
            for j in range(2):
                bi = pb + j
                EM("vector", lambda e, j=j, bi=bi, qcnt=qcnt, kw=kw, slot=slot: e.scalar_tensor_tensor(
                    out=ubuf[0:qcnt, :].rearrange("p (h q) -> p h q", h=H)[:, j * 4:j * 4 + 4, 0:kw],
                    in0=BANK(bi)[0:qcnt, 0:432].rearrange("p (h q) -> p h q", h=4)[:, :, 0:kw],
                    scalar=1.0 / SCAP,
                    in1=bdm[slot][0:qcnt, :].rearrange("p (h q) -> p h q", h=H)[:, j * 4:j * 4 + 4, 0:kw],
                    op0=ALU.mult, op1=ALU.add),
                   reads=[r_bank[bi], r_bdm[slot]], writes=[r_ubufh[j]], bump=True)
            bd_emit(mi, si + 2)
            for j in range(2):
                hs = slice(j * 432, j * 432 + 432)
                EM("scalar", lambda e, qcnt=qcnt, hs=hs, tb=tb: e.activation(
                    out=tb[0:qcnt, hs] if not hasattr(tb, 'tensor') else tb[0:qcnt, hs],
                    in_=ubuf[0:qcnt, hs], func=AF.Tanh),
                   reads=[r_ubufh[j]], writes=[r_tbh[si % 2][j]])
                EM("scalar", lambda e, qcnt=qcnt, hs=hs, tb=tb: e.activation(
                    out=ubuf[0:qcnt, hs], in_=tb[0:qcnt, hs], func=AF.Exp, scale=SCAP),
                   reads=[r_tbh[si % 2][j]], writes=[r_ubufh[j]], bump=True)
                EM("vector", lambda e, qcnt=qcnt, kw=kw, j=j: e.reduce_sum(
                    out=sums[0:qcnt, j * 4:j * 4 + 4],
                    in_=ubuf[0:qcnt, j * 432:j * 432 + 432].rearrange("p (h q) -> p h q", h=4)[:, :, 0:kw],
                    axis=AX.X),
                   reads=[r_ubufh[j]], writes=[r_sums], small=True)
                EM("vector", lambda e, qcnt=qcnt, j=j: e.reciprocal(
                    out=inv[0:qcnt, j * 4:j * 4 + 4], in_=sums[0:qcnt, j * 4:j * 4 + 4]),
                   reads=[r_sums], writes=[r_inv], small=True, bump=True)
                EM("vector", lambda e, qcnt=qcnt, kw=kw, j=j, tb=tb: e.tensor_mul(
                    out=tb[0:qcnt, j * 432:j * 432 + 432].rearrange("p (h q) -> p h q", h=4)[:, :, 0:kw],
                    in0=ubuf[0:qcnt, j * 432:j * 432 + 432].rearrange("p (h q) -> p h q", h=4)[:, :, 0:kw],
                    in1=inv[0:qcnt, j * 4:j * 4 + 4].rearrange("p (h o) -> p h o", o=1).broadcast_to([qcnt, 4, kw])),
                   reads=[r_ubufh[j], r_inv], writes=[r_tbh[si % 2][j]], bump=True)

        def stage2(si):
            q0, nb = mm["subs"][si]
            qcnt = 12 * nb
            kw = qcnt + 12
            pb = (0, 2)[si % 2]
            tb = tbufs[si % 2]
            for h in range(H):
                bi = pb + h // 4
                EM("tensor", lambda e, h=h, bi=bi, qcnt=qcnt, kw=kw, tb=tb: e.transpose(
                    out=BANK(bi)[0:kw, (h % 4) * 96:(h % 4) * 96 + qcnt],
                    in_=tb[0:qcnt, h * 108:h * 108 + kw], identity=idf[0:qcnt, 0:qcnt]),
                   reads=[r_tbh[si % 2][h // 4], r_wres],
                   acq=[r_bank[bi]] if h % 4 == 0 else [],
                   writes=[r_bank[bi]] if h % 4 == 3 else [], bump=(h % 4 == 3))
            for j in range(2):
                bi = pb + j
                EM("vector", lambda e, bi=bi, j=j, kw=kw: e.tensor_copy(
                    out=ptbf[0:kw, j * 384:j * 384 + 384], in_=BANK(bi)[0:kw, 0:384]),
                   reads=[r_bank[bi]], writes=[r_ptbf], bump=True)
            for h in range(H):
                bi = 4 + h // 4
                EM("tensor", lambda e, h=h, bi=bi, si=si, qcnt=qcnt, kw=kw: e.matmul(
                    out=BANK(bi)[0:128, (h % 4) * 96:(h % 4) * 96 + qcnt],
                    lhsT=v_sb[si][0:kw, h * 128:(h + 1) * 128],
                    rhs=ptbf[0:kw, h * 96:h * 96 + qcnt],
                    start=True, stop=True),
                   reads=[r_v[si], r_ptbf],
                   acq=[r_bank[bi]] if h % 4 == 0 else [],
                   writes=[r_bank[bi]] if h % 4 == 3 else [], bump=(h % 4 == 3))
            for bi in (4, 5):
                h0 = (bi - 4) * 4
                EM("vector", lambda e, bi=bi, h0=h0, q0=q0, qcnt=qcnt: e.tensor_copy(
                    out=attn_sb[:].rearrange("p (h t) -> p h t", h=H)[:, h0:h0 + 4, q0:q0 + qcnt],
                    in_=BANK(bi)[0:128, 0:384].rearrange("p (h t) -> p h t", h=4)[:, :, 0:qcnt]),
                   reads=[r_bank[bi]], writes=[r_attn], bump=True)

        for si in range(len(mm["subs"])):
            stage1(si)
            if si > 0:
                stage2(si - 1)
        stage2(len(mm["subs"]) - 1)

    def emit_P5_group(mi, g):
        mm = MACROS[mi]
        if g >= mm["ngrp"]:
            return
        for half in range(2):
            b = 2 * g + half
            for dc in range(8):
                EM("tensor", lambda e, g=g, dc=dc, half=half, b=b: e.matmul(
                    out=BANK(b)[0:128, 0:512],
                    lhsT=attn_sb[:, dc * 384 + g * 128: dc * 384 + g * 128 + 128],
                    rhs=wp_sb[:, dc * D + half * 512: dc * D + half * 512 + 512],
                    start=(dc == 0), stop=(dc == 7)),
                   reads=[r_attn, r_wres], acq=[r_bank[b]] if dc == 0 else [],
                   writes=[r_bank[b]] if dc == 7 else [], bump=(dc == 7))
        for half in range(2):
            b = 2 * g + half
            EM("vector", lambda e, g=g, half=half, b=b: e.tensor_copy(
                out=s_sb[:, g * D + half * 512:g * D + half * 512 + 512],
                in_=BANK(b)[0:128, 0:512]),
               reads=[r_bank[b]], writes=[r_s], bump=True)

    def emit_P5(mi):
        mm = MACROS[mi]
        for g in range(mm["ngrp"]):
            emit_P5_group(mi, g)
        if mi + 1 < n_macro:
            po = mm["ntok"]
            EM("vector", lambda e, po=po: e.tensor_copy(
                out=k_sb[:].rearrange("p (h t) -> p h t", h=H)[:, :, 0:12],
                in_=k_sb[:].rearrange("p (h t) -> p h t", h=H)[:, :, po:po + 12]),
               reads=[r_k], writes=[r_k], bump=True)

    def emit_P6(mi):
        mm = MACROS[mi]
        ntok = mm["ntok"]
        xn = xn_sb[mi % 2]; rxn = r_xn[mi % 2]
        xnv = xn[:].rearrange("p (c t) -> p c t", c=8)
        for lc in range(4):
            b = (2, 3, 4, 5)[lc]
            for dc in range(8):
                EM("tensor", lambda e, lc=lc, dc=dc, b=b, xnv=xnv, ntok=ntok: e.matmul(
                    out=BANK(b)[0:128, 0:ntok],
                    lhsT=fc1_sb[:, dc * LAT + lc * 128: dc * LAT + (lc + 1) * 128],
                    rhs=xnv[:, dc, 12:12 + ntok],
                    start=(dc == 0), stop=(dc == 7)),
                   reads=[rxn, r_wres], acq=[r_bank[b]] if dc == 0 else [],
                   writes=[r_bank[b]] if dc == 7 else [], bump=(dc == 7))
            EM("vector", lambda e, lc=lc, b=b, ntok=ntok: e.tensor_copy(
                out=lat_sb[:, lc * 384:lc * 384 + ntok], in_=BANK(b)[0:128, 0:ntok]),
               reads=[r_bank[b]], writes=[r_lat], bump=True)

    def emit_stream_w12(ei):
        sl = ei % 2
        EM("sync", lambda e, ei=ei, sl=sl: e.dma_start(
            out=w1b[sl][:].rearrange("p (c i) -> p c i", c=4),
            in_=w1s[ei].rearrange("(c p) i -> p c i", p=128)),
           reads=[r_wscr], writes=[r_w1b[sl]], dma_sem=("sW10", "sW11")[sl])
        EM("sync", lambda e, ei=ei, sl=sl: e.dma_start(
            out=w2b[sl][:].rearrange("p (c i) -> p c i", c=4),
            in_=w2s[ei].rearrange("(c p) i -> p c i", p=128)),
           reads=[r_wscr], writes=[r_w2b[sl]], dma_sem=("sW20", "sW21")[sl])

    def emit_stream_up(oc):
        sl = oc % 2
        EM("scalar", lambda e, oc=oc, sl=sl: e.dma_start(
            out=upb[sl][:].rearrange("p (c o) -> p c o", c=8),
            in_=ups.rearrange("(c p) o -> p c o", p=128)[:, :, oc * 128:(oc + 1) * 128]),
           reads=[r_wscr], writes=[r_upb[sl]], dma_sem=("sU0", "sU1")[sl])

    def emit_stream_dn(oc):
        sl = oc % 2
        EM("scalar", lambda e, oc=oc, sl=sl: e.dma_start(
            out=dnb[sl][:], in_=dns.rearrange("(g p) o -> p g o", p=128)[:, oc, :]),
           reads=[r_wscr], writes=[r_dnb[sl]], dma_sem=("sD0", "sD1")[sl])

    def emit_stream_updn(oc):
        emit_stream_up(oc)
        emit_stream_dn(oc)

    def emit_P7_expert(mi, ei):
        mm = MACROS[mi]
        ntok = mm["ntok"]
        sl = ei % 2
        wf = wfm[mi % 2]; rwf = r_wfm[mi % 2]
        EM("gpsimd", lambda e, ei=ei, wf=wf, ntok=ntok: e.dma_start(
            out=ubuf[0:1, 0:ntok], in_=wf[ei:ei + 1, 0:ntok]),
           reads=[rwf], writes=[r_ubuf], dma_sem="sRW")
        EM("tensor", lambda e, ntok=ntok: e.matmul(
            out=BANK(6)[0:128, 0:ntok], lhsT=ones1[:], rhs=ubuf[0:1, 0:ntok],
            start=True, stop=True),
           reads=[r_ubuf, r_wres], acq=[r_bank[6]], writes=[r_bank[6]], bump=True)
        for ic in range(4):
            b = 4 + ic % 2
            for lc in range(4):
                EM("tensor", lambda e, ic=ic, lc=lc, b=b, sl=sl, ntok=ntok: e.matmul(
                    out=BANK(b)[0:128, 0:ntok],
                    lhsT=w1b[sl][:, lc * 512 + ic * 128: lc * 512 + (ic + 1) * 128],
                    rhs=lat_sb[:, lc * 384:lc * 384 + ntok],
                    start=(lc == 0), stop=(lc == 3)),
                   reads=[r_w1b[sl], r_lat], acq=[r_bank[b]] if lc == 0 else [],
                   writes=[r_bank[b]] if lc == 3 else [], bump=(lc == 3))
            tr = trelu[0]; rtr = r_trelu[0]
            EM("vector", lambda e, b=b, tr=tr, ntok=ntok: e.tensor_scalar_max(
                out=tr[:, 0:ntok], in0=BANK(b)[0:128, 0:ntok], scalar1=0.0),
               reads=[r_bank[b]], writes=[rtr])
            EM("vector", lambda e, tr=tr, ntok=ntok: e.tensor_mul(
                out=ttw[:, 0:ntok], in0=tr[:, 0:ntok], in1=BANK(6)[0:128, 0:ntok]),
               reads=[rtr, r_bank[6]], writes=[r_ttw])
            EM("vector", lambda e, ic=ic, tr=tr, ntok=ntok: e.tensor_mul(
                out=interbf[:, ic * 384:ic * 384 + ntok], in0=tr[:, 0:ntok], in1=ttw[:, 0:ntok]),
               reads=[rtr, r_ttw], writes=[r_interbf], bump=True)
        for lc in range(4):
            for ic in range(4):
                first = (ei == 0 and ic == 0)
                last = (ei == E - 1 and ic == 3)
                EM("tensor", lambda e, lc=lc, ic=ic, sl=sl, first=first, last=last, ntok=ntok: e.matmul(
                    out=BANK(lc)[0:128, 0:ntok],
                    lhsT=w2b[sl][:, ic * 512 + lc * 128: ic * 512 + (lc + 1) * 128],
                    rhs=interbf[:, ic * 384:ic * 384 + ntok],
                    start=first, stop=last),
                   reads=[r_w2b[sl], r_interbf],
                   acq=[r_bank[lc]] if first else [],
                   writes=[r_bank[lc]] if last else [], bump=last)

    def emit_P8(mi):
        mm = MACROS[mi]
        ntok = mm["ntok"]
        for lc in range(4):
            EM("vector", lambda e, lc=lc, ntok=ntok: e.tensor_copy(
                out=combbf[:, lc * 384:lc * 384 + ntok], in_=BANK(lc)[0:128, 0:ntok]),
               reads=[r_bank[lc]], writes=[r_combbf], bump=True)
        for g in range(mm["ngrp"]):
            for half in range(2):
                b = 2 * g + half
                for lc in range(4):
                    EM("tensor", lambda e, g=g, half=half, lc=lc, b=b: e.matmul(
                        out=BANK(b)[0:128, 0:512],
                        lhsT=combbf[:, lc * 384 + g * 128: lc * 384 + g * 128 + 128],
                        rhs=fc2_sb[:, lc * D + half * 512: lc * D + half * 512 + 512],
                        start=(lc == 0), stop=False),
                       reads=[r_combbf, r_wres],
                       acq=[r_bank[b]] if lc == 0 else [], writes=[], bump=False)

    def emit_P9(mi):
        mm = MACROS[mi]
        ntok = mm["ntok"]
        xn = xn_sb[mi % 2]; rxn = r_xn[mi % 2]
        xnv = xn[:].rearrange("p (c t) -> p c t", c=8)
        def emit_down(oc):
            sl = oc % 2
            for g in range(mm["ngrp"]):
                for half in range(2):
                    fb = 2 * g + half
                    last = (oc == 31)
                    EM("tensor", lambda e, oc=oc, half=half, fb=fb, sl=sl, g=g, last=last: e.matmul(
                        out=BANK(fb)[0:128, 0:512],
                        lhsT=upbf[oc % 2][:, g * 128:g * 128 + 128],
                        rhs=dnb[sl][:, half * 512:half * 512 + 512],
                        start=False, stop=last),
                       reads=[r_upbf[oc % 2], r_dnb[sl]],
                       writes=[r_bank[fb]] if last else [], bump=last)
        for oc in range(32):
            sl = oc % 2
            b = 6 + oc % 2
            for dc in range(8):
                EM("tensor", lambda e, dc=dc, b=b, sl=sl, xnv=xnv, ntok=ntok: e.matmul(
                    out=BANK(b)[0:128, 0:ntok],
                    lhsT=upb[sl][:, dc * 128:(dc + 1) * 128],
                    rhs=xnv[:, dc, 12:12 + ntok],
                    start=(dc == 0), stop=(dc == 7)),
                   reads=[r_upb[sl], rxn], acq=[r_bank[b]] if dc == 0 else [],
                   writes=[r_bank[b]] if dc == 7 else [], bump=(dc == 7))
            tr = trelu[0]; rtr = r_trelu[0]
            EM("vector", lambda e, b=b, tr=tr, ntok=ntok: e.tensor_scalar_max(
                out=tr[:, 0:ntok], in0=BANK(b)[0:128, 0:ntok], scalar1=0.0),
               reads=[r_bank[b]], writes=[rtr])
            EM("scalar", lambda e, oc=oc, tr=tr, ntok=ntok: e.activation(
                out=upbf[oc % 2][:, 0:ntok], in_=tr[:, 0:ntok], func=AF.Square),
               reads=[rtr], writes=[r_upbf[oc % 2]], bump=True)
            sched.bump("vector")
            if oc > 0:
                emit_down(oc - 1)
            if oc + 2 < 32:
                emit_stream_up(oc + 2)
            if 2 <= oc + 1 < 32:
                emit_stream_dn(oc + 1)
        emit_down(31)

    def emit_P10(mi, groups=None):
        mm = MACROS[mi]
        if groups is None:
            groups = list(range(mm["ngrp"]))
        for g in groups:
            t0 = mm["t0"] + g * 128
            xr = x_res[0]; rxr = r_xres[0]
            EM("scalar", lambda e, t0=t0, xr=xr: e.dma_start(out=xr[:], in_=x_in[t0:t0 + 128, :]),
               reads=[rxr], writes=[rxr], dma_sem="sX")
            ob = outb[0]; rob = r_outb[0]
            for half in range(2):
                fb = 2 * g + half
                hs = slice(half * 512, half * 512 + 512)
                EM("vector", lambda e, fb=fb, hs=hs, ob=ob: e.tensor_copy(
                    out=ob[:, hs], in_=BANK(fb)),
                   reads=[r_bank[fb]], writes=[rob, r_bank[fb]], bump=True)
                EM("scalar", lambda e, hs=hs, ob=ob: e.activation(
                    out=ttn[:, hs], in_=ob[:, hs], func=AF.Tanh, scale=0.5),
                   reads=[rob], writes=[r_ttn], bump=True)
                EM("vector", lambda e, hs=hs, ob=ob: e.scalar_tensor_tensor(
                    out=g2b[:, hs], in0=ttn[:, hs], scalar=1.0, in1=ob[:, hs],
                    op0=ALU.add, op1=ALU.mult),
                   reads=[r_ttn, rob], writes=[r_g2], bump=True)
                EM("vector", lambda e, g=g, half=half, hs=hs: e.scalar_tensor_tensor(
                    out=ttn[:, hs], in0=g2b[:, hs], scalar=0.5,
                    in1=s_sb[:, g * D + half * 512:g * D + half * 512 + 512],
                    op0=ALU.mult, op1=ALU.mult),
                   reads=[r_g2, r_s], writes=[r_ttn])
                EM("vector", lambda e, hs=hs, ob=ob, xr=xr: e.tensor_add(
                    out=ob[:, hs], in0=ttn[:, hs], in1=xr[:, hs]),
                   reads=[r_ttn, rxr], writes=[rob], bump=True)
            EM("scalar", lambda e, t0=t0, ob=ob: e.dma_start(out=out_d[t0:t0 + 128, :], in_=ob[:]),
               reads=[rob], writes=[r_out], dma_sem="sO")

    ORDER = ["init", "P012", "P3", "P4", "P5", "P6", "P7", "P8", "P9", "full"]
    lvl = ORDER.index(stop_after)
    if lvl >= 1:
        emit_P012(0)
        for g0 in range(3):
            emit_wtrans(0, g0)
    if lvl >= 2:
        emit_P3(0)
    for mi in range(n_macro):
        if lvl >= 3:
            emit_P4(mi)
        if lvl >= 4:
            emit_P5(mi)
        if lvl >= 5:
            emit_P6(mi)
        if lvl >= 6:
            emit_stream_w12(0)
            emit_stream_w12(1)
            for ei in range(E):
                emit_P7_expert(mi, ei)
                if ei + 2 < E:
                    emit_stream_w12(ei + 2)
                if mi + 1 < n_macro and ei in (1, 3, 5):
                    emit_P012(mi + 1, groups=[(ei - 1) // 2], phase="ln")
                if mi + 1 < n_macro and ei in (2, 4, 6):
                    emit_P012(mi + 1, groups=[ei // 2 - 1], phase="rt")
                if mi + 1 < n_macro and ei in (3, 5, 7):
                    emit_wtrans(mi + 1, ei // 2 - 1)
        elif mi + 1 < n_macro and lvl >= 2:
            emit_P012(mi + 1)
        if lvl >= 7:
            emit_P8(mi)
        if lvl >= 8:
            emit_stream_updn(0)
            emit_stream_updn(1)
            emit_P9(mi)
        if lvl >= 9 and lvl >= 2 and mi + 1 < n_macro:
            emit_P3(mi + 1, part=0)
            emit_P10(mi, groups=[0])
            emit_P3(mi + 1, part=1)
            emit_P10(mi, groups=[1])
            emit_P3(mi + 1, part=2)
            emit_P10(mi, groups=[g for g in range(2, MACROS[mi]["ngrp"])])
        else:
            if lvl >= 9:
                emit_P10(mi)
            if lvl >= 2 and mi + 1 < n_macro:
                emit_P3(mi + 1)

    if debug_dump:
        dbg_specs = [("xn_sb0", xn_sb[0], r_xn[0]), ("q_sb", q_sb, r_q), ("k_sb", k_sb, r_k),
                     ("lat_sb", lat_sb, r_lat), ("s_sb", s_sb, r_s), ("attn_sb", attn_sb, r_attn),
                     ("combbf", combbf, r_combbf), ("v_sb0", v_sb[0], r_v[0]),
                     ("bdm0", bdm[0], r_bdm[0]), ("bdm1", bdm[1], r_bdm[1]),
                     ("ubuf_d", ubuf, r_ubuf), ("wfm0", wfm[0], r_wfm[0])]
        for nm, tt, rr_ in dbg_specs:
            shp = list(tt[:].shape)
            dt_ = tt[:].dtype
            od = nc.dram_tensor("dbg_" + nm, shp, dt_, kind="ExternalOutput").ap()
            EM("sync", lambda e, od=od, tt=tt: e.dma_start(out=od, in_=tt[:]),
               reads=[rr_], writes=[], dma_sem="sO")
    for sname in ("sO", "sW10", "sW11", "sW20", "sW21", "sU0", "sU1", "sD0", "sD1",
                  "sX", "sBW0", "sBW1", "sBM0", "sBM1", "sRW", "sI", "sY0", "sG0", "sG1"):
        if sched.cnt[sname]:
            sched.prog["sync"].append(("wait", sname, sched.cnt[sname]))

    print("prog sizes", {k: len(v) for k, v in sched.prog.items()},
          "sems", {k: v for k, v in sched.cnt.items()}, "spacers", sched.nops)
    with nc.Block() as block:
        sched.replay(block)
    if want_handles:
        h = dict(xn_sb=xn_sb, q_sb=q_sb, k_sb=k_sb, v_sb=v_sb, attn_sb=attn_sb,
                 s_sb=s_sb, lat_sb=lat_sb, combbf=combbf, wfm=wfm, relk_sb=relk_sb,
                 qscale=qscale, bdm=bdm, ubuf=ubuf, tbuf=tbuf, sums=sums, inv=inv,
                 ptbf=ptbf, rt_sig=rt_sig, rt_W=rt_W, rt_idx=rt_idx, biasb=biasb,
                 interbf=interbf, posfm=posfm, rt_sfc=rt_sfc, rt_m=rt_m, rt_sel=rt_sel,
                 pbank=pbank, upbf=upbf, trelu=trelu)
        return nc, h, es
    es.close()
    n_ins = sum(len(v) for v in sched.prog.values())
    return nc


_NC_CACHE = {}

def _get_nc():
    if "nc" not in _NC_CACHE:
        _NC_CACHE["nc"] = build_nc()
    return _NC_CACHE["nc"]


def kernel(L0_attn_hidden_states, L0_attn_position_embeddings, L0_attn_Wq, L0_attn_Wk,
           L0_attn_Wv, L0_attn_Wpost, L0_attn_Wrel, L0_attn_per_dim_scale,
           L0_ffn_gate, L0_ffn_corr_bias, L0_ffn_w1, L0_ffn_w2, L0_ffn_fc1_w,
           L0_ffn_fc2_w, L0_ffn_up_w, L0_ffn_down_w):
    f32 = np.float32
    Ct = lambda a: np.ascontiguousarray(np.asarray(a, dtype=f32))
    x = Ct(L0_attn_hidden_states)
    base = {
        "wqT": Ct(np.asarray(L0_attn_Wq, f32).T),
        "wkT": Ct(np.asarray(L0_attn_Wk, f32).T),
        "wvT": Ct(np.asarray(L0_attn_Wv, f32).T),
        "wpT": Ct(np.asarray(L0_attn_Wpost, f32).T),
        "wrT": Ct(np.asarray(L0_attn_Wrel, f32).T),
        "posemb": Ct(L0_attn_position_embeddings),
        "pds": Ct(L0_attn_per_dim_scale),
        "gateT": Ct(np.asarray(L0_ffn_gate, f32).T),
        "corrb": Ct(L0_ffn_corr_bias),
        "w1T": Ct(np.asarray(L0_ffn_w1, f32).transpose(0, 2, 1)),
        "w2T": Ct(np.asarray(L0_ffn_w2, f32).transpose(0, 2, 1)),
        "fc1T": Ct(np.asarray(L0_ffn_fc1_w, f32).T),
        "fc2T": Ct(np.asarray(L0_ffn_fc2_w, f32).T),
        "upT": Ct(np.asarray(L0_ffn_up_w, f32).T),
        "dnT": Ct(np.asarray(L0_ffn_down_w, f32).T),
        "idbf": np.eye(128, dtype=ml_dtypes.bfloat16),
        "idf": np.eye(128, dtype=np.float32),
        "iota8": np.tile(np.arange(8, dtype=np.float32), (128, 1)),
    }
    in_maps = [dict(base, x=np.ascontiguousarray(x[b])) for b in range(8)]
    nc = _get_nc()
    res = run_bass_kernel_spmd(nc, in_maps, core_ids=list(range(8)))
    return np.stack([np.asarray(r["out"]) for r in res.results], axis=0)


if __name__ == "__main__":
    print("building nc...")
    import time
    t = time.time()
    nc = build_nc()
    print("built in", time.time() - t, "s")
